# version 01
# baseline (speedup 1.0000x reference)
import math
import numpy as np
import ml_dtypes
import concourse.bass as bass
import concourse.mybir as mybir
from concourse.bass_utils import run_bass_kernel_spmd

F32 = mybir.dt.float32
BF16 = mybir.dt.bfloat16
U8 = mybir.dt.uint8
AF = mybir.ActivationFunctionType
OP = mybir.AluOpType
AX = mybir.AxisListType.X
BF = ml_dtypes.bfloat16

NCORE = 8
B, T, D = 2, 8192, 1024
TPC = 2048
NT = TPC // 128


class Sched:

    ENG = ("tensor", "vector", "scalar", "gpsimd", "sync")

    def __init__(self, nc):
        self.nc = nc
        self.ctx = []
        self.sem = {}
        self.cnt = {}
        for e in self.ENG:
            self.sem[e] = self._mk_sem(e)
        for q in ("qsync", "qscalar", "qgpsimd", "qvector"):
            self.sem[q] = self._mk_sem(q)
        self.known = {}
        self.writes = {}
        self.reads = {}
        self.stream = {e: [] for e in
                       ("tensor", "vector", "scalar", "gpsimd", "sync")}

    def _mk_sem(self, name):
        cm = self.nc.semaphore(name)
        h = cm.__enter__()
        self.ctx.append(cm)
        self.cnt[name] = 0
        return h

    @staticmethod
    def _span(ap):
        es = mybir.dt_size(ap.dtype) if hasattr(mybir, "dt_size") else {
            F32: 4, BF16: 2, U8: 1, mybir.dt.uint32: 4, mybir.dt.int32: 4,
            mybir.dt.uint16: 2}.get(ap.dtype, 4)
        lo = ap.offset
        ext = 1
        for st, n in ap.ap:
            ext += abs(st) * (n - 1)
        return (lo * es, (lo + ext) * es)

    @staticmethod
    def _norm(ap):
        return ap if hasattr(ap, "tensor") else ap[:, :]

    @staticmethod
    def _key(ap):
        t = ap.tensor
        return id(t)

    def _waits_for(self, eng, aps, table):
        ws = []
        for ap in aps:
            ap = self._norm(ap)
            k = self._key(ap)
            lo, hi = self._span(ap)
            for e in table.get(k, ()):
                elo, ehi, semname, cnt, weng = e
                if ehi <= lo or elo >= hi:
                    continue
                if weng == "tensor" and eng == "tensor":
                    continue
                ws.append((semname, cnt))
        return ws

    def _calc_waits(self, eng, ws):
        best = {}
        for semname, cnt in ws:
            if semname.startswith("q"):
                cnt = self.cnt[semname]
            if cnt > best.get(semname, -1):
                best[semname] = cnt
        out = []
        for semname, cnt in best.items():
            kk = (eng, semname)
            if self.known.get(kk, 0) >= cnt:
                continue
            out.append((semname, cnt))
            self.known[kk] = cnt
        return out

    def _record(self, table, ap, semname, cnt, eng):
        ap = self._norm(ap)
        k = self._key(ap)
        lo, hi = self._span(ap)
        lst = table.setdefault(k, [])
        lst.append([lo, hi, semname, cnt, eng])
        if len(lst) > 48:
            old, keep = lst[:24], lst[24:]
            hull = {}
            for e in old:
                h = hull.get(e[2])
                if h is None:
                    hull[e[2]] = list(e)
                else:
                    h[0] = min(h[0], e[0])
                    h[1] = max(h[1], e[1])
                    h[3] = max(h[3], e[3])
            table[k] = list(hull.values()) + keep
            return

    def op(self, eng, fn, writes, reads, inc=True):
        ws = self._waits_for(eng, reads, self.writes)
        ws += self._waits_for(eng, writes, self.writes)
        ws += self._waits_for(eng, writes, self.reads)
        waits = self._calc_waits(eng, ws)
        semname = eng
        if inc:
            self.cnt[semname] += 1
            c = self.cnt[semname]
            self.stream[eng].append((waits, fn, semname, 1))
        else:
            c = self.cnt[semname] + 1
            self.stream[eng].append((waits, fn, None, 0))
        for ap in writes:
            self._record(self.writes, ap, semname, c, eng)
        for ap in reads:
            self._record(self.reads, ap, semname, c, eng)

    def dma(self, queue, out_ap, in_ap, **kw):
        qname = "q" + queue
        ws = self._waits_for(queue, [in_ap], self.writes)
        ws += self._waits_for(queue, [out_ap], self.writes)
        ws += self._waits_for(queue, [out_ap], self.reads)
        waits = self._calc_waits(queue, ws)

        def fn(e, out_ap=out_ap, in_ap=in_ap, kw=kw):
            return e.dma_start(out_ap, in_ap, **kw)
        self.cnt[qname] += 16
        self.stream[queue].append((waits, fn, qname, 16))
        c = self.cnt[qname]
        self._record(self.writes, out_ap, qname, c, queue)
        self._record(self.reads, in_ap, qname, c, queue)

    def barrier_drain(self, eng="sync"):
        waits = []
        for semname, cnt in self.cnt.items():
            if cnt > 0 and self.known.get((eng, semname), 0) < cnt:
                waits.append((semname, cnt))
                self.known[(eng, semname)] = cnt
        self.stream[eng].append((waits, None, None, 0))

    def replay(self, block):
        decs = {"tensor": block.tensor, "vector": block.vector,
                "scalar": block.scalar, "gpsimd": block.gpsimd,
                "sync": block.sync}
        for eng, entries in self.stream.items():
            def body(e, entries=entries):
                for waits, fn, semname, amt in entries:
                    for sn, cnt in waits:
                        e.wait_ge(self.sem[sn], cnt)
                    if fn is not None:
                        inst = fn(e)
                        if semname is not None:
                            inst.then_inc(self.sem[semname], amt)
            decs[eng](body)


def dt_np(a):
    return np.ascontiguousarray(a)


def ceil_div(a, b):
    return (a + b - 1) // b


class Prog:

    def __init__(self, name):
        self.nc = bass.Bass("TRN2", target_bir_lowering=False)
        self.name = name
        self.inputs = {}
        self.outputs = {}
        self.sb_ctx = []

    def inp(self, name, shape, dtype=F32):
        ap = self.nc.dram_tensor(name, list(shape), dtype, kind="ExternalInput").ap()
        self.inputs[name] = ap
        return ap

    def out(self, name, shape, dtype=F32):
        ap = self.nc.dram_tensor(name, list(shape), dtype, kind="ExternalOutput").ap()
        self.outputs[name] = ap
        return ap

    def sb(self, name, shape, dtype=BF16):
        cm = self.nc.sbuf_tensor(name, list(shape), dtype)
        h = cm.__enter__()
        self.sb_ctx.append(cm)
        return h

    def ps(self, name, shape, dtype=F32):
        cm = self.nc.psum_tensor(name, list(shape), dtype)
        h = cm.__enter__()
        self.sb_ctx.append(cm)
        return h


def reg_consts(p, S, values):
    n = len(values)
    if n == 0:
        return
    ct = p.sb("c_consts", [128, n], F32)
    for i, v in enumerate(values):
        def f(g, i=i, v=v):
            return g.memset(ct[:, i:i + 1], v)
        S.op("gpsimd", f, [ct[:, i:i + 1]], [])
        p.nc.const_aps.register(F32, v, ct[:, i:i + 1])


def build_const_tiles(p, S):
    ident = p.sb("c_ident", [128, 128], BF16)
    i1 = p.sb("c_iota1", [128, 128], F32)
    i2 = p.sb("c_iota2", [128, 128], F32)
    ones = p.sb("c_ones", [128, 128], BF16)

    def f(g):
        return g.iota(i1[:, :], [[1, 128]], channel_multiplier=0,
                      allow_small_or_imprecise_dtypes=True)
    S.op("gpsimd", f, [i1[:, :]], [])

    def f(g):
        return g.iota(i2[:, :], [[0, 128]], channel_multiplier=1,
                      allow_small_or_imprecise_dtypes=True)
    S.op("gpsimd", f, [i2[:, :]], [])

    def f(v):
        return v.tensor_tensor(ident[:, :], i1[:, :], i2[:, :], OP.is_equal)
    S.op("vector", f, [ident[:, :]], [i1[:, :], i2[:, :]])

    identF = p.sb("c_identF", [128, 128], F32)

    def f(v):
        return v.tensor_tensor(identF[:, :], i1[:, :], i2[:, :], OP.is_equal)
    S.op("vector", f, [identF[:, :]], [i1[:, :], i2[:, :]])

    def f(g):
        return g.memset(ones[:, :], 1.0)
    S.op("gpsimd", f, [ones[:, :]], [])
    p.identF = identF
    return ident, ones


def transpose_tile(S, psT, ident, dst_fm, src_tm_bf16, evac_eng="scalar"):
    pt = psT[:, 0:128]

    def f(t):
        return t.transpose(pt, src_tm_bf16, ident[:, :])
    S.op("tensor", f, [pt], [src_tm_bf16, ident[:, :]])

    def f(e):
        if evac_eng == "scalar":
            return e.copy(dst_fm, pt)
        return e.tensor_copy(dst_fm, pt)
    S.op(evac_eng, f, [dst_fm], [pt])


def layernorm_tiles(p, S, x_tiles, xln_tiles, nt, scratch, stats, eps=1e-5):
    for i in range(nt):
        ln_one_tile(p, S, x_tiles[i], xln_tiles[i], scratch, stats, eps)


def ln_one_tile(p, S, x, xln_out, scratch, stats, eps=1e-5, Dm=1024):
    if True:
        sumx = stats[:, 0:1]
        sumsq = stats[:, 1:2]

        def f(v, x=x, sumx=sumx):
            return v.tensor_scalar(scratch[:, :], x, 1.0, 0.0, OP.mult,
                                   OP.add, accum_out=sumx)
        S.op("vector", f, [scratch[:, :], sumx], [x])

        def f(v, x=x, sumsq=sumsq):
            return v.scalar_tensor_tensor(scratch[:, :], x, 1.0, x, OP.mult,
                                          OP.mult, accum_out=sumsq)
        S.op("vector", f, [scratch[:, :], sumsq], [x])

        mean = stats[:, 2:3]
        var = stats[:, 3:4]
        rstd = stats[:, 4:5]
        mb = stats[:, 5:6]

        def f(v):
            return v.tensor_scalar(mean, sumx, 1.0 / Dm, None, OP.mult)
        S.op("vector", f, [mean], [sumx])

        def f(v):
            return v.scalar_tensor_tensor(var, mean, -1.0, mean, OP.mult,
                                          OP.mult)
        S.op("vector", f, [var], [mean])

        def f(v):
            return v.scalar_tensor_tensor(var, sumsq, 1.0 / Dm, var, OP.mult,
                                          OP.add)
        S.op("vector", f, [var], [sumsq, var])

        def f(v):
            return v.tensor_scalar(var, var, eps, None, OP.add)
        S.op("vector", f, [var], [var])

        def f(sc):
            return sc.activation(rstd, var, AF.Ln, bias=0.0, scale=1.0)
        S.op("scalar", f, [rstd], [var])

        def f(sc):
            return sc.activation(rstd, rstd, AF.Exp, bias=0.0, scale=-0.5)
        S.op("scalar", f, [rstd], [rstd])

        def f(v):
            return v.scalar_tensor_tensor(mb, mean, -1.0, rstd, OP.mult,
                                          OP.mult)
        S.op("vector", f, [mb], [mean, rstd])

        def f(v, x=x, o=xln_out):
            return v.tensor_scalar(o, x, rstd, mb, OP.mult, OP.add)
        S.op("vector", f, [xln_out], [x, rstd, mb])


def mm_tiled(p, S, out_psum_fn, lhsT_fn, rhs_fn, nk, extra_first=None,
             start_clear=True):
    steps = []
    if extra_first is not None:
        steps.append(extra_first)
    for k in range(nk):
        steps.append((lhsT_fn(k), rhs_fn(k)))
    n = len(steps)
    for j, (lt, rh) in enumerate(steps):
        st = (j == 0) and start_clear
        sp = (j == n - 1)

        def f(t, lt=lt, rh=rh, st=st, sp=sp):
            return t.matmul(out_psum_fn, lt, rh, start=st, stop=sp)
        S.op("tensor", f, [out_psum_fn], [lt, rh], inc=(j == n - 1))


class Arena:

    def __init__(self, tensor, ncols):
        self.t = tensor
        self.n = ncols
        self.free_list = [(0, ncols)]

    def alloc(self, n, align=8):
        for idx, (lo, sz) in enumerate(self.free_list):
            alo = (lo + align - 1) // align * align
            if alo + n <= lo + sz:
                pre = (lo, alo - lo)
                post = (alo + n, lo + sz - (alo + n))
                repl = [p for p in (pre, post) if p[1] > 0]
                self.free_list[idx:idx + 1] = repl
                return alo
        raise MemoryError(f"arena OOM: want {n}, free={self.free_list}")

    def free(self, off, n):
        self.free_list.append((off, n))
        self.free_list.sort()
        merged = []
        for lo, sz in self.free_list:
            if merged and merged[-1][0] + merged[-1][1] == lo:
                merged[-1] = (merged[-1][0], merged[-1][1] + sz)
            else:
                merged.append((lo, sz))
        self.free_list = [tuple(m) for m in merged]


def evac(S, dst, src_psum, eng="scalar", scale=None):
    if scale is None:
        if eng == "scalar":
            def f(e):
                return e.copy(dst, src_psum)
        else:
            def f(e):
                return e.tensor_copy(dst, src_psum)
        S.op(eng, f, [dst], [src_psum])
    else:
        if eng == "scalar":
            def f(e):
                return e.activation(dst, src_psum, AF.Copy, bias=0.0, scale=scale)
        else:
            def f(e):
                return e.tensor_scalar(dst, src_psum, scale, None, OP.mult)
        S.op(eng, f, [dst], [src_psum])


def fp4_quant_tiles(p, S, AFa, ABa, x_f32, out_bf16, np_, nfree, tag):
    nb = nfree // 16
    C = 6291456.0
    sc_sz = 3 * nb + nfree * 4
    sc_off = AFa.alloc(sc_sz)
    A = AFa.t
    bmax = A[0:np_, sc_off:sc_off + nb]
    srec = A[0:np_, sc_off + nb:sc_off + 2 * nb]
    scl = A[0:np_, sc_off + 2 * nb:sc_off + 3 * nb]
    base = sc_off + 3 * nb
    u = A[0:np_, base:base + nfree]
    qlow = A[0:np_, base + nfree:base + 2 * nfree]
    a = A[0:np_, base + 2 * nfree:base + 3 * nfree]
    h = A[0:np_, base + 3 * nfree:base + 4 * nfree]
    qhs = a
    m_off = ABa.alloc(nfree // 2 + 8)
    mask = ABa.t[0:np_, m_off:m_off + nfree // 2].bitcast(U8)

    x3 = x_f32.rearrange("p (b i) -> p b i", i=16)

    def f(v):
        return v.tensor_reduce(bmax.unsqueeze(2), x3, axis=AX, op=OP.max,
                               apply_absolute_value=True)
    S.op("vector", f, [bmax], [x_f32])

    def f(v):
        return v.tensor_scalar(scl, bmax, 6e-12, 1.0 / 6.0, OP.max, OP.mult)
    S.op("vector", f, [scl], [bmax])

    def f(v):
        return v.reciprocal(srec, scl)
    S.op("vector", f, [srec], [scl])

    u3 = u.rearrange("p (b i) -> p b i", i=16)
    sre3 = srec.unsqueeze(2).broadcast_to((np_, nb, 16))

    def f(v):
        return v.tensor_tensor(u3, x3, sre3, OP.mult)
    S.op("vector", f, [u], [x_f32, srec])

    def f(v):
        return v.tensor_scalar(qlow, u, C, C, OP.add, OP.subtract)
    S.op("vector", f, [qlow], [u])

    def f(sc):
        return sc.activation(a, u, AF.Abs, bias=0.0, scale=1.0)
    S.op("scalar", f, [a], [u])

    def f(v):
        return v.tensor_scalar(h, a, 2.5, 2.0, OP.is_gt, OP.add)
    S.op("vector", f, [h], [a])

    def f(v):
        return v.scalar_tensor_tensor(h, a, 3.5, h, OP.is_gt, OP.add)
    S.op("vector", f, [h], [a, h])

    def f(v):
        return v.scalar_tensor_tensor(h, a, 5.0, h, OP.is_gt, OP.add)
    S.op("vector", f, [h], [a, h])

    def f(v):
        return v.scalar_tensor_tensor(h, a, 5.0, h, OP.is_gt, OP.add)
    S.op("vector", f, [h], [a, h])

    def f(v):
        return v.tensor_scalar(mask, a, 2.25, None, OP.is_gt)
    S.op("vector", f, [mask], [a])

    ui = u.bitcast(mybir.dt.uint32)
    hi = h.bitcast(mybir.dt.uint32)
    qi = qhs.bitcast(mybir.dt.uint32)

    def f(v):
        return v.tensor_scalar(qi, ui, 0x80000000, None, OP.bitwise_and)
    S.op("vector", f, [qhs], [u])

    def f(v):
        return v.tensor_tensor(qi, qi, hi, OP.bitwise_or)
    S.op("vector", f, [qhs], [qhs, h])

    def f(v):
        return v.select(u, mask, qhs, qlow, add_drain=True)
    S.op("vector", f, [u], [mask, qhs, qlow])

    o3 = out_bf16.rearrange("p (b i) -> p b i", i=16)
    scl3 = scl.unsqueeze(2).broadcast_to((np_, nb, 16))

    def f(v):
        return v.tensor_tensor(o3, u3, scl3, OP.mult)
    S.op("vector", f, [out_bf16], [u, scl])

    AFa.free(sc_off, sc_sz)
    ABa.free(m_off, nfree // 2 + 8)


def prog_start(p, af_cols, ab_cols, n_ps=6, n_psT=2):
    nc = p.nc
    S = Sched(nc)
    AFt = p.sb("AF", [128, af_cols], F32)
    ABt = p.sb("AB", [128, ab_cols], BF16)
    AFa = Arena(AFt, af_cols)
    ABa = Arena(ABt, ab_cols)
    ps = [p.ps(f"ps{i}", [128, 512], F32) for i in range(n_ps)]
    psT = [p.ps(f"psT{i}", [128, 1024], BF16) for i in range(n_psT)]
    ident, ones = build_const_tiles(p, S)
    return S, AFt, ABt, AFa, ABa, ps, psT, ident, ones


def prog_end(p, S):
    S.barrier_drain("sync")
    with p.nc.Block() as block:
        S.replay(block)
    for c in reversed(S.ctx):
        c.__exit__(None, None, None)
    for c in reversed(p.sb_ctx):
        c.__exit__(None, None, None)


def build_launch1a():
    p = Prog("l1a")
    nc = p.nc
    x0 = p.inp("x0", [TPC, D])
    wq = p.inp("aWqT", [D, D])
    wk = p.inp("aWkT", [D, D])
    wv = p.inp("aWvT", [D, D])
    prj = p.inp("aproj", [D, 256])
    wg8 = p.inp("wg8", [352, D])
    wu8 = p.inp("wu8", [352, D])
    wd8 = p.inp("wd8", [128, 2816])

    kvo = p.out("kvpart", [256, D])
    qpo = p.out("qp", [256, TPC], BF16)
    wgq = p.out("wgq", [352, D], BF16)
    wuq = p.out("wuq", [352, D], BF16)
    wdq = p.out("wdq", [128, 2816], BF16)

    S, AFt, ABt, AFa, ABa, ps, psT, ident, ones = prog_start(p, 26000, 52000)

    xres_off = AFa.alloc(NT * 1024)
    xres = lambda i: AFt[:, xres_off + 1024 * i: xres_off + 1024 * (i + 1)]
    st_off = AFa.alloc(64)
    stats = AFt[:, st_off:st_off + 64]

    for i in range(NT):
        S.dma("sync", xres(i), x0[128 * i:128 * (i + 1), :])

    def quant_w(w_in, w_out, rows, cols):
        nt_ = ceil_div(rows, 128)
        cch = 1024 if cols % 1024 == 0 else 1408
        off = AFa.alloc(cch)
        qf = AFt[:, off:off + cch]
        ob = ABa.alloc(cch)
        qb = ABt[:, ob:ob + cch]
        for i in range(nt_):
            r0, r1 = 128 * i, min(128 * (i + 1), rows)
            nr = r1 - r0
            for c0 in range(0, cols, cch):
                S.dma("gpsimd", qf[0:nr, :], w_in[r0:r1, c0:c0 + cch])
                obf = qb[0:nr, 0:cch]
                fp4_quant_tiles(p, S, AFa, ABa, qf[0:nr, :], obf, nr, cch, "w")
                S.dma("gpsimd", w_out[r0:r1, c0:c0 + cch], obf)
        AFa.free(off, cch)
        ABa.free(ob, cch)

    quant_w(wg8, wgq, 352, D)
    quant_w(wu8, wuq, 352, D)
    quant_w(wd8, wdq, 128, 2816)

    xln_off = ABa.alloc(2 * 1024)
    xlnT_off = ABa.alloc(8 * 2048)
    scr_off = ABa.alloc(1024)
    xlnT = ABt[:, xlnT_off:xlnT_off + 8 * 2048]
    scr = ABt[:, scr_off:scr_off + 1024]
    for i in range(NT):
        xln = ABt[:, xln_off + 1024 * (i % 2): xln_off + 1024 * (i % 2) + 1024]
        ln_one_tile(p, S, xres(i), xln, scr, stats)
        for j in range(8):
            src = xln[:, 128 * j:128 * (j + 1)]
            dst = xlnT[:, 2048 * j + 128 * i: 2048 * j + 128 * (i + 1)]
            transpose_tile(S, psT[(i * 8 + j) % 2], ident, dst, src,
                           evac_eng=("scalar", "vector")[j % 2])

    wslot_off = ABa.alloc(8 * 1024)
    wslot = ABt[:, wslot_off:wslot_off + 8 * 1024]
    prj_off = ABa.alloc(8 * 256)
    prjt = ABt[:, prj_off:prj_off + 8 * 256]
    for dk in range(8):
        S.dma("gpsimd", prjt[:, 256 * dk:256 * (dk + 1)],
              prj[128 * dk:128 * (dk + 1), :])

    def load_w(dram):
        for dk in range(8):
            S.dma("gpsimd", wslot[:, 1024 * dk:1024 * (dk + 1)],
                  dram[128 * dk:128 * (dk + 1), :])

    elu_off = AFa.alloc(512)
    eluf = AFt[:, elu_off:elu_off + 512]

    def elu_evac(dst_bf, psum, n):
        m1 = eluf[:, 0:n]

        def f(v):
            return v.tensor_scalar(m1, psum[:, 0:n], 0.0, None, OP.min)
        S.op("vector", f, [m1], [psum])

        def f(sc):
            return sc.activation(m1, m1, AF.Exp, bias=0.0, scale=1.0)
        S.op("scalar", f, [m1], [m1])

        def f(v):
            return v.scalar_tensor_tensor(dst_bf, psum[:, 0:n], 0.0, m1,
                                          OP.max, OP.add)
        S.op("vector", f, [dst_bf], [psum, m1])

    kp_off = ABa.alloc(16 * 256)
    kptm = ABt[:, kp_off:kp_off + 16 * 256]
    kfm_off = ABa.alloc(2 * 1024)
    load_w(wk)
    for i in range(NT):
        kfm = ABt[:, kfm_off + 1024 * (i % 2): kfm_off + 1024 * (i % 2) + 1024]
        for j in range(8):
            pb = ps[(i * 8 + j) % 4]

            def lt(dk, j=j):
                return wslot[:, 1024 * dk + 128 * j: 1024 * dk + 128 * (j + 1)]

            def rh(dk, i=i):
                return xlnT[:, 2048 * dk + 128 * i: 2048 * dk + 128 * (i + 1)]
            mm_tiled(p, S, pb[:, 0:128], lt, rh, 8)
            evac(S, kfm[:, 128 * j:128 * (j + 1)], pb[:, 0:128],
                 eng=("scalar", "vector")[j % 2])
        pb = ps[4]

        def lt(je, kfm=kfm):
            return kfm[:, 128 * je:128 * (je + 1)]

        def rh(je):
            return prjt[:, 256 * je:256 * (je + 1)]
        mm_tiled(p, S, pb[:, 0:256], lt, rh, 8)
        elu_evac(kptm[:, 256 * i:256 * (i + 1)], pb, 256)
    ABa.free(kfm_off, 2 * 1024)

    load_w(wv)
    vt_off = ABa.alloc(2 * 1024)
    kvps = [ps[0], ps[1], ps[2], ps[3]]
    for i in range(NT):
        vtm = ABt[:, vt_off + 1024 * (i % 2): vt_off + 1024 * (i % 2) + 1024]
        for ncc in range(2):
            pb = ps[4 + (i * 2 + ncc) % 2]

            def lt(dk, i=i):
                return xlnT[:, 2048 * dk + 128 * i: 2048 * dk + 128 * (i + 1)]

            def rh(dk, ncc=ncc):
                return wslot[:, 1024 * dk + 512 * ncc: 1024 * dk + 512 * (ncc + 1)]
            mm_tiled(p, S, pb[:, :], lt, rh, 8)
            evac(S, vtm[:, 512 * ncc:512 * (ncc + 1)], pb[:, :],
                 eng=("scalar", "vector")[ncc])
        for pj in range(2):
            for ncc in range(2):
                def f(t, i=i, pj=pj, ncc=ncc, vtm=vtm):
                    return t.matmul(
                        kvps[pj * 2 + ncc][:, :],
                        kptm[:, 256 * i + 128 * pj: 256 * i + 128 * (pj + 1)],
                        vtm[:, 512 * ncc:512 * (ncc + 1)],
                        start=(i == 0), stop=(i == NT - 1))
                S.op("tensor", f, [kvps[pj * 2 + ncc][:, :]],
                     [kptm, ABt[:, vt_off:vt_off + 2048]],
                     inc=(i == NT - 1))
    ABa.free(vt_off, 2 * 1024)

    kvp_off = AFa.alloc(2 * 1024)
    for pj in range(2):
        for ncc in range(2):
            evac(S, AFt[:, kvp_off + 1024 * pj + 512 * ncc:
                        kvp_off + 1024 * pj + 512 * (ncc + 1)],
                 kvps[pj * 2 + ncc][:, :], eng=("scalar", "vector")[ncc])
    for pj in range(2):
        S.dma("sync", kvo[128 * pj:128 * (pj + 1), :],
              AFt[:, kvp_off + 1024 * pj:kvp_off + 1024 * (pj + 1)])
    AFa.free(kvp_off, 2 * 1024)

    load_w(wq)
    qp_off = ABa.alloc(2 * 2048)
    qpfm = ABt[:, qp_off:qp_off + 2 * 2048]
    qfm_off = ABa.alloc(8 * 512)
    for tc in range(4):
        qfm = ABt[:, qfm_off:qfm_off + 8 * 512]
        for j in range(8):
            pb = ps[j % 4]

            def lt(dk, j=j):
                return wslot[:, 1024 * dk + 128 * j: 1024 * dk + 128 * (j + 1)]

            def rh(dk, tc=tc):
                return xlnT[:, 2048 * dk + 512 * tc: 2048 * dk + 512 * (tc + 1)]
            mm_tiled(p, S, pb[:, :], lt, rh, 8)
            evac(S, qfm[:, 512 * j:512 * (j + 1)], pb[:, :],
                 eng=("scalar", "vector")[j % 2])
        for pj in range(2):
            pb = ps[4 + pj]

            def lt(je, pj=pj):
                return prjt[:, 256 * je + 128 * pj: 256 * je + 128 * (pj + 1)]

            def rh(je, qfm=qfm):
                return qfm[:, 512 * je:512 * (je + 1)]
            mm_tiled(p, S, pb[:, :], lt, rh, 8)
            elu_evac(qpfm[:, 2048 * pj + 512 * tc: 2048 * pj + 512 * (tc + 1)],
                     pb, 512)
    for pj in range(2):
        S.dma("sync", qpo[128 * pj:128 * (pj + 1), :],
              qpfm[:, 2048 * pj:2048 * (pj + 1)])

    prog_end(p, S)
    return p


def build_launch1b():
    p = Prog("l1b")
    x0 = p.inp("x0", [TPC, D])
    qpi = p.inp("qp", [256, TPC], BF16)
    kvp = p.inp("kvparts", [1024, D])
    fw = p.inp("fWT", [D, D])
    fb = p.inp("fb", [D])
    fvr = p.inp("fVr", [D, 32])
    fct = p.inp("fCT", [32, D])
    bwa = p.inp("bWattn", [D, 3 * D])

    x2o = p.out("x2", [TPC, D])
    qkvo = p.out("qkv", [TPC, 3 * D], BF16)

    S, AFt, ABt, AFa, ABa, ps, psT, ident, ones = prog_start(p, 26000, 52000)

    xres_off = AFa.alloc(NT * 1024)
    xres = lambda i: AFt[:, xres_off + 1024 * i: xres_off + 1024 * (i + 1)]
    st_off = AFa.alloc(64)
    stats = AFt[:, st_off:st_off + 64]
    zcol_off = AFa.alloc(16)
    zcols = AFt[:, zcol_off:zcol_off + 16]

    for i in range(NT):
        S.dma("sync", xres(i), x0[128 * i:128 * (i + 1), :])

    kvf_off = AFa.alloc(2 * 1024)
    kvt_off = AFa.alloc(2 * 1024)
    kv_off = ABa.alloc(2 * 1024)
    kvsb = ABt[:, kv_off:kv_off + 2048]
    for pj in range(2):
        acc = AFt[:, kvf_off + 1024 * pj:kvf_off + 1024 * (pj + 1)]
        for j in range(4):
            t = AFt[:, kvt_off + 1024 * (j % 2):kvt_off + 1024 * (j % 2) + 1024]
            S.dma("sync", t, kvp[256 * j + 128 * pj:256 * j + 128 * (pj + 1), :])
            if j == 0:
                def f(v, t=t, acc=acc):
                    return v.tensor_copy(acc, t)
                S.op("vector", f, [acc], [t])
            else:
                def f(v, t=t, acc=acc):
                    return v.tensor_tensor(acc, acc, t, OP.add)
                S.op("vector", f, [acc], [acc, t])
        def f(v, acc=acc, pj=pj):
            return v.tensor_copy(kvsb[:, 1024 * pj:1024 * (pj + 1)], acc)
        S.op("vector", f, [kvsb], [acc])
    AFa.free(kvf_off, 2 * 1024)
    AFa.free(kvt_off, 2 * 1024)

    qp_off = ABa.alloc(2 * 2048)
    qpfm = ABt[:, qp_off:qp_off + 2 * 2048]
    for pj in range(2):
        S.dma("sync", qpfm[:, 2048 * pj:2048 * (pj + 1)],
              qpi[128 * pj:128 * (pj + 1), :])

    for i in range(NT):
        pb = ps[i % 2]

        def lt(pj, i=i):
            return qpfm[:, 2048 * pj + 128 * i: 2048 * pj + 128 * (i + 1)]

        def rh(pj):
            return ones[:, 0:1]
        mm_tiled(p, S, pb[:, 0:1], lt, rh, 2)
        zc = zcols[:, i:i + 1]

        def f(v, pb=pb, zc=zc):
            return v.tensor_scalar(zc, pb[:, 0:1], 1e-8, None, OP.add)
        S.op("vector", f, [zc], [pb])

        def f(v, zc=zc):
            return v.reciprocal(zc, zc)
        S.op("vector", f, [zc], [zc])

    for i in range(NT):
        for ncc in range(2):
            pb = ps[2 + (i * 2 + ncc) % 2]

            def lt(pj, i=i):
                return qpfm[:, 2048 * pj + 128 * i: 2048 * pj + 128 * (i + 1)]

            def rh(pj, ncc=ncc):
                return kvsb[:, 1024 * pj + 512 * ncc: 1024 * pj + 512 * (ncc + 1)]
            mm_tiled(p, S, pb[:, :], lt, rh, 2)
            xs = xres(i)[:, 512 * ncc:512 * (ncc + 1)]

            def f(v, pb=pb, xs=xs, i=i):
                return v.scalar_tensor_tensor(xs, pb[:, :], zcols[:, i:i + 1],
                                              xs, OP.mult, OP.add)
            S.op("vector", f, [xs], [pb, zcols, xs])
    ABa.free(qp_off, 2 * 2048)
    ABa.free(kv_off, 2048)

    xln_off = ABa.alloc(2 * 1024)
    xlnT_off = ABa.alloc(8 * 2048)
    scr_off = ABa.alloc(1024)
    xlnT = ABt[:, xlnT_off:xlnT_off + 8 * 2048]
    scr = ABt[:, scr_off:scr_off + 1024]

    def ln_T():
        for i in range(NT):
            xln = ABt[:, xln_off + 1024 * (i % 2): xln_off + 1024 * (i % 2) + 1024]
            ln_one_tile(p, S, xres(i), xln, scr, stats)
            for j in range(8):
                src = xln[:, 128 * j:128 * (j + 1)]
                dst = xlnT[:, 2048 * j + 128 * i: 2048 * j + 128 * (i + 1)]
                transpose_tile(S, psT[(i * 8 + j) % 2], ident, dst, src,
                               evac_eng=("scalar", "vector")[j % 2])

    ln_T()
    wslot_off = ABa.alloc(8 * 1024)
    wslot = ABt[:, wslot_off:wslot_off + 8 * 1024]
    for dk in range(8):
        S.dma("gpsimd", wslot[:, 1024 * dk:1024 * (dk + 1)],
              fw[128 * dk:128 * (dk + 1), :])
    fvr_off = ABa.alloc(256)
    fvrt = ABt[:, fvr_off:fvr_off + 256]
    for dk in range(8):
        S.dma("gpsimd", fvrt[:, 32 * dk:32 * (dk + 1)],
              fvr[128 * dk:128 * (dk + 1), :])
    fct_off = ABa.alloc(1024)
    fctt = ABt[:, fct_off:fct_off + 1024]
    S.dma("gpsimd", fctt[0:32, :], fct[:, :])
    fb_off = ABa.alloc(1024)
    fbt = ABt[:, fb_off:fb_off + 1024]
    S.dma("gpsimd", fbt[0:1, :], fb.unsqueeze(0))

    z2_off = ABa.alloc(2048)
    z2 = ABt[:, z2_off:z2_off + 2048]
    for tc in range(4):
        pb = ps[tc % 2]

        def lt(dk):
            return fvrt[:, 32 * dk:32 * (dk + 1)]

        def rh(dk, tc=tc):
            return xlnT[:, 2048 * dk + 512 * tc: 2048 * dk + 512 * (tc + 1)]
        mm_tiled(p, S, pb[0:32, :], lt, rh, 8)
        evac(S, z2[0:32, 512 * tc:512 * (tc + 1)], pb[0:32, :],
             eng=("scalar", "vector")[tc % 2])

    for i in range(NT):
        for ncc in range(2):
            pb = ps[(i * 2 + ncc) % 4]

            def lt(dk, i=i):
                return xlnT[:, 2048 * dk + 128 * i: 2048 * dk + 128 * (i + 1)]

            def rh(dk, ncc=ncc):
                return wslot[:, 1024 * dk + 512 * ncc: 1024 * dk + 512 * (ncc + 1)]
            mm_tiled(p, S, pb[:, :], lt, rh, 8,
                     extra_first=(ones[0:1, 0:128],
                                  fbt[0:1, 512 * ncc:512 * (ncc + 1)]))

            def f(t, pb=pb, i=i, ncc=ncc):
                return t.matmul(pb[:, :], z2[0:32, 128 * i:128 * (i + 1)],
                                fctt[0:32, 512 * ncc:512 * (ncc + 1)],
                                start=False, stop=True)
            S.op("tensor", f, [pb[:, :]], [z2, fctt])
            xs = xres(i)[:, 512 * ncc:512 * (ncc + 1)]

            def f(v, pb=pb, xs=xs):
                return v.scalar_tensor_tensor(xs, pb[:, :], 1.0, xs,
                                              OP.mult, OP.add)
            S.op("vector", f, [xs], [pb, xs])
    ABa.free(z2_off, 2048)
    ABa.free(fvr_off, 256)
    ABa.free(fct_off, 1024)
    ABa.free(fb_off, 1024)
    ABa.free(wslot_off, 8 * 1024)

    for i in range(NT):
        S.dma("sync", x2o[128 * i:128 * (i + 1), :], xres(i))

    bw_off = ABa.alloc(8 * 3072)
    bwt = ABt[:, bw_off:bw_off + 8 * 3072]
    for dk in range(8):
        S.dma("gpsimd", bwt[:, 3072 * dk:3072 * (dk + 1)],
              bwa[128 * dk:128 * (dk + 1), :])
    stg_off = ABa.alloc(2 * 3072)
    ln_T()
    for i in range(NT):
        stg = ABt[:, stg_off + 3072 * (i % 2):stg_off + 3072 * (i % 2) + 3072]
        for ncc in range(6):
            pb = ps[ncc % 4]

            def lt(dk, i=i):
                return xlnT[:, 2048 * dk + 128 * i: 2048 * dk + 128 * (i + 1)]

            def rh(dk, ncc=ncc):
                return bwt[:, 3072 * dk + 512 * ncc: 3072 * dk + 512 * (ncc + 1)]
            mm_tiled(p, S, pb[:, :], lt, rh, 8)
            scale = 0.125 if ncc < 2 else None
            evac(S, stg[:, 512 * ncc:512 * (ncc + 1)], pb[:, :],
                 eng=("scalar", "vector")[ncc % 2], scale=scale)
        S.dma("sync", qkvo[128 * i:128 * (i + 1), :], stg)

    prog_end(p, S)
    return p


def build_launch2(nh=4, ntqc=16, rev=False):
    p = Prog("l2")
    nc = p.nc
    qf = p.inp("qfm", [128, 2 * T], BF16)
    kf = p.inp("kfm", [128, 2 * T], BF16)
    vt = p.inp("vtm", [T, 256], BF16)
    yo = p.out("yfm", [128, 2 * T], BF16)

    S, AFt, ABt, AFa, ABa, ps, psT, ident, ones = prog_start(p, 4000, 62000)
    reg_consts(p, S, [1e12])

    qf_off = ABa.alloc(2 * T)
    kf_off = ABa.alloc(2 * T)
    vt_off = ABa.alloc(64 * 256)
    qfs = ABt[:, qf_off:qf_off + 2 * T]
    kfs = ABt[:, kf_off:kf_off + 2 * T]
    vts = ABt[:, vt_off:vt_off + 64 * 256]
    for j in range(2):
        S.dma("sync", qfs[:, T * j:T * (j + 1)], qf[:, T * j:T * (j + 1)])
        S.dma("sync", kfs[:, T * j:T * (j + 1)], kf[:, T * j:T * (j + 1)])
    for i in range(8):
        S.dma("scalar", vts[:, 2048 * i:2048 * (i + 1)].rearrange(
            "p (a b) -> p a b", b=256),
            vt[1024 * i:1024 * (i + 1), :].rearrange("(a p) b -> p a b", p=128))

    msk_off = ABa.alloc(4 * 512)
    io1_off = AFa.alloc(512)
    io2_off = AFa.alloc(512)
    io1 = AFt[:, io1_off:io1_off + 512]
    io2 = AFt[:, io2_off:io2_off + 512]

    def f(g):
        return g.iota(io1[:, :], [[1, 512]], channel_multiplier=0,
                      allow_small_or_imprecise_dtypes=True)
    S.op("gpsimd", f, [io1], [])

    def f(g):
        return g.iota(io2[:, 0:1], [[0, 1]], channel_multiplier=1,
                      allow_small_or_imprecise_dtypes=True)
    S.op("gpsimd", f, [io2[:, 0:1]], [])
    for j in range(4):
        mj = ABt[:, msk_off + 512 * j:msk_off + 512 * (j + 1)]

        def f(v, mj=mj, j=j):
            return v.scalar_tensor_tensor(mj, io1[:, :], -128.0 * j,
                                          io2[:, 0:1].broadcast_to((128, 512)),
                                          OP.add, OP.is_ge)
        S.op("vector", f, [mj], [io1, io2])

    att_off = ABa.alloc(3 * 512)
    sb_off = ABa.alloc(3 * 512)
    u_off = ABa.alloc(3 * 512)
    wc_off = ABa.alloc(3 * 512)
    stg_off = ABa.alloc(2 * 512)

    hl = list(range(nh))
    if rev:
        hl = hl[::-1]
    for h in hl:
        pb_ = 64 * (h % 2)
        cb = T * (h // 2)
        for tqc in range(ntqc):
            yps = ps[4 + tqc % 2]
            ns = 4 * tqc + 4
            for st in range(ns):
                r = st % 3
                sps = ps[st % 2]
                attb = ABt[:, att_off + 512 * r:att_off + 512 * (r + 1)]
                sbt = AFt[:, 1024 + 512 * r:1024 + 512 * (r + 1)] if False \
                    else ABt[:, sb_off + 512 * r:sb_off + 512 * (r + 1)]
                ut = ABt[:, u_off + 512 * r:u_off + 512 * (r + 1)]
                wct = ABt[:, wc_off + 512 * r:wc_off + 512 * (r + 1)]

                def f(t, st=st, tqc=tqc, sps=sps, pb_=pb_, cb=cb):
                    return t.matmul(
                        sps[:, :],
                        kfs[pb_:pb_ + 64, cb + 128 * st:cb + 128 * (st + 1)],
                        qfs[pb_:pb_ + 64, cb + 512 * tqc:cb + 512 * (tqc + 1)],
                        start=True, stop=True)
                S.op("tensor", f, [sps[:, :]],
                     [kfs[pb_:pb_ + 64, cb + 128 * st:cb + 128 * (st + 1)],
                      qfs[pb_:pb_ + 64, cb + 512 * tqc:cb + 512 * (tqc + 1)]])

                def f(sc, sps=sps, sbt=sbt):
                    return sc.copy(sbt, sps[:, :])
                S.op("scalar", f, [sbt], [sps])

                def f(sc, sps=sps, wct=wct):
                    return sc.activation(wct, sps[:, :], AF.Relu,
                                         bias=1e12, scale=1e10)
                S.op("scalar", f, [wct], [sps])

                def f(v, sbt=sbt, ut=ut):
                    return v.tensor_scalar(ut, sbt, -1e-5, 1e-3, OP.mult, OP.add)
                S.op("vector", f, [ut], [sbt])

                def f(v, sbt=sbt, wct2=attb):
                    return v.tensor_scalar(wct2, sbt, 0.0, None, OP.max)
                S.op("vector", f, [attb], [sbt])

                def f(v, sbt=sbt, attb=attb):
                    return v.scalar_tensor_tensor(attb, sbt, 0.01, attb,
                                                  OP.add, OP.mult)
                S.op("vector", f, [attb], [sbt, attb])

                def f(v, ut=ut, wct=wct):
                    return v.tensor_tensor(ut, ut, wct, OP.min)
                S.op("vector", f, [ut], [ut, wct])

                def f(v, attb=attb, ut=ut):
                    return v.scalar_tensor_tensor(attb, attb, 1e-3, ut,
                                                  OP.mult, OP.add)
                S.op("vector", f, [attb], [attb, ut])

                dj = st - 4 * tqc
                if dj >= 0:
                    mj = ABt[:, msk_off + 512 * dj:msk_off + 512 * (dj + 1)]

                    def f(v, attb=attb, mj=mj):
                        return v.tensor_tensor(attb, attb, mj, OP.mult)
                    S.op("vector", f, [attb], [attb, mj])

                def f(t, st=st, h=h, attb=attb, yps=yps, ns=ns):
                    return t.matmul(
                        yps[0:64, :],
                        vts[:, 256 * st + 64 * h:256 * st + 64 * h + 64],
                        attb, start=(st == 0), stop=(st == ns - 1))
                S.op("tensor", f, [yps[0:64, :]],
                     [vts[:, 256 * st + 64 * h:256 * st + 64 * h + 64], attb],
                     inc=(st == ns - 1))
            stg = ABt[:, stg_off + 512 * (tqc % 2):stg_off + 512 * (tqc % 2) + 512]

            def f(v, yps=yps, stg=stg):
                return v.tensor_copy(stg[0:64, :], yps[0:64, :])
            S.op("vector", f, [stg[0:64, :]], [yps[0:64, :]])
            S.dma("sync", yo[pb_:pb_ + 64, cb + 512 * tqc:cb + 512 * (tqc + 1)],
                  stg[0:64, :])

    prog_end(p, S)
    return p


def build_launch3():
    p = Prog("l3")
    x2i = p.inp("x2", [TPC, D])
    yT = p.inp("yT", [D, TPC], BF16)
    bwp = p.inp("bWproj", [D, D])
    fw = p.inp("fWT", [D, D])
    fb = p.inp("fb", [D])
    fvr = p.inp("fVr", [D, 32])
    fct = p.inp("fCT", [32, D])
    dwq = p.inp("dWqT", [D, 128])
    dwk = p.inp("dWkT", [D, 128])
    dwv = p.inp("dWvT", [D, 128])
    dwg = p.inp("dWgT", [D, 128])
    dwab = p.inp("dWabT", [D, 2])
    dbias = p.inp("dbias", [8, 128])

    x4o = p.out("x4", [TPC, D])
    dqo = p.out("dq", [TPC, 128], BF16)
    dko = p.out("dk", [TPC, 128], BF16)
    dvo = p.out("dv", [TPC, 128], BF16)
    dgo = p.out("dgate", [TPC, 128], BF16)
    dabo = p.out("dab", [TPC, 2])

    S, AFt, ABt, AFa, ABa, ps, psT, ident, ones = prog_start(p, 24000, 54500)

    xres_off = AFa.alloc(NT * 1024)
    xres = lambda i: AFt[:, xres_off + 1024 * i: xres_off + 1024 * (i + 1)]
    st_off = AFa.alloc(64)
    stats = AFt[:, st_off:st_off + 64]

    for i in range(NT):
        S.dma("sync", xres(i), x2i[128 * i:128 * (i + 1), :])

    yT_off = ABa.alloc(8 * 2048)
    yTs = ABt[:, yT_off:yT_off + 8 * 2048]
    for dk_ in range(8):
        S.dma("gpsimd", yTs[:, 2048 * dk_:2048 * (dk_ + 1)],
              yT[128 * dk_:128 * (dk_ + 1), :])
    wslot_off = ABa.alloc(8 * 1024)
    wslot = ABt[:, wslot_off:wslot_off + 8 * 1024]
    for dk_ in range(8):
        S.dma("gpsimd", wslot[:, 1024 * dk_:1024 * (dk_ + 1)],
              bwp[128 * dk_:128 * (dk_ + 1), :])
    for i in range(NT):
        for ncc in range(2):
            pb = ps[(i * 2 + ncc) % 4]

            def lt(dk_, i=i):
                return yTs[:, 2048 * dk_ + 128 * i: 2048 * dk_ + 128 * (i + 1)]

            def rh(dk_, ncc=ncc):
                return wslot[:, 1024 * dk_ + 512 * ncc: 1024 * dk_ + 512 * (ncc + 1)]
            mm_tiled(p, S, pb[:, :], lt, rh, 8)
            xs = xres(i)[:, 512 * ncc:512 * (ncc + 1)]

            def f(v, pb=pb, xs=xs):
                return v.scalar_tensor_tensor(xs, pb[:, :], 1.0, xs,
                                              OP.mult, OP.add)
            S.op("vector", f, [xs], [pb, xs])
    ABa.free(yT_off, 8 * 2048)

    xln_off = ABa.alloc(2 * 1024)
    xlnT_off = ABa.alloc(8 * 2048)
    scr_off = ABa.alloc(1024)
    xlnT = ABt[:, xlnT_off:xlnT_off + 8 * 2048]
    scr = ABt[:, scr_off:scr_off + 1024]

    def ln_T():
        for i in range(NT):
            xln = ABt[:, xln_off + 1024 * (i % 2): xln_off + 1024 * (i % 2) + 1024]
            ln_one_tile(p, S, xres(i), xln, scr, stats)
            for j in range(8):
                src = xln[:, 128 * j:128 * (j + 1)]
                dst = xlnT[:, 2048 * j + 128 * i: 2048 * j + 128 * (i + 1)]
                transpose_tile(S, psT[(i * 8 + j) % 2], ident, dst, src,
                               evac_eng=("scalar", "vector")[j % 2])

    ln_T()
    for dk_ in range(8):
        S.dma("gpsimd", wslot[:, 1024 * dk_:1024 * (dk_ + 1)],
              fw[128 * dk_:128 * (dk_ + 1), :])
    fvr_off = ABa.alloc(256)
    fvrt = ABt[:, fvr_off:fvr_off + 256]
    for dk_ in range(8):
        S.dma("gpsimd", fvrt[:, 32 * dk_:32 * (dk_ + 1)],
              fvr[128 * dk_:128 * (dk_ + 1), :])
    fct_off = ABa.alloc(1024)
    fctt = ABt[:, fct_off:fct_off + 1024]
    S.dma("gpsimd", fctt[0:32, :], fct[:, :])
    fb_off = ABa.alloc(1024)
    fbt = ABt[:, fb_off:fb_off + 1024]
    S.dma("gpsimd", fbt[0:1, :], fb.unsqueeze(0))

    z2_off = ABa.alloc(2048)
    z2 = ABt[:, z2_off:z2_off + 2048]
    for tc in range(4):
        pb = ps[tc % 2]

        def lt(dk_):
            return fvrt[:, 32 * dk_:32 * (dk_ + 1)]

        def rh(dk_, tc=tc):
            return xlnT[:, 2048 * dk_ + 512 * tc: 2048 * dk_ + 512 * (tc + 1)]
        mm_tiled(p, S, pb[0:32, :], lt, rh, 8)
        evac(S, z2[0:32, 512 * tc:512 * (tc + 1)], pb[0:32, :],
             eng=("scalar", "vector")[tc % 2])

    for i in range(NT):
        for ncc in range(2):
            pb = ps[(i * 2 + ncc) % 4]

            def lt(dk_, i=i):
                return xlnT[:, 2048 * dk_ + 128 * i: 2048 * dk_ + 128 * (i + 1)]

            def rh(dk_, ncc=ncc):
                return wslot[:, 1024 * dk_ + 512 * ncc: 1024 * dk_ + 512 * (ncc + 1)]
            mm_tiled(p, S, pb[:, :], lt, rh, 8,
                     extra_first=(ones[0:1, 0:128],
                                  fbt[0:1, 512 * ncc:512 * (ncc + 1)]))

            def f(t, pb=pb, i=i, ncc=ncc):
                return t.matmul(pb[:, :], z2[0:32, 128 * i:128 * (i + 1)],
                                fctt[0:32, 512 * ncc:512 * (ncc + 1)],
                                start=False, stop=True)
            S.op("tensor", f, [pb[:, :]], [z2, fctt])
            xs = xres(i)[:, 512 * ncc:512 * (ncc + 1)]

            def f(v, pb=pb, xs=xs):
                return v.scalar_tensor_tensor(xs, pb[:, :], 1.0, xs,
                                              OP.mult, OP.add)
            S.op("vector", f, [xs], [pb, xs])
    ABa.free(z2_off, 2048)
    ABa.free(fvr_off, 256)
    ABa.free(fct_off, 1024)
    ABa.free(fb_off, 1024)

    for i in range(NT):
        S.dma("sync", x4o[128 * i:128 * (i + 1), :], xres(i))

    ln_T()
    dw_off = ABa.alloc(4 * 8 * 128 + 8 * 2)
    dws = ABt[:, dw_off:dw_off + 4 * 1024 + 16]
    for wi, wdram in enumerate([dwq, dwk, dwv, dwg]):
        for dk_ in range(8):
            S.dma("gpsimd", dws[:, 1024 * wi + 128 * dk_:1024 * wi + 128 * (dk_ + 1)],
                  wdram[128 * dk_:128 * (dk_ + 1), :])
    for dk_ in range(8):
        S.dma("gpsimd", dws[:, 4096 + 2 * dk_:4096 + 2 * (dk_ + 1)],
              dwab[128 * dk_:128 * (dk_ + 1), :])
    db_off = ABa.alloc(8 * 128)
    dbt = ABt[:, db_off:db_off + 8 * 128]
    for r in range(8):
        S.dma("gpsimd", dbt[r:r + 1, 0:128] if r == 0 else dbt[0:1, 128 * r:128 * (r + 1)],
              dbias[r:r + 1, :])

    nrm_off = AFa.alloc(8)
    nrm = AFt[:, nrm_off:nrm_off + 8]
    kf_off = AFa.alloc(512)
    kf32 = AFt[:, kf_off:kf_off + 512]
    stg_off = ABa.alloc(2 * 128)

    for i in range(NT):
        for wi, (outd, brow) in enumerate([(dqo, 0), (dko, 1), (dvo, 2),
                                           (dgo, 3)]):
            pb = ps[wi % 2]

            def lt(dk_, i=i):
                return xlnT[:, 2048 * dk_ + 128 * i: 2048 * dk_ + 128 * (i + 1)]

            def rh(dk_, wi=wi):
                return dws[:, 1024 * wi + 128 * dk_:1024 * wi + 128 * (dk_ + 1)]
            mm_tiled(p, S, pb[:, 0:128], lt, rh, 8,
                     extra_first=(ones[0:1, 0:128],
                                  dbt[0:1, 128 * brow:128 * (brow + 1)]))
            stg = ABt[:, stg_off:stg_off + 128]
            if wi == 1:
                def f(v, pb=pb):
                    return v.tensor_copy(kf32[:, 0:128], pb[:, 0:128])
                S.op("vector", f, [kf32[:, 0:128]], [pb])

                def f(v, stg=stg):
                    return v.scalar_tensor_tensor(
                        stg, kf32[:, 0:128], 1.0, kf32[:, 0:128],
                        OP.mult, OP.mult, accum_out=nrm[:, 0:1])
                S.op("vector", f, [stg, nrm[:, 0:1]], [kf32[:, 0:128]])

                def f(sc):
                    return sc.activation(nrm[:, 1:2], nrm[:, 0:1], AF.Sqrt,
                                         bias=0.0, scale=1.0)
                S.op("scalar", f, [nrm[:, 1:2]], [nrm[:, 0:1]])

                def f(v):
                    return v.tensor_scalar(nrm[:, 2:3], nrm[:, 1:2], 1e-12,
                                           None, OP.max)
                S.op("vector", f, [nrm[:, 2:3]], [nrm[:, 1:2]])

                def f(v):
                    return v.reciprocal(nrm[:, 3:4], nrm[:, 2:3])
                S.op("vector", f, [nrm[:, 3:4]], [nrm[:, 2:3]])

                def f(v, stg=stg):
                    return v.tensor_scalar(stg, kf32[:, 0:128], nrm[:, 3:4],
                                           None, OP.mult)
                S.op("vector", f, [stg], [kf32[:, 0:128], nrm])
            elif wi == 3:
                def f(sc, pb=pb, stg=stg):
                    return sc.activation(stg, pb[:, 0:128], AF.Sigmoid,
                                         bias=0.0, scale=1.0)
                S.op("scalar", f, [stg], [pb])
            else:
                evac(S, stg, pb[:, 0:128], eng=("scalar", "vector")[wi % 2])
            S.dma("sync", outd[128 * i:128 * (i + 1), :], stg)
        pb = ps[4]

        def lt(dk_, i=i):
            return xlnT[:, 2048 * dk_ + 128 * i: 2048 * dk_ + 128 * (i + 1)]

        def rh(dk_):
            return dws[:, 4096 + 2 * dk_:4096 + 2 * (dk_ + 1)]
        mm_tiled(p, S, pb[:, 0:2], lt, rh, 8,
                 extra_first=(ones[0:1, 0:128], dbt[0:1, 512 + 0:512 + 2]))

        def f(sc, pb=pb):
            return sc.activation(kf32[:, 0:2], pb[:, 0:2], AF.Sigmoid,
                                 bias=0.0, scale=1.0)
        S.op("scalar", f, [kf32[:, 0:2]], [pb])
        S.dma("sync", dabo[128 * i:128 * (i + 1), :], kf32[:, 0:2])

    prog_end(p, S)
    return p


def build_launch4(dbg=False):
    p = Prog("l4")
    NCH = 64
    kfm = p.inp("kfm", [128, T], BF16)
    ktm = p.inp("ktm", [T, 128], BF16)
    qpk = p.inp("qpk", [32, T], BF16)
    vbl = p.inp("vbl", [T, 32], BF16)
    abr = p.inp("abr", [NCH, 128])
    bbr = p.inp("bbr", [NCH, 128])
    bcT = p.inp("bcT", [128, NCH])
    opo = p.out("opart", [T, 128])
    if dbg:
        dbgo = p.out("dbg", [128, 6 * 128], mybir.dt.float32)
        p.dbg2o = p.out("dbg2", [128, 3 * 128], mybir.dt.float32)

    S, AFt, ABt, AFa, ABa, ps, psT, ident, ones = prog_start(p, 8000, 60000)

    kf_off = ABa.alloc(T)
    kfs = ABt[:, kf_off:kf_off + T]
    kt_off = ABa.alloc(T)
    kts = ABt[:, kt_off:kt_off + T]
    qp_off = ABa.alloc(T)
    qps = ABt[:, qp_off:qp_off + T]
    vb_off = ABa.alloc(NCH * 32)
    vbs = ABt[:, vb_off:vb_off + NCH * 32]
    for j in range(4):
        S.dma("sync", kfs[:, 2048 * j:2048 * (j + 1)],
              kfm[:, 2048 * j:2048 * (j + 1)])
        S.dma("sync",
              kts[:, 2048 * j:2048 * (j + 1)].rearrange("p (a b) -> p a b", b=128),
              ktm[2048 * j:2048 * (j + 1), :].rearrange("(a p) b -> p a b", p=128))
        S.dma("sync", qps[0:32, 2048 * j:2048 * (j + 1)],
              qpk[:, 2048 * j:2048 * (j + 1)])
    S.dma("sync", vbs[:, :].rearrange("p (a b) -> p a b", b=32),
          vbl.rearrange("(a p) b -> p a b", p=128))
    ab_off = AFa.alloc(128)
    abs_ = AFt[0:NCH, ab_off:ab_off + 128]
    bb_off = AFa.alloc(128)
    bbs = AFt[0:NCH, bb_off:bb_off + 128]
    bcT_off = AFa.alloc(NCH)
    bcTs = AFt[:, bcT_off:bcT_off + NCH]
    S.dma("sync", abs_, abr[:, :])
    S.dma("sync", bbs, bbr[:, :])
    S.dma("sync", bcTs, bcT[:, :])

    pr_off = AFa.alloc(8 * NCH + 4 * 128 + 512)
    la_all = AFt[0:NCH, pr_off + 8 * NCH:pr_off + 8 * NCH + 128]
    lab_all = AFt[0:NCH, pr_off + 8 * NCH + 128:pr_off + 8 * NCH + 256]
    lnb = AFt[0:NCH, pr_off + 8 * NCH + 256:pr_off + 8 * NCH + 384]
    laT = AFt[:, pr_off:pr_off + NCH]
    labT = AFt[:, pr_off + NCH:pr_off + 2 * NCH]
    nlaT = AFt[:, pr_off + 2 * NCH:pr_off + 3 * NCH]
    alT = AFt[:, pr_off + 3 * NCH:pr_off + 4 * NCH]
    gaT = AFt[:, pr_off + 4 * NCH:pr_off + 5 * NCH]
    aCb = AFt[:, pr_off + 5 * NCH:pr_off + 6 * NCH]
    bal = AFt[:, pr_off + 6 * NCH:pr_off + 7 * NCH]
    scr1 = AFt[:, pr_off + 7 * NCH:pr_off + 8 * NCH]

    def f(sc):
        return sc.activation(la_all, abs_, AF.Ln, bias=0.0, scale=1.0)
    S.op("scalar", f, [la_all], [abs_])

    def f(v):
        return v.tensor_tensor_scan(la_all, la_all, la_all, 0.0, OP.add,
                                    OP.bypass)
    S.op("vector", f, [la_all], [la_all])

    def f(sc):
        return sc.activation(lnb, bbs, AF.Ln, bias=0.0, scale=1.0)
    S.op("scalar", f, [lnb], [bbs])

    def f(v):
        return v.tensor_tensor(lab_all, la_all, lnb, OP.add)
    S.op("vector", f, [lab_all], [la_all, lnb])

    def t32(dst, src, n_in, n_out):
        def f(t):
            return t.transpose(ps[5][0:n_out, 0:n_in], src,
                               p.identF[0:n_in, 0:n_in])
        S.op("tensor", f, [ps[5][0:n_out, 0:n_in]], [src, ident[:, :]])

        def f(v):
            return v.tensor_copy(dst, ps[5][0:n_out, 0:n_in])
        S.op("vector", f, [dst], [ps[5][0:n_out, 0:n_in]])

    t32(laT, la_all, NCH, 128)
    t32(labT, lab_all, NCH, 128)

    def f(v):
        return v.tensor_scalar(nlaT, laT, -1.0, None, OP.mult)
    S.op("vector", f, [nlaT], [laT])

    def f(sc):
        return sc.activation(alT, laT, AF.Exp, bias=0.0, scale=1.0)
    S.op("scalar", f, [alT], [laT])

    onesF_off = AFa.alloc(128)
    onesF = AFt[:, onesF_off:onesF_off + 128]

    def f(g):
        return g.memset(onesF[0:1, :], 1.0)
    S.op("gpsimd", f, [onesF[0:1, :]], [])

    laCc = AFt[0:NCH, pr_off + 8 * NCH + 384:pr_off + 8 * NCH + 385]

    def f(v):
        return v.tensor_copy(laCc, la_all[0:NCH, 127:128])
    S.op("vector", f, [laCc], [la_all])
    laCr = AFt[0:1, pr_off + 8 * NCH + 386:pr_off + 8 * NCH + 386 + NCH]
    t32(laCr, laCc, NCH, 1)
    onesb = ones

    def f(t):
        return t.matmul(ps[4][:, 0:NCH], onesF[0:1, 0:128], laCr,
                        start=True, stop=True)
    S.op("tensor", f, [ps[4][:, 0:NCH]], [onesF, laCr])

    def f(sc):
        return sc.activation(aCb, ps[4][:, 0:NCH], AF.Exp, bias=0.0, scale=1.0)
    S.op("scalar", f, [aCb], [ps[4]])

    def f(v):
        return v.scalar_tensor_tensor(scr1, laT, -1.0, ps[4][:, 0:NCH],
                                      OP.mult, OP.add)
    S.op("vector", f, [scr1], [laT, ps[4]])

    def f(sc):
        return sc.activation(gaT, scr1, AF.Exp, bias=0.0, scale=1.0)
    S.op("scalar", f, [gaT], [scr1])

    def f(v):
        return v.scalar_tensor_tensor(bal, bcTs, -1.0, alT, OP.mult, OP.mult)
    S.op("vector", f, [bal], [bcTs, alT])

    mU_off = ABa.alloc(2 * 128)
    strictUneg = ABt[:, mU_off:mU_off + 128]
    maskUincl = ABt[:, mU_off + 128:mU_off + 256]
    io1 = AFt[:, pr_off + 8 * NCH + 512:pr_off + 8 * NCH + 512 + 128]
    io2 = AFt[:, pr_off + 8 * NCH + 640:pr_off + 8 * NCH + 641]

    def f(g):
        return g.iota(io1, [[1, 128]], channel_multiplier=0,
                      allow_small_or_imprecise_dtypes=True)
    S.op("gpsimd", f, [io1], [])

    def f(g):
        return g.iota(io2, [[0, 1]], channel_multiplier=1,
                      allow_small_or_imprecise_dtypes=True)
    S.op("gpsimd", f, [io2], [])

    def f(v):
        return v.scalar_tensor_tensor(maskUincl, io1, 0.0,
                                      io2.broadcast_to((128, 128)),
                                      OP.add, OP.is_ge)
    S.op("vector", f, [maskUincl], [io1, io2])

    def f(v):
        return v.scalar_tensor_tensor(scr1[:, 0:128] if False else strictUneg,
                                      io1, 0.0,
                                      io2.broadcast_to((128, 128)),
                                      OP.add, OP.is_gt)
    S.op("vector", f, [strictUneg], [io1, io2])

    def f(v):
        return v.tensor_scalar(strictUneg, strictUneg, -1.0, None, OP.mult)
    S.op("vector", f, [strictUneg], [strictUneg])

    TT_off = ABa.alloc(NCH * 128)
    Tts = ABt[:, TT_off:TT_off + NCH * 128]
    GG_off = ABa.alloc(NCH * 128)
    Ggs = ABt[:, GG_off:GG_off + NCH * 128]
    w1_off = ABa.alloc(6 * 128)
    wrk = lambda i: ABt[:, w1_off + 128 * i:w1_off + 128 * (i + 1)]
    rrow_off = AFa.alloc(256)
    larow = AFt[0:1, rrow_off:rrow_off + 128]
    labrow = AFt[0:1, rrow_off + 128:rrow_off + 256]
    o1x_off = AFa.alloc(256)

    def bf_t(dst, src):
        def f(t):
            return t.transpose(psT[0][:, 0:128], src, ident[:, :])
        S.op("tensor", f, [psT[0][:, 0:128]], [src, ident[:, :]])

        def f(v):
            return v.tensor_copy(dst, psT[0][:, 0:128])
        S.op("vector", f, [dst], [psT[0][:, 0:128]])

    for ch in range(NCH):
        Tt = Tts[:, 128 * ch:128 * (ch + 1)]
        Gg = Ggs[:, 128 * ch:128 * (ch + 1)]
        def f(t, ch=ch):
            return t.matmul(ps[0][:, 0:128], kfs[:, 128 * ch:128 * (ch + 1)],
                            kfs[:, 128 * ch:128 * (ch + 1)], start=True,
                            stop=True)
        S.op("tensor", f, [ps[0][:, 0:128]],
             [kfs[:, 128 * ch:128 * (ch + 1)]])
        t32(larow, laT[:, ch:ch + 1], 128, 1)
        t32(labrow, labT[:, ch:ch + 1], 128, 1)

        def f(t):
            return t.matmul(ps[1][:, 0:128], onesF[0:1, 0:128], labrow,
                            start=True, stop=True)
        S.op("tensor", f, [ps[1][:, 0:128]], [onesF, labrow])

        def f(t):
            return t.matmul(ps[2][:, 0:128], onesF[0:1, 0:128], larow,
                            start=True, stop=True)
        S.op("tensor", f, [ps[2][:, 0:128]], [onesF, larow])

        A1 = wrk(0)
        ex1 = AFt[:, o1x_off:o1x_off + 128]
        ex2 = AFt[:, o1x_off + 128:o1x_off + 256]

        def f(v, ch=ch):
            return v.tensor_scalar(ex1, ps[1][:, 0:128], nlaT[:, ch:ch + 1],
                                   None, OP.add)
        S.op("vector", f, [ex1], [ps[1], nlaT])

        def f(v):
            return v.tensor_scalar(ex1, ex1, 0.0, None, OP.min)
        S.op("vector", f, [ex1], [ex1])

        def f(sc, A1=A1):
            return sc.activation(A1, ex1, AF.Exp, bias=0.0, scale=1.0)
        S.op("scalar", f, [A1], [ex1])

        def f(v, ch=ch):
            return v.tensor_scalar(ex2, ps[2][:, 0:128], nlaT[:, ch:ch + 1],
                                   None, OP.add)
        S.op("vector", f, [ex2], [ps[2], nlaT])

        def f(v):
            return v.tensor_scalar(ex2, ex2, 0.0, None, OP.min)
        S.op("vector", f, [ex2], [ex2])

        def f(sc, Gg=Gg):
            return sc.activation(Gg, ex2, AF.Exp, bias=0.0, scale=1.0)
        S.op("scalar", f, [Gg], [ex2])

        if dbg and ch == 0:
            dstg2_off = AFa.alloc(3 * 128)
            dstg2 = AFt[:, dstg2_off:dstg2_off + 3 * 128]

            def f(v):
                return v.tensor_copy(dstg2[0:1, 0:128], larow)
            S.op("vector", f, [dstg2[:, 0:128]], [larow])

            def f(v):
                return v.tensor_copy(dstg2[:, 128:256], ps[2][:, 0:128])
            S.op("vector", f, [dstg2[:, 128:256]], [ps[2]])

            def f(v):
                return v.tensor_copy(dstg2[:, 256:384], ex2)
            S.op("vector", f, [dstg2[:, 256:384]], [ex2])
            S.dma("sync", p.dbg2o, dstg2)

        def f(v, Gg=Gg):
            return v.tensor_tensor(Gg, Gg, maskUincl, OP.mult)
        S.op("vector", f, [Gg], [Gg, maskUincl])

        Nt = wrk(1)

        def f(v, A1=A1, Nt=Nt):
            return v.tensor_tensor(Nt, A1, ps[0][:, 0:128], OP.mult)
        S.op("vector", f, [Nt], [A1, ps[0]])

        def f(v, Nt=Nt):
            return v.tensor_tensor(Nt, Nt, strictUneg, OP.mult)
        S.op("vector", f, [Nt], [Nt, strictUneg])

        curT = wrk(2)

        def f(v, curT=curT, Nt=Nt):
            return v.tensor_copy(curT, Nt)
        S.op("vector", f, [curT], [Nt])

        def f(v, Tt=Tt, Nt=Nt):
            return v.tensor_tensor(Tt, Nt, ident[:, :], OP.add)
        S.op("vector", f, [Tt], [Nt, ident])
        for j in range(1, 7):
            un = wrk(3)
            bf_t(un, curT)
            def f(t, un=un, curT=curT):
                return t.matmul(ps[3][:, 0:128], curT, un,
                                start=True, stop=True)
            S.op("tensor", f, [ps[3][:, 0:128]], [un, curT])
            squn = wrk(4)

            def f(v, squn=squn):
                return v.tensor_copy(squn, ps[3][:, 0:128])
            S.op("vector", f, [squn], [ps[3][:, 0:128]])
            if j < 6:
                def f(t, un=un, curT=curT):
                    return t.matmul(ps[4][:, 0:128], un, curT,
                                    start=True, stop=True)
                S.op("tensor", f, [ps[4][:, 0:128]], [un, curT])
            unI = wrk(5)

            def f(v, unI=unI, squn=squn):
                return v.tensor_tensor(unI, squn, ident[:, :], OP.add)
            S.op("vector", f, [unI], [squn, ident])

            def f(t, unI=unI, Tt=Tt):
                return t.matmul(ps[2][:, 0:128], unI, Tt, start=True,
                                stop=True)
            S.op("tensor", f, [ps[2][:, 0:128]], [unI, Tt])

            def f(v, Tt=Tt):
                return v.tensor_copy(Tt, ps[2][:, 0:128])
            S.op("vector", f, [Tt], [ps[2][:, 0:128]])
            if j < 6:
                def f(v, curT=curT):
                    return v.tensor_copy(curT, ps[4][:, 0:128])
                S.op("vector", f, [curT], [ps[4][:, 0:128]])

    s0_off = AFa.alloc(32)
    S0T = AFt[:, s0_off:s0_off + 32]

    def f(g):
        return g.memset(S0T, 0.0)
    S.op("gpsimd", f, [S0T], [])
    sb_off = ABa.alloc(4 * 128)
    S0bf = ABt[0:32, sb_off:sb_off + 128]
    S0Tbf = ABt[:, sb_off + 128:sb_off + 160]
    Rbf = ABt[:, sb_off + 160:sb_off + 192]
    Ybf = ABt[:, sb_off + 192:sb_off + 224]
    Ypr = ABt[:, sb_off + 224:sb_off + 256]
    YTbf = ABt[0:32, sb_off + 256:sb_off + 384]
    W2m = ABt[:, sb_off + 384:sb_off + 512]
    t1b = ABt[:, sb_off + 512:sb_off + 544]
    o1_off = AFa.alloc(256)
    o1f = AFt[:, o1_off:o1_off + 128]
    ostg = AFt[:, o1_off + 128:o1_off + 256]

    for ch in range(NCH):
        qch = qps[0:32, 128 * ch:128 * (ch + 1)]
        kfc = kfs[:, 128 * ch:128 * (ch + 1)]
        ktc = kts[:, 128 * ch:128 * (ch + 1)]
        vc = vbs[:, 32 * ch:32 * (ch + 1)]
        Tt = Tts[:, 128 * ch:128 * (ch + 1)]
        Gg = Ggs[:, 128 * ch:128 * (ch + 1)]

        def f(v):
            return v.tensor_copy(S0Tbf, S0T)
        S.op("vector", f, [S0Tbf], [S0T])
        def f(t):
            return t.transpose(ps[5][0:32, 0:128], S0T, p.identF[:, :])
        S.op("tensor", f, [ps[5][0:32, 0:128]], [S0T, p.identF])

        def f(v):
            return v.tensor_copy(S0bf, ps[5][0:32, 0:128])
        S.op("vector", f, [S0bf], [ps[5][0:32, 0:128]])

        def f(t, kfc=kfc):
            return t.matmul(ps[0][:, 0:32], kfc, S0Tbf, start=True, stop=True)
        S.op("tensor", f, [ps[0][:, 0:32]], [kfc, S0Tbf])

        def f(v, vc=vc, ch=ch):
            return v.tensor_scalar(t1b, vc, bcTs[:, ch:ch + 1], None, OP.mult)
        S.op("vector", f, [t1b], [vc, bcTs])

        def f(v, ch=ch):
            return v.scalar_tensor_tensor(Rbf, ps[0][:, 0:32],
                                          bal[:, ch:ch + 1], t1b,
                                          OP.mult, OP.add)
        S.op("vector", f, [Rbf], [ps[0], bal, t1b])

        def f(t, Tt=Tt):
            return t.matmul(ps[1][:, 0:32], Tt, Rbf, start=True, stop=True)
        S.op("tensor", f, [ps[1][:, 0:32]], [Tt, Rbf])

        def f(v):
            return v.tensor_copy(Ybf, ps[1][:, 0:32])
        S.op("vector", f, [Ybf], [ps[1]])

        def f(v, ch=ch):
            return v.tensor_scalar(Ypr, ps[1][:, 0:32], gaT[:, ch:ch + 1],
                                   None, OP.mult)
        S.op("vector", f, [Ypr], [ps[1], gaT])

        bf_t(YTbf, Ybf) if False else None
        def f(t):
            return t.transpose(psT[0][0:32, 0:128], Ybf, ident[:, :])
        S.op("tensor", f, [psT[0][0:32, 0:128]], [Ybf, ident])

        def f(v):
            return v.tensor_copy(YTbf, psT[0][0:32, 0:128])
        S.op("vector", f, [YTbf], [psT[0][0:32, 0:128]])

        def f(t, qch=qch):
            return t.matmul(ps[2][:, 0:128], YTbf, qch, start=True, stop=True)
        S.op("tensor", f, [ps[2][:, 0:128]], [YTbf, qch])

        def f(v, Gg=Gg):
            return v.tensor_tensor(W2m, ps[2][:, 0:128], Gg, OP.mult)
        S.op("vector", f, [W2m], [ps[2], Gg])

        def f(t, qch=qch):
            return t.matmul(ps[3][:, 0:128], qch, S0bf, start=True, stop=True)
        S.op("tensor", f, [ps[3][:, 0:128]], [qch, S0bf])

        def f(v, ch=ch):
            return v.tensor_scalar(o1f, ps[3][:, 0:128], alT[:, ch:ch + 1],
                                   None, OP.mult)
        S.op("vector", f, [o1f], [ps[3], alT])

        def f(t, ktc=ktc):
            return t.matmul(ps[4][:, 0:128], W2m, ktc, start=True, stop=True)
        S.op("tensor", f, [ps[4][:, 0:128]], [W2m, ktc])

        def f(v):
            return v.scalar_tensor_tensor(ostg, ps[4][:, 0:128], 1.0, o1f,
                                          OP.mult, OP.add)
        S.op("vector", f, [ostg], [ps[4], o1f])
        S.dma("sync", opo[128 * ch:128 * (ch + 1), :], ostg)

        def f(t, ktc=ktc):
            return t.matmul(ps[5][:, 0:32], ktc, Ypr, start=True, stop=True)
        S.op("tensor", f, [ps[5][:, 0:32]], [ktc, Ypr])

        def f(v, ch=ch):
            return v.scalar_tensor_tensor(S0T, S0T, aCb[:, ch:ch + 1],
                                          ps[5][:, 0:32], OP.mult, OP.add)
        S.op("vector", f, [S0T], [S0T, aCb, ps[5]])

        if dbg and ch == 0:
            dstg_off = AFa.alloc(6 * 128)
            dstg = AFt[:, dstg_off:dstg_off + 6 * 128]
            for di, src_ap in enumerate([nlaT[:, 0:64], la_all[0:64, 0:128],
                                         ex2, Ggs[:, 0:128],
                                         Tts[:, 0:128], ostg]):
                np_ = src_ap.shape[0]

                def f(v, di=di, src_ap=src_ap, np_=np_):
                    return v.tensor_copy(
                        dstg[0:np_, 128 * di:128 * di + src_ap.shape[1]],
                        src_ap)
                S.op("vector", f, [dstg[:, 128 * di:128 * (di + 1)]],
                     [src_ap])
            S.dma("sync", dbgo, dstg)

    prog_end(p, S)
    return p


def emit_ffn4(p, S, AFt, ABt, AFa, ABa, ps, psT, ident, ones, xres, stats,
              wgT, wuT, wdT):
    I = 2816
    IH = 1408
    wg_off = ABa.alloc(8 * IH)
    wgs = ABt[:, wg_off:wg_off + 8 * IH]
    wu_off = ABa.alloc(8 * IH)
    wus = ABt[:, wu_off:wu_off + 8 * IH]
    wd_off = ABa.alloc(11 * 1024)
    wds = ABt[:, wd_off:wd_off + 11 * 1024]
    lnf_off = AFa.alloc(1024)
    lnf = AFt[:, lnf_off:lnf_off + 1024]
    xq_off = ABa.alloc(1024)
    xq = ABt[:, xq_off:xq_off + 1024]
    xqT_off = ABa.alloc(8 * 128)
    xqT = ABt[:, xqT_off:xqT_off + 8 * 128]
    g_off = ABa.alloc(IH)
    gb = ABt[:, g_off:g_off + IH]
    u_off = ABa.alloc(IH)
    ub = ABt[:, u_off:u_off + IH]
    h_off = AFa.alloc(IH)
    hf = AFt[:, h_off:h_off + IH]
    hq_off = ABa.alloc(IH)
    hq = ABt[:, hq_off:hq_off + IH]
    hqT_off = ABa.alloc(11 * 128)
    hqT = ABt[:, hqT_off:hqT_off + 11 * 128]
    scr_off = ABa.alloc(1024)
    scr = ABt[:, scr_off:scr_off + 1024]

    CH = [512, 512, 384]
    for half in range(2):
        h0 = IH * half
        for dk_ in range(8):
            S.dma("gpsimd", wgs[:, IH * dk_:IH * (dk_ + 1)],
                  wgT[128 * dk_:128 * (dk_ + 1), h0:h0 + IH])
            S.dma("gpsimd", wus[:, IH * dk_:IH * (dk_ + 1)],
                  wuT[128 * dk_:128 * (dk_ + 1), h0:h0 + IH])
        for dk_ in range(11):
            S.dma("gpsimd", wds[:, 1024 * dk_:1024 * (dk_ + 1)],
                  wdT[128 * (11 * half + dk_):128 * (11 * half + dk_ + 1), :])
        for i in range(NT):
            ln_one_tile(p, S, xres(i), lnf, scr, stats)
            fp4_quant_tiles(p, S, AFa, ABa, lnf, xq, 128, 1024, "xq")
            for j in range(8):
                transpose_tile(S, psT[j % 2], ident,
                               xqT[:, 128 * j:128 * (j + 1)],
                               xq[:, 128 * j:128 * (j + 1)],
                               evac_eng=("scalar", "vector")[j % 2])
            c0 = 0
            for nc_i, w in enumerate(CH):
                pbg = ps[nc_i % 2]
                pbu = ps[2 + nc_i % 2]

                def ltx(dk_):
                    return xqT[:, 128 * dk_:128 * (dk_ + 1)]

                def rhg(dk_, c0=c0, w=w):
                    return wgs[:, IH * dk_ + c0:IH * dk_ + c0 + w]

                def rhu(dk_, c0=c0, w=w):
                    return wus[:, IH * dk_ + c0:IH * dk_ + c0 + w]
                mm_tiled(p, S, pbg[:, 0:w], ltx, rhg, 8)
                mm_tiled(p, S, pbu[:, 0:w], ltx, rhu, 8)

                def f(sc, pbg=pbg, c0=c0, w=w):
                    return sc.activation(gb[:, c0:c0 + w], pbg[:, 0:w],
                                         AF.Silu, bias=0.0, scale=1.0)
                S.op("scalar", f, [gb[:, c0:c0 + w]], [pbg])

                def f(v, pbu=pbu, c0=c0, w=w):
                    return v.tensor_copy(ub[:, c0:c0 + w], pbu[:, 0:w])
                S.op("vector", f, [ub[:, c0:c0 + w]], [pbu])

                def f(v, c0=c0, w=w):
                    return v.tensor_tensor(hf[:, c0:c0 + w], gb[:, c0:c0 + w],
                                           ub[:, c0:c0 + w], OP.mult)
                S.op("vector", f, [hf[:, c0:c0 + w]],
                     [gb[:, c0:c0 + w], ub[:, c0:c0 + w]])
                c0 += w
            fp4_quant_tiles(p, S, AFa, ABa, hf, hq, 128, IH, "hq")
            for j in range(11):
                transpose_tile(S, psT[j % 2], ident,
                               hqT[:, 128 * j:128 * (j + 1)],
                               hq[:, 128 * j:128 * (j + 1)],
                               evac_eng=("scalar", "vector")[j % 2])
            for ncc in range(2):
                pb = ps[4 + ncc]

                def lth(dk_):
                    return hqT[:, 128 * dk_:128 * (dk_ + 1)]

                def rhd(dk_, ncc=ncc):
                    return wds[:, 1024 * dk_ + 512 * ncc:
                               1024 * dk_ + 512 * (ncc + 1)]
                mm_tiled(p, S, pb[:, :], lth, rhd, 11)
                xs = xres(i)[:, 512 * ncc:512 * (ncc + 1)]

                def f(v, pb=pb, xs=xs):
                    return v.scalar_tensor_tensor(xs, pb[:, :], 1.0, xs,
                                                  OP.mult, OP.add)
                S.op("vector", f, [xs], [pb, xs])

    for off, n in [(wg_off, 8 * IH), (wu_off, 8 * IH), (wd_off, 11 * 1024),
                   (xq_off, 1024), (xqT_off, 8 * 128), (g_off, IH),
                   (u_off, IH), (hq_off, IH), (hqT_off, 11 * 128),
                   (scr_off, 1024)]:
        ABa.free(off, n)
    AFa.free(lnf_off, 1024)
    AFa.free(h_off, IH)


def build_launch5():
    p = Prog("l5")
    x4i = p.inp("x4", [TPC, D])
    opr = p.inp("oparts", [4 * TPC, 128])
    gti = p.inp("gate", [TPC, 128], BF16)
    dwo = p.inp("dWoT", [128, D])
    dbo = p.inp("dbo", [D])
    wgT = p.inp("wgqT", [D, 2816], BF16)
    wuT = p.inp("wuqT", [D, 2816], BF16)
    wdT = p.inp("wdqT", [2816, D], BF16)
    wif = p.inp("WiT_f", [D, 4384])
    wib = p.inp("WiT_b", [D, 4384])

    x6o = p.out("x6", [TPC, D])
    zxf = p.out("zx_f", [2336, TPC], BF16)
    zxb = p.out("zx_b", [2336, TPC], BF16)
    ztf = p.out("zt_f", [TPC, 2048], BF16)
    ztb = p.out("zt_b", [TPC, 2048], BF16)

    S, AFt, ABt, AFa, ABa, ps, psT, ident, ones = prog_start(p, 25500, 53000)

    xres_off = AFa.alloc(NT * 1024)
    xres = lambda i: AFt[:, xres_off + 1024 * i: xres_off + 1024 * (i + 1)]
    st_off = AFa.alloc(64)
    stats = AFt[:, st_off:st_off + 64]
    for i in range(NT):
        S.dma("sync", xres(i), x4i[128 * i:128 * (i + 1), :])

    wo_off = ABa.alloc(1024)
    wos = ABt[:, wo_off:wo_off + 1024]
    S.dma("gpsimd", wos, dwo[:, :])
    bo_off = ABa.alloc(1024)
    bos = ABt[:, bo_off:bo_off + 1024]
    S.dma("gpsimd", bos[0:1, :], dbo.unsqueeze(0))
    gt_off = ABa.alloc(128)
    gts = ABt[:, gt_off:gt_off + 128]
    oacc_off = AFa.alloc(2 * 128)
    og_off = ABa.alloc(128)
    ogs = ABt[:, og_off:og_off + 128]
    for i in range(NT):
        oa = AFt[:, oacc_off:oacc_off + 128]
        ot = AFt[:, oacc_off + 128:oacc_off + 256]
        for j in range(4):
            src = opr[TPC * j + 128 * i:TPC * j + 128 * (i + 1), :]
            dst = oa if j == 0 else ot
            S.dma("sync", dst, src)
            if j > 0:
                def f(v, oa=oa, ot=ot):
                    return v.tensor_tensor(oa, oa, ot, OP.add)
                S.op("vector", f, [oa], [oa, ot])
        S.dma("sync", gts, gti[128 * i:128 * (i + 1), :])

        def f(v, oa=oa):
            return v.tensor_tensor(ogs, oa, gts, OP.mult)
        S.op("vector", f, [ogs], [oa, gts])
        for ncc in range(2):
            pb = ps[ncc % 2]

            def lt(dk_):
                return ogs[:, 0:128]
            def f(t, pb=pb, ncc=ncc):
                return t.matmul(pb[:, :], ogs[:, 0:128],
                                wos[:, 512 * ncc:512 * (ncc + 1)],
                                start=True, stop=False)
            S.op("tensor", f, [pb[:, :]], [ogs, wos], inc=False)

            def f(t, pb=pb, ncc=ncc):
                return t.matmul(pb[:, :], ones[0:1, 0:128],
                                bos[0:1, 512 * ncc:512 * (ncc + 1)],
                                start=False, stop=True)
            S.op("tensor", f, [pb[:, :]], [ones, bos])
            xs = xres(i)[:, 512 * ncc:512 * (ncc + 1)]

            def f(v, pb=pb, xs=xs):
                return v.scalar_tensor_tensor(xs, pb[:, :], 1.0, xs,
                                              OP.mult, OP.add)
            S.op("vector", f, [xs], [pb, xs])
    ABa.free(wo_off, 1024)
    ABa.free(bo_off, 1024)
    ABa.free(gt_off, 128)
    ABa.free(og_off, 128)
    AFa.free(oacc_off, 2 * 128)

    emit_ffn4(p, S, AFt, ABt, AFa, ABa, ps, psT, ident, ones, xres, stats,
              wgT, wuT, wdT)
    for i in range(NT):
        S.dma("sync", x6o[128 * i:128 * (i + 1), :], xres(i))

    xln_off = ABa.alloc(2 * 1024)
    xlnT_off = ABa.alloc(8 * 2048)
    scr_off = ABa.alloc(1024)
    xlnT = ABt[:, xlnT_off:xlnT_off + 8 * 2048]
    scr = ABt[:, scr_off:scr_off + 1024]
    for i in range(NT):
        xln = ABt[:, xln_off + 1024 * (i % 2): xln_off + 1024 * (i % 2) + 1024]
        ln_one_tile(p, S, xres(i), xln, scr, stats)
        for j in range(8):
            src = xln[:, 128 * j:128 * (j + 1)]
            dst = xlnT[:, 2048 * j + 128 * i: 2048 * j + 128 * (i + 1)]
            transpose_tile(S, psT[(i * 8 + j) % 2], ident, dst, src,
                           evac_eng=("scalar", "vector")[j % 2])

    wi_off = ABa.alloc(8 * 2048)
    wis = ABt[:, wi_off:wi_off + 8 * 2048]
    stg_off = ABa.alloc(2 * 2048)

    for dire, (wdram, zxo, zto) in enumerate([(wif, zxf, ztf),
                                              (wib, zxb, ztb)]):
        for ck in range(4):
            for dk_ in range(8):
                S.dma("gpsimd", wis[:, 2048 * dk_ + 512 * 0:2048 * dk_ + 512],
                      wdram[128 * dk_:128 * (dk_ + 1), 512 * ck:512 * (ck + 1)]) \
                    if False else None
        for dk_ in range(8):
            S.dma("gpsimd", wis[:, 2048 * dk_:2048 * (dk_ + 1)],
                  wdram[128 * dk_:128 * (dk_ + 1), 0:2048])
        for i in range(NT):
            stg = ABt[:, stg_off + 2048 * (i % 2):stg_off + 2048 * (i % 2) + 2048]
            for ncc in range(4):
                pb = ps[ncc % 4]

                def lt(dk_, i=i):
                    return xlnT[:, 2048 * dk_ + 128 * i: 2048 * dk_ + 128 * (i + 1)]

                def rh(dk_, ncc=ncc):
                    return wis[:, 2048 * dk_ + 512 * ncc: 2048 * dk_ + 512 * (ncc + 1)]
                mm_tiled(p, S, pb[:, :], lt, rh, 8)
                evac(S, stg[:, 512 * ncc:512 * (ncc + 1)], pb[:, :],
                     eng=("scalar", "vector")[ncc % 2])
            S.dma("sync", zto[128 * i:128 * (i + 1), :], stg)
        for ct in range(19):
            m = 128 if ct < 18 else 32
            for dk_ in range(8):
                S.dma("gpsimd", wis[:, 2048 * dk_:2048 * dk_ + m],
                      wdram[128 * dk_:128 * (dk_ + 1),
                            2048 + 128 * ct:2048 + 128 * ct + m])
            for tc in range(4):
                pb = ps[tc % 4]

                def lt(dk_, m=m):
                    return wis[:, 2048 * dk_:2048 * dk_ + m]

                def rh(dk_, tc=tc):
                    return xlnT[:, 2048 * dk_ + 512 * tc: 2048 * dk_ + 512 * (tc + 1)]
                mm_tiled(p, S, pb[0:m, :], lt, rh, 8)
                stg = ABt[:, stg_off + 512 * (tc % 2):stg_off + 512 * (tc % 2) + 512]
                evac(S, stg[0:m, 0:512], pb[0:m, :],
                     eng=("scalar", "vector")[tc % 2])
                S.dma("sync", zxo[128 * ct:128 * ct + m,
                                  512 * tc:512 * (tc + 1)], stg[0:m, 0:512])

    prog_end(p, S)
    return p


def build_launch6a():
    p = Prog("l6a")
    xi = p.inp("xbc", [1152, T], BF16)
    cw = p.inp("convw", [1152, 4])
    cb = p.inp("convb", [1152, 1])
    yo = p.out("conv", [1152, T], BF16)

    S, AFt, ABt, AFa, ABa, ps, psT, ident, ones = prog_start(p, 4000, 40000)
    x_off = ABa.alloc(T + 8)
    acc_off = AFa.alloc(2048)
    cw_off = AFa.alloc(8)
    cws = AFt[:, cw_off:cw_off + 4]
    cbs = AFt[:, cw_off + 4:cw_off + 5]
    stg_off = ABa.alloc(2 * 2048)

    for ct in range(9):
        xs = ABt[:, x_off:x_off + T + 8]

        def f(g):
            return g.memset(xs[:, 0:8], 0.0)
        S.op("gpsimd", f, [xs[:, 0:8]], [])
        for j in range(4):
            S.dma("sync", xs[:, 8 + 2048 * j:8 + 2048 * (j + 1)],
                  xi[128 * ct:128 * (ct + 1), 2048 * j:2048 * (j + 1)])
        S.dma("gpsimd", cws, cw[128 * ct:128 * (ct + 1), :])
        S.dma("gpsimd", cbs, cb[128 * ct:128 * (ct + 1), :])
        for tc in range(4):
            a = AFt[:, acc_off:acc_off + 2048]
            base = 8 + 2048 * tc - 3

            def f(v, a=a, base=base):
                return v.tensor_scalar(a, xs[:, base:base + 2048],
                                       cws[:, 0:1], None, OP.mult)
            S.op("vector", f, [a], [xs, cws])
            for j in range(1, 4):
                def f(v, a=a, base=base, j=j):
                    return v.scalar_tensor_tensor(
                        a, xs[:, base + j:base + j + 2048], cws[:, j:j + 1],
                        a, OP.mult, OP.add)
                S.op("vector", f, [a], [xs, cws, a])
            stg = ABt[:, stg_off + 2048 * (tc % 2):stg_off + 2048 * (tc % 2) + 2048]

            def f(sc, a=a, stg=stg):
                return sc.activation(stg, a, AF.Silu, bias=cbs[:, 0:1],
                                     scale=1.0)
            S.op("scalar", f, [stg], [a, cbs])
            S.dma("sync", yo[128 * ct:128 * (ct + 1),
                             2048 * tc:2048 * (tc + 1)], stg)
    prog_end(p, S)
    return p


def build_launch6b():
    p = Prog("l6b")
    NCH = 64
    Xt = p.inp("Xtm", [T, 1024], BF16)
    Bf = p.inp("Bfm", [128, T], BF16)
    Bt = p.inp("Btm", [T, 128], BF16)
    Cf = p.inp("Cfm", [128, T], BF16)
    dtr = p.inp("dtraw", [16, T], BF16)
    hpar = p.inp("hpar", [16, 4])
    DpE = p.inp("DpE", [1, 1024])
    Yo = p.out("Ytm", [T, 1024], BF16)

    S, AFt, ABt, AFa, ABa, ps, psT, ident, ones = prog_start(p, 24500, 48500)
    nc = p.nc
    lfd = nc.dram_tensor("lflat", [NCH, 2048], F32).ap()

    Bf_off = ABa.alloc(T)
    Bfs = ABt[:, Bf_off:Bf_off + T]
    Bt_off = ABa.alloc(T)
    Bts = ABt[:, Bt_off:Bt_off + T]
    Cf_off = ABa.alloc(T)
    Cfs = ABt[:, Cf_off:Cf_off + T]
    dt_off = ABa.alloc(T)
    dts = ABt[0:16, dt_off:dt_off + T]
    for j in range(4):
        S.dma("sync", Bfs[:, 2048 * j:2048 * (j + 1)], Bf[:, 2048 * j:2048 * (j + 1)])
        S.dma("sync", Cfs[:, 2048 * j:2048 * (j + 1)], Cf[:, 2048 * j:2048 * (j + 1)])
        S.dma("sync",
              Bts[:, 2048 * j:2048 * (j + 1)].rearrange("p (a b) -> p a b", b=128),
              Bt[2048 * j:2048 * (j + 1), :].rearrange("(a p) b -> p a b", p=128))
    dtr_off = ABa.alloc(T)
    dtrs = ABt[0:16, dtr_off:dtr_off + T]
    S.dma("sync", dtrs, dtr[:, :])
    hp_off = AFa.alloc(64)
    hps = AFt[0:16, hp_off:hp_off + 4]
    S.dma("sync", hps, hpar[:, :])
    dpe_off = AFa.alloc(1024)
    dpes = AFt[0:1, dpe_off:dpe_off + 1024]
    S.dma("sync", dpes, DpE[:, :])

    pc_off = AFa.alloc(8)
    dtb = AFt[0:16, pc_off:pc_off + 1]
    negA = AFt[0:16, pc_off + 1:pc_off + 2]

    def t16(dst, src, n_in, n_out, base=0):
        def f(t):
            return t.transpose(ps[5][0:n_out, 0:n_in], src,
                               p.identF[base:base + n_in, base:base + n_in])
        S.op("tensor", f, [ps[5][0:n_out, 0:n_in]], [src, p.identF[:, :]])

        def f(v):
            return v.tensor_copy(dst, ps[5][0:n_out, 0:n_in])
        S.op("vector", f, [dst], [ps[5][0:n_out, 0:n_in]])

    def f(v):
        return v.tensor_copy(dtb, hps[0:16, 0:1])
    S.op("vector", f, [dtb], [hps])

    def f(sc):
        return sc.activation(negA, hps[0:16, 1:2], AF.Exp, bias=0.0, scale=1.0)
    S.op("scalar", f, [negA], [hps])

    def f(v):
        return v.tensor_scalar(negA, negA, -1.0, None, OP.mult)
    S.op("vector", f, [negA], [negA])

    la_off = AFa.alloc(T + 4096)
    lall = AFt[0:16, la_off:la_off + T]
    lnd = AFt[32:48, la_off:la_off + T]
    wk1 = AFt[0:16, la_off + T:la_off + T + 2048]
    wk2 = AFt[0:16, la_off + T + 2048:la_off + T + 4096]
    for j in range(4):
        sl = slice(2048 * j, 2048 * (j + 1))

        def f(v, sl=sl):
            return v.tensor_scalar(wk1[:, 0:2048], dtrs[:, sl], dtb[:, 0:1],
                                   None, OP.add)
        S.op("vector", f, [wk1], [dtrs, dtb])

        def f(v, sl=sl):
            return v.scalar_tensor_tensor(lall[:, sl], wk1[:, 0:2048], -1.0,
                                          wk1[:, 0:2048], OP.mult, OP.max)
        S.op("vector", f, [lall[:, sl]], [wk1])

        def f(sc, sl=sl):
            return sc.activation(lall[:, sl], lall[:, sl], AF.Exp, bias=0.0,
                                 scale=-1.0)
        S.op("scalar", f, [lall[:, sl]], [lall[:, sl]])

        def f(sc, sl=sl):
            return sc.activation(lall[:, sl], lall[:, sl], AF.Ln, bias=1.0,
                                 scale=1.0)
        S.op("scalar", f, [lall[:, sl]], [lall[:, sl]])

        def f(v, sl=sl):
            return v.scalar_tensor_tensor(wk2[:, 0:2048], wk1[:, 0:2048], 0.0,
                                          lall[:, sl], OP.max, OP.add)
        S.op("vector", f, [wk2], [wk1, lall[:, sl]])
        def f(v, sl=sl):
            return v.tensor_copy(dts[:, sl], wk2[:, 0:2048])
        S.op("vector", f, [dts[:, sl]], [wk2])

        def f(sc, sl=sl):
            return sc.activation(lnd[:, sl], wk2[:, 0:2048], AF.Ln, bias=0.0,
                                 scale=1.0)
        S.op("scalar", f, [lnd[:, sl]], [wk2])

        def f(v, sl=sl):
            return v.tensor_scalar(lall[:, sl], wk2[:, 0:2048], negA[:, 0:1],
                                   None, OP.mult)
        S.op("vector", f, [lall[:, sl]], [wk2, negA])

    for ch in range(NCH):
        sl = slice(128 * ch, 128 * (ch + 1))

        def f(v, sl=sl):
            return v.tensor_tensor_scan(lall[:, sl], lall[:, sl], lall[:, sl],
                                        0.0, OP.add, OP.bypass)
        S.op("vector", f, [lall[:, sl]], [lall[:, sl]])

    ABa.free(dtr_off, T)
    S.dma("sync",
          lfd.rearrange("c (h t) -> h c t", h=16),
          lall.rearrange("p (c t) -> p c t", t=128))

    mu_off = ABa.alloc(128)
    p.maskU = ABt[:, mu_off:mu_off + 128]
    io_off = AFa.alloc(132)
    io1 = AFt[:, io_off:io_off + 128]
    io2 = AFt[:, io_off + 128:io_off + 129]

    def f(g):
        return g.iota(io1, [[1, 128]], channel_multiplier=0,
                      allow_small_or_imprecise_dtypes=True)
    S.op("gpsimd", f, [io1], [])

    def f(g):
        return g.iota(io2, [[0, 1]], channel_multiplier=1,
                      allow_small_or_imprecise_dtypes=True)
    S.op("gpsimd", f, [io2], [])

    def f(v):
        return v.scalar_tensor_tensor(p.maskU, io1, 0.0,
                                      io2.broadcast_to((128, 128)),
                                      OP.add, OP.is_ge)
    S.op("vector", f, [p.maskU], [io1, io2])

    oF_off = AFa.alloc(128)
    onesF = AFt[:, oF_off:oF_off + 128]

    def f(g):
        return g.memset(onesF[0:1, :], 1.0)
    S.op("gpsimd", f, [onesF[0:1, :]], [])
    dpx_off = AFa.alloc(1024)
    dpx = AFt[:, dpx_off:dpx_off + 1024]
    for half in range(2):
        def f(t, half=half):
            return t.matmul(ps[0][:, 0:512], onesF[0:1, 0:128],
                            dpes[0:1, 512 * half:512 * (half + 1)],
                            start=True, stop=True)
        S.op("tensor", f, [ps[0][:, 0:512]], [onesF, dpes])

        def f(v, half=half):
            return v.tensor_copy(dpx[:, 512 * half:512 * (half + 1)],
                                 ps[0][:, 0:512])
        S.op("vector", f, [dpx[:, 512 * half:512 * (half + 1)]], [ps[0]])

    st_off = AFa.alloc(1024)
    stf = AFt[:, st_off:st_off + 1024]

    def f(g):
        return g.memset(stf, 0.0)
    S.op("gpsimd", f, [stf], [])
    stb_off = ABa.alloc(1024)
    stb = ABt[:, stb_off:stb_off + 1024]

    def f(v):
        return v.tensor_copy(stb, stf)
    S.op("vector", f, [stb], [stf])

    X_off = ABa.alloc(2 * 1024)
    lf_off = AFa.alloc(2 * 2048)
    att_off = ABa.alloc(128)
    atts = ABt[:, att_off:att_off + 128]
    Dall_off = ABa.alloc(2048)
    Dall = ABt[:, Dall_off:Dall_off + 2048]
    ltc_off = AFa.alloc(16 + 16 + 16 + 16)
    lTc = AFt[:, ltc_off:ltc_off + 16]
    lbTc = AFt[:, ltc_off + 16:ltc_off + 32]
    e1T = AFt[:, ltc_off + 32:ltc_off + 48]
    e2T = AFt[:, ltc_off + 48:ltc_off + 64]
    lCr_off = AFa.alloc(20)
    lCc = AFt[0:16, lCr_off:lCr_off + 1]
    lCr = AFt[0:1, lCr_off + 2:lCr_off + 18]
    ecf_off = AFa.alloc(1044)
    ecf = AFt[0:1, ecf_off:ecf_off + 1024]
    ecsrc = AFt[0:1, ecf_off + 1024:ecf_off + 1040]
    E_off = AFa.alloc(2048)
    Ef = AFt[:, E_off:E_off + 2048]
    t1_off = ABa.alloc(3 * 1024)
    t1b = ABt[:, t1_off:t1_off + 1024]
    t3b = ABt[:, t1_off + 1024:t1_off + 2048]
    Xpp = ABt[:, t1_off + 2048:t1_off + 3072]
    ecx_off = AFa.alloc(1024)
    ecx = AFt[:, ecx_off:ecx_off + 1024]
    ystg_off = ABa.alloc(2 * 1024)

    for ch in range(NCH):
        sl = slice(128 * ch, 128 * (ch + 1))
        Xc = ABt[:, X_off + 1024 * (ch % 2):X_off + 1024 * (ch % 2) + 1024]
        S.dma("scalar", Xc.rearrange("p b -> p b"),
              Xt[sl, :])
        lfr = AFt[0:1, lf_off + 2048 * (ch % 2):lf_off + 2048 * (ch % 2) + 2048]
        S.dma("sync", lfr, lfd[ch:ch + 1, :])

        def f(t, sl=sl):
            return t.matmul(ps[0][:, 0:128], Bfs[:, sl], Cfs[:, sl],
                            start=True, stop=True)
        S.op("tensor", f, [ps[0][:, 0:128]], [Bfs[:, sl], Cfs[:, sl]])

        def f(v):
            return v.tensor_tensor(atts[:, 0:128], ps[0][:, 0:128],
                                   p.maskU[:, :], OP.mult)
        S.op("vector", f, [atts[:, 0:128]], [ps[0], p.maskU[:, :]])

        t16(lTc, lall[:, sl], 16, 128)
        t16(lbTc, lnd[:, sl], 16, 128, base=32)

        def f(v):
            return v.tensor_tensor(lbTc, lbTc, lTc, OP.subtract)
        S.op("vector", f, [lbTc], [lbTc, lTc])

        for half in range(4):
            def f(t, half=half, lfr=lfr):
                return t.matmul(ps[1 + half % 2][:, 0:512],
                                onesF[0:1, 0:128],
                                lfr[0:1, 512 * half:512 * (half + 1)],
                                start=True, stop=True)
            S.op("tensor", f, [ps[1 + half % 2][:, 0:512]], [onesF, lfr])

            def f(v, half=half):
                return v.tensor_copy(Ef[:, 512 * half:512 * (half + 1)],
                                     ps[1 + half % 2][:, 0:512])
            S.op("vector", f, [Ef[:, 512 * half:512 * (half + 1)]],
                 [ps[1 + half % 2]])

        lbe = lbTc.unsqueeze(2).broadcast_to((128, 16, 128))
        E3 = Ef.rearrange("p (h t) -> p h t", h=16)

        def f(v):
            return v.tensor_tensor(E3, E3, lbe, OP.add)
        S.op("vector", f, [Ef], [Ef, lbTc])

        def f(v):
            return v.tensor_scalar(Ef, Ef, 80.0, None, OP.min)
        S.op("vector", f, [Ef], [Ef])

        def f(sc):
            return sc.activation(Dall, Ef, AF.Exp, bias=0.0, scale=1.0)
        S.op("scalar", f, [Dall], [Ef])

        ae = atts[:, 0:128].unsqueeze(1).broadcast_to((128, 16, 128))
        D3 = Dall.rearrange("p (h t) -> p h t", h=16)

        def f(v):
            return v.tensor_tensor(D3, D3, ae, OP.mult)
        S.op("vector", f, [Dall], [Dall, atts])

        for h in range(16):
            pb = ps[2 + h // 8]

            def f(t, h=h, pb=pb, Xc=Xc):
                return t.matmul(pb[:, 64 * (h % 8):64 * (h % 8) + 64],
                                Dall[:, 128 * h:128 * (h + 1)],
                                Xc[:, 64 * h:64 * (h + 1)],
                                start=True, stop=True)
            S.op("tensor", f, [pb[:, 64 * (h % 8):64 * (h % 8) + 64]],
                 [Dall, Xc], inc=(h % 8 == 7))

        for half in range(2):
            pb = ps[4 + half]

            def f(t, half=half, pb=pb, sl=sl):
                return t.matmul(pb[:, :], Cfs[:, sl],
                                stb[:, 512 * half:512 * (half + 1)],
                                start=True, stop=True)
            S.op("tensor", f, [pb[:, :]], [Cfs[:, sl], stb])

        def f(sc):
            return sc.activation(e2T, lTc, AF.Exp, bias=0.0, scale=1.0)
        S.op("scalar", f, [e2T], [lTc])

        ystg = ABt[:, ystg_off + 1024 * (ch % 2):ystg_off + 1024 * (ch % 2) + 1024]
        e2e = e2T.unsqueeze(2).broadcast_to((128, 16, 64))
        t13 = t1b.rearrange("p (h q) -> p h q", h=16)

        Xc3 = Xc.rearrange("p (h q) -> p h q", h=16)

        def f(v, Xc3=Xc3):
            return v.tensor_tensor(t13, Xc3,
                                   dpx.rearrange("p (h q) -> p h q", h=16),
                                   OP.mult)
        S.op("vector", f, [t1b], [Xc, dpx])
        for half in range(2):
            hsl = slice(512 * half, 512 * (half + 1))

            def f(v, half=half, hsl=hsl):
                return v.scalar_tensor_tensor(
                    t3b[:, hsl], ps[2 + half][:, :], 1.0, t1b[:, hsl],
                    OP.mult, OP.add)
            S.op("vector", f, [t3b[:, hsl]], [ps[2 + half], t1b])
        for half in range(2):
            hsl = slice(512 * half, 512 * (half + 1))
            y3h = ystg[:, hsl].rearrange("p (h q) -> p h q", h=8)
            e2h = e2T[:, 8 * half:8 * (half + 1)].unsqueeze(2).broadcast_to(
                (128, 8, 64))

            def f(v, half=half, y3h=y3h, e2h=e2h):
                return v.tensor_tensor(y3h, ps[4 + half][:, :].rearrange(
                    "p (h q) -> p h q", h=8), e2h, OP.mult)
            S.op("vector", f, [ystg[:, hsl]], [ps[4 + half], e2T])

            def f(v, hsl=hsl, ystg=ystg):
                return v.tensor_tensor(ystg[:, hsl], ystg[:, hsl],
                                       t3b[:, hsl], OP.add)
            S.op("vector", f, [ystg[:, hsl]], [ystg[:, hsl], t3b[:, hsl]])
        S.dma("sync", Yo[sl, :], ystg)

        def f(v, ch=ch):
            return v.tensor_copy(lCc, lall[0:16, 128 * ch + 127:128 * (ch + 1)])
        S.op("vector", f, [lCc], [lall])
        t16(lCr, lCc, 16, 1)

        def f(t):
            return t.matmul(ps[0][:, 0:16], onesF[0:1, 0:128], lCr,
                            start=True, stop=True)
        S.op("tensor", f, [ps[0][:, 0:16]], [onesF, lCr])

        def f(v):
            return v.scalar_tensor_tensor(e1T, lbTc, 1.0, ps[0][:, 0:16],
                                          OP.mult, OP.add)
        S.op("vector", f, [e1T], [lbTc, ps[0]])

        def f(sc):
            return sc.activation(e1T, e1T, AF.Exp, bias=0.0, scale=1.0)
        S.op("scalar", f, [e1T], [e1T])

        e1e = e1T.unsqueeze(2).broadcast_to((128, 16, 64))
        X3 = Xpp.rearrange("p (h q) -> p h q", h=16)

        def f(v, Xc3=Xc3):
            return v.tensor_tensor(X3, Xc3, e1e, OP.mult)
        S.op("vector", f, [Xpp], [Xc, e1T])

        for half in range(2):
            pb = ps[half]

            def f(t, half=half, pb=pb, sl=sl):
                return t.matmul(pb[:, :], Bts[:, sl],
                                Xpp[:, 512 * half:512 * (half + 1)],
                                start=True, stop=True)
            S.op("tensor", f, [pb[:, :]], [Bts[:, sl], Xpp])

        def f(v):
            return v.tensor_copy(ecsrc[0:1, 0:16], lCr)
        S.op("vector", f, [ecsrc[0:1, 0:16]], [lCr])
        ecs = ecsrc[0:1, 0:16].unsqueeze(2).broadcast_to((1, 16, 64))

        def f(v):
            return v.tensor_copy(
                ecf[0:1, :].rearrange("p (h q) -> p h q", h=16), ecs)
        S.op("vector", f, [ecf], [ecsrc])

        def f(sc):
            return sc.activation(ecf[0:1, :], ecf[0:1, :], AF.Exp, bias=0.0,
                                 scale=1.0)
        S.op("scalar", f, [ecf], [ecf])
        for half in range(2):
            def f(t, half=half):
                return t.matmul(ps[6][:, 0:512] if False else
                                ps[2 + half][:, :], onesF[0:1, 0:128],
                                ecf[0:1, 512 * half:512 * (half + 1)],
                                start=True, stop=True)
            S.op("tensor", f, [ps[2 + half][:, :]], [onesF, ecf])

            def f(v, half=half):
                return v.tensor_copy(ecx[:, 512 * half:512 * (half + 1)],
                                     ps[2 + half][:, :])
            S.op("vector", f, [ecx[:, 512 * half:512 * (half + 1)]],
                 [ps[2 + half]])
        for half in range(2):
            hsl = slice(512 * half, 512 * (half + 1))

            def f(v, hsl=hsl):
                return v.tensor_tensor(stf[:, hsl], stf[:, hsl], ecx[:, hsl],
                                       OP.mult)
            S.op("vector", f, [stf[:, hsl]], [stf[:, hsl], ecx[:, hsl]])

            def f(v, half=half, hsl=hsl):
                return v.scalar_tensor_tensor(stf[:, hsl], ps[half][:, :],
                                              1.0, stf[:, hsl], OP.mult,
                                              OP.add)
            S.op("vector", f, [stf[:, hsl]], [ps[half], stf[:, hsl]])

        def f(v):
            return v.tensor_copy(stb, stf)
        S.op("vector", f, [stb], [stf])

    prog_end(p, S)
    return p


def build_launch7():
    p = Prog("l7")
    Yf = p.inp("Yf", [TPC, 2048], BF16)
    Yb = p.inp("Yb", [TPC, 2048], BF16)
    zf = p.inp("zf", [TPC, 2048], BF16)
    zb = p.inp("zb", [TPC, 2048], BF16)
    nw = p.inp("nw", [2, 2048])
    of = p.out("YNf", [TPC, 2048], BF16)
    ob = p.out("YNb", [TPC, 2048], BF16)

    S, AFt, ABt, AFa, ABa, ps, psT, ident, ones = prog_start(p, 16000, 30000)
    oF_off = AFa.alloc(128)
    onesF = AFt[:, oF_off:oF_off + 128]

    def f(g):
        return g.memset(onesF[0:1, :], 1.0)
    S.op("gpsimd", f, [onesF[0:1, :]], [])
    nw_off = AFa.alloc(4 * 2048)
    nrow = [AFt[0:1, nw_off:nw_off + 2048],
            AFt[0:1, nw_off + 2048:nw_off + 2 * 2048]]
    S.dma("sync", nrow[0], nw[0:1, :])
    S.dma("sync", nrow[1], nw[1:2, :])
    nwE = [AFt[:, nw_off + 2 * 2048:nw_off + 3 * 2048],
           AFt[:, nw_off + 3 * 2048:nw_off + 4 * 2048]]
    for d in range(2):
        for hf in range(4):
            def f(t, d=d, hf=hf):
                return t.matmul(ps[0][:, 0:512], onesF[0:1, 0:128],
                                nrow[d][0:1, 512 * hf:512 * (hf + 1)],
                                start=True, stop=True)
            S.op("tensor", f, [ps[0][:, 0:512]], [onesF, nrow[d]])

            def f(v, d=d, hf=hf):
                return v.tensor_copy(nwE[d][:, 512 * hf:512 * (hf + 1)],
                                     ps[0][:, 0:512])
            S.op("vector", f, [nwE[d][:, 512 * hf:512 * (hf + 1)]], [ps[0]])

    yt_off = ABa.alloc(2 * 2048)
    zt_off = ABa.alloc(2 * 2048)
    sz_off = ABa.alloc(2048)
    szs = ABt[:, sz_off:sz_off + 2048]
    tf_off = AFa.alloc(2048)
    tfs = AFt[:, tf_off:tf_off + 2048]
    st_off = AFa.alloc(8)
    stats = AFt[:, st_off:st_off + 8]
    stg_off = ABa.alloc(2 * 2048)
    scr_off = ABa.alloc(2048)
    scrb = ABt[:, scr_off:scr_off + 2048]

    for d, (Yi, zi, oo) in enumerate([(Yf, zf, of), (Yb, zb, ob)]):
        for i in range(NT):
            yts = ABt[:, yt_off + 2048 * (i % 2):yt_off + 2048 * (i % 2) + 2048]
            zts = ABt[:, zt_off + 2048 * (i % 2):zt_off + 2048 * (i % 2) + 2048]
            S.dma("sync", yts, Yi[128 * i:128 * (i + 1), :])
            S.dma("scalar", zts, zi[128 * i:128 * (i + 1), :])

            def f(sc, zts=zts):
                return sc.activation(szs, zts, AF.Silu, bias=0.0, scale=1.0)
            S.op("scalar", f, [szs], [zts])

            def f(v, yts=yts):
                return v.tensor_tensor(tfs, yts, szs, OP.mult)
            S.op("vector", f, [tfs], [yts, szs])

            def f(v):
                return v.scalar_tensor_tensor(scrb, tfs, 1.0, tfs, OP.mult,
                                              OP.mult, accum_out=stats[:, 0:1])
            S.op("vector", f, [scrb, stats[:, 0:1]], [tfs])

            def f(v):
                return v.tensor_scalar(stats[:, 1:2], stats[:, 0:1],
                                       1.0 / 2048.0, 1e-5, OP.mult, OP.add)
            S.op("vector", f, [stats[:, 1:2]], [stats[:, 0:1]])

            def f(sc):
                return sc.activation(stats[:, 2:3], stats[:, 1:2], AF.Ln,
                                     bias=0.0, scale=1.0)
            S.op("scalar", f, [stats[:, 2:3]], [stats[:, 1:2]])

            def f(sc):
                return sc.activation(stats[:, 3:4], stats[:, 2:3], AF.Exp,
                                     bias=0.0, scale=-0.5)
            S.op("scalar", f, [stats[:, 3:4]], [stats[:, 2:3]])

            stg = ABt[:, stg_off + 2048 * (i % 2):stg_off + 2048 * (i % 2) + 2048]

            def f(v, d=d):
                return v.tensor_scalar(tfs, tfs, stats[:, 3:4], None, OP.mult)
            S.op("vector", f, [tfs], [tfs, stats])

            def f(v, d=d, stg=stg):
                return v.tensor_tensor(stg, tfs, nwE[d], OP.mult)
            S.op("vector", f, [stg], [tfs, nwE[d]])
            S.dma("sync", oo[128 * i:128 * (i + 1), :], stg)

    prog_end(p, S)
    return p


def build_launch8():
    p = Prog("l8")
    x6i = p.inp("x6", [TPC, D])
    ynf = p.inp("YNfT", [2048, TPC], BF16)
    ynb = p.inp("YNbT", [2048, TPC], BF16)
    wof = p.inp("WoT_f", [2048, D])
    wob = p.inp("WoT_b", [2048, D])
    wgT = p.inp("wgqT", [D, 2816], BF16)
    wuT = p.inp("wuqT", [D, 2816], BF16)
    wdT = p.inp("wdqT", [2816, D], BF16)
    x8o = p.out("x8", [TPC, D])

    S, AFt, ABt, AFa, ABa, ps, psT, ident, ones = prog_start(p, 25500, 53000)
    xres_off = AFa.alloc(NT * 1024)
    xres = lambda i: AFt[:, xres_off + 1024 * i: xres_off + 1024 * (i + 1)]
    st_off = AFa.alloc(64)
    stats = AFt[:, st_off:st_off + 64]
    for i in range(NT):
        S.dma("sync", xres(i), x6i[128 * i:128 * (i + 1), :])

    yn_off = ABa.alloc(16 * 2048)
    yns = ABt[:, yn_off:yn_off + 16 * 2048]
    wo_off = ABa.alloc(16 * 1024)
    wos = ABt[:, wo_off:wo_off + 16 * 1024]
    for dk_ in range(16):
        S.dma("gpsimd", yns[:, 2048 * dk_:2048 * (dk_ + 1)],
              ynf[128 * dk_:128 * (dk_ + 1), :])
        S.dma("gpsimd", wos[:, 1024 * dk_:1024 * (dk_ + 1)],
              wof[128 * dk_:128 * (dk_ + 1), :])
    for half in range(2):
        if half == 1:
            for dk_ in range(16):
                S.dma("gpsimd", yns[:, 2048 * dk_:2048 * (dk_ + 1)],
                      ynb[128 * dk_:128 * (dk_ + 1), :])
                S.dma("gpsimd", wos[:, 1024 * dk_:1024 * (dk_ + 1)],
                      wob[128 * dk_:128 * (dk_ + 1), :])
        for i in range(NT):
            for ncc in range(2):
                pb = ps[(i * 2 + ncc) % 4]

                def lt(dk_, i=i):
                    return yns[:, 2048 * dk_ + 128 * i:2048 * dk_ + 128 * (i + 1)]

                def rh(dk_, ncc=ncc):
                    return wos[:, 1024 * dk_ + 512 * ncc:1024 * dk_ + 512 * (ncc + 1)]
                mm_tiled(p, S, pb[:, :], lt, rh, 16)
                xs = xres(i)[:, 512 * ncc:512 * (ncc + 1)]

                def f(v, pb=pb, xs=xs):
                    return v.scalar_tensor_tensor(xs, pb[:, :], 0.5, xs,
                                                  OP.mult, OP.add)
                S.op("vector", f, [xs], [pb, xs])
    ABa.free(yn_off, 16 * 2048)
    ABa.free(wo_off, 16 * 1024)

    emit_ffn4(p, S, AFt, ABt, AFa, ABa, ps, psT, ident, ones, xres, stats,
              wgT, wuT, wdT)
    for i in range(NT):
        S.dma("sync", x8o[128 * i:128 * (i + 1), :], xres(i))
    prog_end(p, S)
    return p


_PROGS = {}


def _prog(name):
    if name not in _PROGS:
        _PROGS[name] = {
            "l1a": build_launch1a, "l1b": build_launch1b, "l2": build_launch2,
            "l3": build_launch3, "l4": build_launch4, "l5": build_launch5,
            "l6a": build_launch6a, "l6b": build_launch6b, "l7": build_launch7,
            "l8": build_launch8,
        }[name]()
    return _PROGS[name]


def _run(name, in_maps):
    p = _prog(name)
    return run_bass_kernel_spmd(p.nc, in_maps, core_ids=list(range(8)))


CT = np.ascontiguousarray


def kernel(**inputs):
    inp = inputs
    x = np.asarray(inp["L0_attn_a_x"])

    aWqT = CT(inp["L0_attn_a_Wq"].T)
    aWkT = CT(inp["L0_attn_a_Wk"].T)
    aWvT = CT(inp["L0_attn_a_Wv"].T)
    ins = []
    for c in range(8):
        b, qq = c // 4, c % 4
        ins.append({
            "x0": CT(x[b, 2048 * qq:2048 * (qq + 1), :]),
            "aWqT": aWqT, "aWkT": aWkT, "aWvT": aWvT,
            "aproj": CT(inp["L0_attn_a_proj"]),
            "wg8": CT(inp["L1_ffn_w_gate"][352 * c:352 * (c + 1), :]),
            "wu8": CT(inp["L1_ffn_w_up"][352 * c:352 * (c + 1), :]),
            "wd8": CT(inp["L1_ffn_w_down"][128 * c:128 * (c + 1), :]),
        })
    ra = _run("l1a", ins)
    wgq = np.concatenate([np.asarray(ra.results[c]["wgq"]) for c in range(8)], 0)
    wuq = np.concatenate([np.asarray(ra.results[c]["wuq"]) for c in range(8)], 0)
    wdq = np.concatenate([np.asarray(ra.results[c]["wdq"]) for c in range(8)], 0)
    wgqT = CT(wgq.T)
    wuqT = CT(wuq.T)
    wdqT = CT(wdq.T)

    fWT = CT(inp["L0_ffn_W"].T)
    fCT_ = CT(inp["L0_ffn_C"].T)
    ins = []
    for c in range(8):
        b, qq = c // 4, c % 4
        kvparts = np.concatenate([np.asarray(ra.results[b * 4 + j]["kvpart"])
                                  for j in range(4)], 0)
        ins.append({
            "x0": CT(x[b, 2048 * qq:2048 * (qq + 1), :]),
            "qp": CT(np.asarray(ra.results[c]["qp"])),
            "kvparts": CT(kvparts),
            "fWT": fWT, "fb": CT(inp["L0_ffn_b"]),
            "fVr": CT(inp["L0_ffn_V_r"]), "fCT": fCT_,
            "bWattn": CT(inp["L0_attn_b_W_attn"]),
        })
    rb = _run("l1b", ins)
    x2 = [[np.asarray(rb.results[b * 4 + j]["x2"]) for j in range(4)]
          for b in range(2)]
    qkv = [np.concatenate([np.asarray(rb.results[b * 4 + j]["qkv"])
                           for j in range(4)], 0) for b in range(2)]

    ins = []
    for c in range(8):
        b, hg = c // 4, c % 4
        heads = [4 * hg + j for j in range(4)]
        qfm = np.zeros((128, 2 * 8192), dtype=BF)
        kfm = np.zeros((128, 2 * 8192), dtype=BF)
        vtm = np.zeros((8192, 256), dtype=BF)
        qk = qkv[b]
        for j, h in enumerate(heads):
            pb_, cb = 64 * (j % 2), 8192 * (j // 2)
            qfm[pb_:pb_ + 64, cb:cb + 8192] = qk[:, 64 * h:64 * (h + 1)].T
            kfm[pb_:pb_ + 64, cb:cb + 8192] = \
                qk[:, 1024 + 64 * h:1024 + 64 * (h + 1)].T
            vtm[:, 64 * j:64 * (j + 1)] = qk[:, 2048 + 64 * h:2048 + 64 * (h + 1)]
        ins.append({"qfm": CT(qfm), "kfm": CT(kfm), "vtm": CT(vtm)})
    r2 = _run("l2", ins)
    yT = [np.zeros((1024, 8192), dtype=BF) for _ in range(2)]
    for c in range(8):
        b, hg = c // 4, c % 4
        yfm = np.asarray(r2.results[c]["yfm"])
        for j in range(4):
            h = 4 * hg + j
            pb_, cb = 64 * (j % 2), 8192 * (j // 2)
            yT[b][64 * h:64 * (h + 1), :] = yfm[pb_:pb_ + 64, cb:cb + 8192]

    dbias = np.zeros((8, 128), np.float32)
    dbias[0, :] = inp["L1_attn_a_bq"]
    dbias[1, :] = inp["L1_attn_a_bk"]
    dbias[2, :] = inp["L1_attn_a_bv"]
    dbias[3, :] = inp["L1_attn_a_bg"]
    dbias[4, 0] = np.asarray(inp["L1_attn_a_ba"])[0]
    dbias[4, 1] = np.asarray(inp["L1_attn_a_bb"])[0]
    dWqT = CT(inp["L1_attn_a_Wq"].T)
    dWkT = CT(inp["L1_attn_a_Wk"].T)
    dWvT = CT(inp["L1_attn_a_Wv"].T)
    dWgT = CT(inp["L1_attn_a_Wg"].T)
    dWabT = CT(np.concatenate([inp["L1_attn_a_Wa"].T,
                               inp["L1_attn_a_Wb"].T], 1))
    ins = []
    for c in range(8):
        b, qq = c // 4, c % 4
        ins.append({
            "x2": CT(x2[b][qq]),
            "yT": CT(yT[b][:, 2048 * qq:2048 * (qq + 1)]),
            "bWproj": CT(inp["L0_attn_b_W_proj"]),
            "fWT": fWT, "fb": CT(inp["L0_ffn_b"]),
            "fVr": CT(inp["L0_ffn_V_r"]), "fCT": fCT_,
            "dWqT": dWqT, "dWkT": dWkT, "dWvT": dWvT, "dWgT": dWgT,
            "dWabT": dWabT, "dbias": dbias,
        })
    r3 = _run("l3", ins)
    x4 = [[np.asarray(r3.results[b * 4 + j]["x4"]) for j in range(4)]
          for b in range(2)]
    dq = [np.concatenate([np.asarray(r3.results[b * 4 + j]["dq"])
                          for j in range(4)], 0) for b in range(2)]
    dk = [np.concatenate([np.asarray(r3.results[b * 4 + j]["dk"])
                          for j in range(4)], 0) for b in range(2)]
    dv = [np.concatenate([np.asarray(r3.results[b * 4 + j]["dv"])
                          for j in range(4)], 0) for b in range(2)]
    dg = [np.concatenate([np.asarray(r3.results[b * 4 + j]["dgate"])
                          for j in range(4)], 0) for b in range(2)]
    dab = [np.concatenate([np.asarray(r3.results[b * 4 + j]["dab"])
                           for j in range(4)], 0) for b in range(2)]

    ins = []
    for c in range(8):
        b, vb = c // 4, c % 4
        ins.append({
            "kfm": CT(dk[b].T), "ktm": CT(dk[b]),
            "qpk": CT(dq[b].T[32 * vb:32 * (vb + 1), :]),
            "vbl": CT(dv[b][:, 32 * vb:32 * (vb + 1)]),
            "abr": CT(dab[b][:, 0].reshape(64, 128)),
            "bbr": CT(dab[b][:, 1].reshape(64, 128)),
            "bcT": CT(dab[b][:, 1].reshape(64, 128).T),
        })
    r4 = _run("l4", ins)

    ins = []
    for c in range(8):
        b, qq = c // 4, c % 4
        sl = slice(2048 * qq, 2048 * (qq + 1))
        op4 = np.concatenate([np.asarray(r4.results[b * 4 + j]["opart"])[sl]
                              for j in range(4)], 0)
        ins.append({
            "x4": CT(x4[b][qq]),
            "oparts": CT(op4),
            "gate": CT(dg[b][sl]),
            "dWoT": CT(inp["L1_attn_a_Wo"].T),
            "dbo": CT(inp["L1_attn_a_bo"]),
            "wgqT": wgqT, "wuqT": wuqT, "wdqT": wdqT,
            "WiT_f": CT(inp["L1_attn_b_Wi_f"].T),
            "WiT_b": CT(inp["L1_attn_b_Wi_b"].T),
        })
    r5 = _run("l5", ins)
    x6 = [[np.asarray(r5.results[b * 4 + j]["x6"]) for j in range(4)]
          for b in range(2)]
    zxf = [np.concatenate([np.asarray(r5.results[b * 4 + j]["zx_f"])
                           for j in range(4)], 1) for b in range(2)]
    zxb_nat = [np.concatenate([np.asarray(r5.results[b * 4 + j]["zx_b"])
                               for j in range(4)], 1) for b in range(2)]
    zxb = [CT(z[:, ::-1]) for z in zxb_nat]
    ztf = [np.concatenate([np.asarray(r5.results[b * 4 + j]["zt_f"])
                           for j in range(4)], 0) for b in range(2)]
    ztb_nat = [np.concatenate([np.asarray(r5.results[b * 4 + j]["zt_b"])
                               for j in range(4)], 0) for b in range(2)]

    ins = []
    meta = []
    for c in range(8):
        b, dire, hh = c // 4, (c // 2) % 2, c % 2
        zx = (zxf, zxb)[dire][b]
        sfx = "fb"[dire]
        cwf = np.asarray(inp[f"L1_attn_b_conv_w_{sfx}"])[:, 0, :]
        cbf = np.asarray(inp[f"L1_attn_b_conv_b_{sfx}"])[:, None]
        sl = slice(1152 * hh, 1152 * (hh + 1))
        ins.append({"xbc": CT(zx[sl, :]),
                    "convw": CT(cwf[sl]), "convb": CT(cbf[sl])})
        meta.append((b, dire, hh))
    r6a = _run("l6a", ins)
    conv = [[None, None], [None, None]]
    for b in range(2):
        for dire in range(2):
            cs = [np.asarray(r6a.results[b * 4 + dire * 2 + hh]["conv"])
                  for hh in range(2)]
            conv[b][dire] = np.concatenate(cs, 0)

    ins = []
    meta = []
    for c in range(8):
        b, dire, hh = c // 4, (c // 2) % 2, c % 2
        sfx = "fb"[dire]
        cv = conv[b][dire]
        zx = (zxf, zxb)[dire][b]
        Xh = cv[1024 * hh:1024 * (hh + 1)]
        Bm = cv[2048:2176]
        Cm = cv[2176:2304]
        dtraw = zx[2304 + 16 * hh:2304 + 16 * (hh + 1), :]
        hsl = slice(16 * hh, 16 * (hh + 1))
        hpar = np.zeros((16, 4), np.float32)
        hpar[:, 0] = inp[f"L1_attn_b_dt_bias_{sfx}"][hsl]
        hpar[:, 1] = inp[f"L1_attn_b_A_log_{sfx}"][hsl]
        DpE = np.repeat(np.asarray(inp[f"L1_attn_b_D_{sfx}"])[hsl],
                        64)[None, :].astype(np.float32)
        ins.append({
            "Xtm": CT(Xh.T), "Bfm": CT(Bm), "Btm": CT(Bm.T), "Cfm": CT(Cm),
            "dtraw": CT(dtraw), "hpar": hpar, "DpE": CT(DpE),
        })
        meta.append((b, dire, hh))
    r6b = _run("l6b", ins)
    Y = [[None, None], [None, None]]
    for b in range(2):
        for dire in range(2):
            ys = [np.asarray(r6b.results[b * 4 + dire * 2 + hh]["Ytm"])
                  for hh in range(2)]
            Y[b][dire] = np.concatenate(ys, 1)

    nw = np.stack([np.asarray(inp["L1_attn_b_norm_w_f"]),
                   np.asarray(inp["L1_attn_b_norm_w_b"])]).astype(np.float32)
    ins = []
    for c in range(8):
        b, qq = c // 4, c % 4
        sl = slice(2048 * qq, 2048 * (qq + 1))
        ztb_rev = ztb_nat[b][::-1]
        ins.append({
            "Yf": CT(Y[b][0][sl]), "Yb": CT(Y[b][1][sl]),
            "zf": CT(ztf[b][sl]), "zb": CT(ztb_rev[sl]),
            "nw": CT(nw),
        })
    r7 = _run("l7", ins)
    YNf = [np.concatenate([np.asarray(r7.results[b * 4 + j]["YNf"])
                           for j in range(4)], 0) for b in range(2)]
    YNb = [np.concatenate([np.asarray(r7.results[b * 4 + j]["YNb"])
                           for j in range(4)], 0) for b in range(2)]

    ins = []
    for c in range(8):
        b, qq = c // 4, c % 4
        sl = slice(2048 * qq, 2048 * (qq + 1))
        ynb_unrev = YNb[b][::-1]
        ins.append({
            "x6": CT(x6[b][qq]),
            "YNfT": CT(YNf[b].T[:, sl]),
            "YNbT": CT(ynb_unrev.T[:, sl]),
            "WoT_f": CT(inp["L1_attn_b_Wo_f"].T),
            "WoT_b": CT(inp["L1_attn_b_Wo_b"].T),
            "wgqT": wgqT, "wuqT": wuqT, "wdqT": wdqT,
        })
    r8 = _run("l8", ins)
    out = np.zeros((2, 8192, 1024), np.float32)
    for c in range(8):
        b, qq = c // 4, c % 4
        out[b, 2048 * qq:2048 * (qq + 1)] = np.asarray(r8.results[c]["x8"])
    return out


# version 06
# speedup vs baseline: 1.2980x; 1.2980x over previous
import math
import numpy as np
import ml_dtypes
import concourse.bass as bass
import concourse.mybir as mybir
from concourse.bass_utils import run_bass_kernel_spmd

F32 = mybir.dt.float32
BF16 = mybir.dt.bfloat16
U8 = mybir.dt.uint8
AF = mybir.ActivationFunctionType
OP = mybir.AluOpType
AX = mybir.AxisListType.X
BF = ml_dtypes.bfloat16

NCORE = 8
B, T, D = 2, 8192, 1024
TPC = 2048
NT = TPC // 128


class Sched:

    ENG = ("tensor", "vector", "scalar", "gpsimd", "sync")

    NQPOOL = 12

    def __init__(self, nc):
        self.nc = nc
        self.ctx = []
        self.sem = {}
        self.cnt = {}
        for e in self.ENG:
            self.sem[e] = self._mk_sem(e)
        self.qpool = {}
        self.qrr = {}
        for q in ("qsync", "qscalar", "qgpsimd", "qvector"):
            self.qpool[q] = []
            self.qrr[q] = 0
            for j in range(self.NQPOOL):
                name = f"{q}_{j}"
                self.sem[name] = self._mk_sem(name)
                self.qpool[q].append(name)
        self.same_engine_sync = Sched.SAME_ENGINE_SYNC
        self.known = {}
        self.writes = {}
        self.reads = {}
        self.stream = {e: [] for e in
                       ("tensor", "vector", "scalar", "gpsimd", "sync")}

    def _mk_sem(self, name):
        cm = self.nc.semaphore(name)
        h = cm.__enter__()
        self.ctx.append(cm)
        self.cnt[name] = 0
        return h

    @staticmethod
    def _span(ap):
        es = mybir.dt_size(ap.dtype) if hasattr(mybir, "dt_size") else {
            F32: 4, BF16: 2, U8: 1, mybir.dt.uint32: 4, mybir.dt.int32: 4,
            mybir.dt.uint16: 2}.get(ap.dtype, 4)
        lo = ap.offset
        ext = 1
        for st, n in ap.ap:
            ext += abs(st) * (n - 1)
        return (lo * es, (lo + ext) * es)

    @staticmethod
    def _norm(ap):
        return ap if hasattr(ap, "tensor") else ap[:, :]

    @staticmethod
    def _key(ap):
        t = ap.tensor
        return id(t)

    SAME_ENGINE_SYNC = True

    def _waits_for(self, eng, aps, table):
        ws = []
        for ap in aps:
            ap = self._norm(ap)
            k = self._key(ap)
            lo, hi = self._span(ap)
            for e in table.get(k, ()):
                elo, ehi, semname, cnt, weng = e
                if ehi <= lo or elo >= hi:
                    continue
                if weng == eng and semname == eng and (
                        weng == "tensor" or not self.same_engine_sync):
                    continue
                ws.append((semname, cnt))
        return ws

    def _calc_waits(self, eng, ws):
        best = {}
        for semname, cnt in ws:
            if semname.startswith("q"):
                cnt = self.cnt[semname]
            if cnt > best.get(semname, -1):
                best[semname] = cnt
        out = []
        for semname, cnt in best.items():
            kk = (eng, semname)
            if self.known.get(kk, 0) >= cnt:
                continue
            out.append((semname, cnt))
            self.known[kk] = cnt
        return out

    def _record(self, table, ap, semname, cnt, eng):
        ap = self._norm(ap)
        k = self._key(ap)
        lo, hi = self._span(ap)
        lst = table.setdefault(k, [])
        lst.append([lo, hi, semname, cnt, eng])
        if len(lst) > 192:
            old, keep = lst[:96], lst[96:]
            hull = {}
            for e in old:
                h = hull.get(e[2])
                if h is None:
                    hull[e[2]] = list(e)
                else:
                    h[0] = min(h[0], e[0])
                    h[1] = max(h[1], e[1])
                    h[3] = max(h[3], e[3])
            table[k] = list(hull.values()) + keep
            return

    def op(self, eng, fn, writes, reads, inc=True):
        ws = self._waits_for(eng, reads, self.writes)
        ws += self._waits_for(eng, writes, self.writes)
        ws += self._waits_for(eng, writes, self.reads)
        waits = self._calc_waits(eng, ws)
        semname = eng
        inc = True
        self.cnt[semname] += 1
        c = self.cnt[semname]
        self.stream[eng].append((waits, fn, semname, 1))
        for ap in writes:
            self._record(self.writes, ap, semname, c, eng)
        for ap in reads:
            self._record(self.reads, ap, semname, c, eng)

    def dma(self, queue, out_ap, in_ap, **kw):
        pool = self.qpool["q" + queue]
        qname = pool[self.qrr["q" + queue] % len(pool)]
        self.qrr["q" + queue] += 1
        ws = self._waits_for(queue, [in_ap], self.writes)
        ws += self._waits_for(queue, [out_ap], self.writes)
        ws += self._waits_for(queue, [out_ap], self.reads)
        waits = self._calc_waits(queue, ws)

        def fn(e, out_ap=out_ap, in_ap=in_ap, kw=kw):
            return e.dma_start(out_ap, in_ap, **kw)
        self.cnt[qname] += 16
        self.stream[queue].append((waits, fn, qname, 16))
        c = self.cnt[qname]
        self._record(self.writes, out_ap, qname, c, queue)
        self._record(self.reads, in_ap, qname, c, queue)

    def barrier_drain(self, eng="sync"):
        waits = []
        for semname, cnt in self.cnt.items():
            if cnt > 0 and self.known.get((eng, semname), 0) < cnt:
                waits.append((semname, cnt))
                self.known[(eng, semname)] = cnt
        self.stream[eng].append((waits, None, None, 0))

    def replay(self, block):
        decs = {"tensor": block.tensor, "vector": block.vector,
                "scalar": block.scalar, "gpsimd": block.gpsimd,
                "sync": block.sync}
        for eng, entries in self.stream.items():
            def body(e, entries=entries):
                for waits, fn, semname, amt in entries:
                    for sn, cnt in waits:
                        e.wait_ge(self.sem[sn], cnt)
                    if fn is not None:
                        inst = fn(e)
                        if semname is not None:
                            inst.then_inc(self.sem[semname], amt)
            decs[eng](body)


def dt_np(a):
    return np.ascontiguousarray(a)


def ceil_div(a, b):
    return (a + b - 1) // b


class Prog:

    def __init__(self, name):
        self.nc = bass.Bass("TRN2", target_bir_lowering=False)
        self.name = name
        self.inputs = {}
        self.outputs = {}
        self.sb_ctx = []

    def inp(self, name, shape, dtype=F32):
        ap = self.nc.dram_tensor(name, list(shape), dtype, kind="ExternalInput").ap()
        self.inputs[name] = ap
        return ap

    def out(self, name, shape, dtype=F32):
        ap = self.nc.dram_tensor(name, list(shape), dtype, kind="ExternalOutput").ap()
        self.outputs[name] = ap
        return ap

    def sb(self, name, shape, dtype=BF16):
        cm = self.nc.sbuf_tensor(name, list(shape), dtype)
        h = cm.__enter__()
        self.sb_ctx.append(cm)
        return h

    def ps(self, name, shape, dtype=F32):
        cm = self.nc.psum_tensor(name, list(shape), dtype)
        h = cm.__enter__()
        self.sb_ctx.append(cm)
        return h


def reg_consts(p, S, values):
    n = len(values)
    if n == 0:
        return
    ct = p.sb("c_consts", [128, n], F32)
    for i, v in enumerate(values):
        def f(g, i=i, v=v):
            return g.memset(ct[:, i:i + 1], v)
        S.op("gpsimd", f, [ct[:, i:i + 1]], [])
        p.nc.const_aps.register(F32, v, ct[:, i:i + 1])
    p.c_consts = ct


def build_const_tiles(p, S):
    ident = p.sb("c_ident", [128, 128], BF16)
    i1 = p.sb("c_iota1", [128, 128], F32)
    i2 = p.sb("c_iota2", [128, 128], F32)
    ones = p.sb("c_ones", [128, 128], BF16)

    def f(g):
        return g.iota(i1[:, :], [[1, 128]], channel_multiplier=0,
                      allow_small_or_imprecise_dtypes=True)
    S.op("gpsimd", f, [i1[:, :]], [])

    def f(g):
        return g.iota(i2[:, :], [[0, 128]], channel_multiplier=1,
                      allow_small_or_imprecise_dtypes=True)
    S.op("gpsimd", f, [i2[:, :]], [])

    def f(v):
        return v.tensor_tensor(ident[:, :], i1[:, :], i2[:, :], OP.is_equal)
    S.op("vector", f, [ident[:, :]], [i1[:, :], i2[:, :]])

    identF = p.sb("c_identF", [128, 128], F32)

    def f(v):
        return v.tensor_tensor(identF[:, :], i1[:, :], i2[:, :], OP.is_equal)
    S.op("vector", f, [identF[:, :]], [i1[:, :], i2[:, :]])

    def f(g):
        return g.memset(ones[:, :], 1.0)
    S.op("gpsimd", f, [ones[:, :]], [])
    p.identF = identF
    return ident, ones


def transpose_tile(S, psT, ident, dst_fm, src_tm_bf16, evac_eng="scalar"):
    pt = psT[:, 0:128]

    def f(t):
        return t.transpose(pt, src_tm_bf16, ident[:, :])
    S.op("tensor", f, [pt], [src_tm_bf16, ident[:, :]])

    def f(e):
        if evac_eng == "scalar":
            return e.copy(dst_fm, pt)
        return e.tensor_copy(dst_fm, pt)
    S.op(evac_eng, f, [dst_fm], [pt])


def layernorm_tiles(p, S, x_tiles, xln_tiles, nt, scratch, stats, eps=1e-5):
    for i in range(nt):
        ln_one_tile(p, S, x_tiles[i], xln_tiles[i], scratch, stats, eps)


def ln_one_tile(p, S, x, xln_out, scratch, stats, eps=1e-5, Dm=1024):
    if True:
        sumx = stats[:, 0:1]
        sumsq = stats[:, 1:2]

        def f(v, x=x, sumx=sumx):
            return v.tensor_scalar(scratch[:, :], x, 1.0, 0.0, OP.mult,
                                   OP.add, accum_out=sumx)
        S.op("vector", f, [scratch[:, :], sumx], [x])

        def f(v, x=x, sumsq=sumsq):
            return v.scalar_tensor_tensor(scratch[:, :], x, 1.0, x, OP.mult,
                                          OP.mult, accum_out=sumsq)
        S.op("vector", f, [scratch[:, :], sumsq], [x])

        mean = stats[:, 2:3]
        var = stats[:, 3:4]
        rstd = stats[:, 4:5]
        mb = stats[:, 5:6]

        def f(v):
            return v.tensor_scalar(mean, sumx, 1.0 / Dm, None, OP.mult)
        S.op("vector", f, [mean], [sumx])

        def f(v):
            return v.scalar_tensor_tensor(var, mean, -1.0, mean, OP.mult,
                                          OP.mult)
        S.op("vector", f, [var], [mean])

        def f(v):
            return v.scalar_tensor_tensor(var, sumsq, 1.0 / Dm, var, OP.mult,
                                          OP.add)
        S.op("vector", f, [var], [sumsq, var])

        def f(v):
            return v.tensor_scalar(var, var, eps, None, OP.add)
        S.op("vector", f, [var], [var])

        def f(sc):
            return sc.activation(rstd, var, AF.Ln, bias=0.0, scale=1.0)
        S.op("scalar", f, [rstd], [var])

        def f(sc):
            return sc.activation(rstd, rstd, AF.Exp, bias=0.0, scale=-0.5)
        S.op("scalar", f, [rstd], [rstd])

        def f(v):
            return v.scalar_tensor_tensor(mb, mean, -1.0, rstd, OP.mult,
                                          OP.mult)
        S.op("vector", f, [mb], [mean, rstd])

        def f(v, x=x, o=xln_out):
            return v.tensor_scalar(o, x, rstd, mb, OP.mult, OP.add)
        S.op("vector", f, [xln_out], [x, rstd, mb])


def mm_tiled(p, S, out_psum_fn, lhsT_fn, rhs_fn, nk, extra_first=None,
             start_clear=True):
    steps = []
    if extra_first is not None:
        steps.append(extra_first)
    for k in range(nk):
        steps.append((lhsT_fn(k), rhs_fn(k)))
    n = len(steps)
    for j, (lt, rh) in enumerate(steps):
        st = (j == 0) and start_clear
        sp = (j == n - 1)

        def f(t, lt=lt, rh=rh, st=st, sp=sp):
            return t.matmul(out_psum_fn, lt, rh, start=st, stop=sp)
        S.op("tensor", f, [out_psum_fn], [lt, rh], inc=(j == n - 1))


class Arena:

    def __init__(self, tensor, ncols):
        self.t = tensor
        self.n = ncols
        self.free_list = [(0, ncols)]

    def alloc(self, n, align=8):
        for idx, (lo, sz) in enumerate(self.free_list):
            alo = (lo + align - 1) // align * align
            if alo + n <= lo + sz:
                pre = (lo, alo - lo)
                post = (alo + n, lo + sz - (alo + n))
                repl = [p for p in (pre, post) if p[1] > 0]
                self.free_list[idx:idx + 1] = repl
                return alo
        raise MemoryError(f"arena OOM: want {n}, free={self.free_list}")

    def free(self, off, n):
        self.free_list.append((off, n))
        self.free_list.sort()
        merged = []
        for lo, sz in self.free_list:
            if merged and merged[-1][0] + merged[-1][1] == lo:
                merged[-1] = (merged[-1][0], merged[-1][1] + sz)
            else:
                merged.append((lo, sz))
        self.free_list = [tuple(m) for m in merged]


def evac(S, dst, src_psum, eng="scalar", scale=None):
    if scale is None:
        if eng == "scalar":
            def f(e):
                return e.copy(dst, src_psum)
        else:
            def f(e):
                return e.tensor_copy(dst, src_psum)
        S.op(eng, f, [dst], [src_psum])
    else:
        if eng == "scalar":
            def f(e):
                return e.activation(dst, src_psum, AF.Copy, bias=0.0, scale=scale)
        else:
            def f(e):
                return e.tensor_scalar(dst, src_psum, scale, None, OP.mult)
        S.op(eng, f, [dst], [src_psum])


def fp4_quant_tiles(p, S, AFa, ABa, x_f32, out_bf16, np_, nfree, tag):
    nb = nfree // 16
    C = 6291456.0
    sc_sz = 3 * nb + nfree * 4
    sc_off = AFa.alloc(sc_sz)
    A = AFa.t
    bmax = A[0:np_, sc_off:sc_off + nb]
    srec = A[0:np_, sc_off + nb:sc_off + 2 * nb]
    scl = A[0:np_, sc_off + 2 * nb:sc_off + 3 * nb]
    base = sc_off + 3 * nb
    u = A[0:np_, base:base + nfree]
    qlow = A[0:np_, base + nfree:base + 2 * nfree]
    a = A[0:np_, base + 2 * nfree:base + 3 * nfree]
    h = A[0:np_, base + 3 * nfree:base + 4 * nfree]
    qhs = a
    m_off = ABa.alloc(nfree // 2 + 8)
    mask = ABa.t[0:np_, m_off:m_off + nfree // 2].bitcast(U8)

    x3 = x_f32.rearrange("p (b i) -> p b i", i=16)

    def f(v):
        return v.tensor_reduce(bmax.unsqueeze(2), x3, axis=AX, op=OP.max,
                               apply_absolute_value=True)
    S.op("vector", f, [bmax], [x_f32])

    def f(v):
        return v.tensor_scalar(scl, bmax, 6e-12, 1.0 / 6.0, OP.max, OP.mult)
    S.op("vector", f, [scl], [bmax])

    def f(v):
        return v.reciprocal(srec, scl)
    S.op("vector", f, [srec], [scl])

    u3 = u.rearrange("p (b i) -> p b i", i=16)
    sre3 = srec.unsqueeze(2).broadcast_to((np_, nb, 16))

    def f(v):
        return v.tensor_tensor(u3, x3, sre3, OP.mult)
    S.op("vector", f, [u], [x_f32, srec])

    def f(v):
        return v.tensor_scalar(qlow, u, C, C, OP.add, OP.subtract)
    S.op("vector", f, [qlow], [u])

    def f(sc):
        return sc.activation(a, u, AF.Abs, bias=0.0, scale=1.0)
    S.op("scalar", f, [a], [u])

    def f(v):
        return v.tensor_scalar(h, a, 2.5, 2.0, OP.is_gt, OP.add)
    S.op("vector", f, [h], [a])

    def f(v):
        return v.scalar_tensor_tensor(h, a, 3.5, h, OP.is_gt, OP.add)
    S.op("vector", f, [h], [a, h])

    def f(v):
        return v.scalar_tensor_tensor(h, a, 5.0, h, OP.is_gt, OP.add)
    S.op("vector", f, [h], [a, h])

    def f(v):
        return v.scalar_tensor_tensor(h, a, 5.0, h, OP.is_gt, OP.add)
    S.op("vector", f, [h], [a, h])

    def f(v):
        return v.tensor_scalar(mask, a, 2.25, None, OP.is_gt)
    S.op("vector", f, [mask], [a])

    ui = u.bitcast(mybir.dt.uint32)
    hi = h.bitcast(mybir.dt.uint32)
    qi = qhs.bitcast(mybir.dt.uint32)

    def f(v):
        return v.tensor_scalar(qi, ui, 0x80000000, None, OP.bitwise_and)
    S.op("vector", f, [qhs], [u])

    def f(v):
        return v.tensor_tensor(qi, qi, hi, OP.bitwise_or)
    S.op("vector", f, [qhs], [qhs, h])

    def f(v):
        return v.select(u, mask, qhs, qlow, add_drain=True)
    S.op("vector", f, [u], [mask, qhs, qlow])

    o3 = out_bf16.rearrange("p (b i) -> p b i", i=16)
    scl3 = scl.unsqueeze(2).broadcast_to((np_, nb, 16))

    def f(v):
        return v.tensor_tensor(o3, u3, scl3, OP.mult)
    S.op("vector", f, [out_bf16], [u, scl])

    AFa.free(sc_off, sc_sz)
    ABa.free(m_off, nfree // 2 + 8)


def prog_start(p, af_cols, ab_cols, n_ps=6, n_psT=2):
    nc = p.nc
    S = Sched(nc)
    AFt = p.sb("AF", [128, af_cols], F32)
    ABt = p.sb("AB", [128, ab_cols], BF16)
    AFa = Arena(AFt, af_cols)
    ABa = Arena(ABt, ab_cols)
    ps = [p.ps(f"ps{i}", [128, 512], F32) for i in range(n_ps)]
    psT = [p.ps(f"psT{i}", [128, 1024], BF16) for i in range(n_psT)]
    ident, ones = build_const_tiles(p, S)
    return S, AFt, ABt, AFa, ABa, ps, psT, ident, ones


def prog_end(p, S):
    S.barrier_drain("sync")
    with p.nc.Block() as block:
        S.replay(block)
    for c in reversed(S.ctx):
        c.__exit__(None, None, None)
    for c in reversed(p.sb_ctx):
        c.__exit__(None, None, None)


def build_launch1a():
    p = Prog("l1a")
    nc = p.nc
    x0 = p.inp("x0", [TPC, D])
    wq = p.inp("aWqT", [D, D])
    wk = p.inp("aWkT", [D, D])
    wv = p.inp("aWvT", [D, D])
    prj = p.inp("aproj", [D, 256])
    wg8 = p.inp("wg8", [352, D])
    wu8 = p.inp("wu8", [352, D])
    wd8 = p.inp("wd8", [128, 2816])

    kvo = p.out("kvpart", [256, D])
    qpo = p.out("qp", [256, TPC], BF16)
    wgq = p.out("wgq", [352, D], BF16)
    wuq = p.out("wuq", [352, D], BF16)
    wdq = p.out("wdq", [128, 2816], BF16)

    S, AFt, ABt, AFa, ABa, ps, psT, ident, ones = prog_start(p, 26000, 52000)

    xres_off = AFa.alloc(NT * 1024)
    xres = lambda i: AFt[:, xres_off + 1024 * i: xres_off + 1024 * (i + 1)]
    st_off = AFa.alloc(64)
    stats = AFt[:, st_off:st_off + 64]

    for i in range(NT):
        S.dma("sync", xres(i), x0[128 * i:128 * (i + 1), :])

    def quant_w(w_in, w_out, rows, cols):
        nt_ = ceil_div(rows, 128)
        cch = 1024 if cols % 1024 == 0 else 1408
        off = AFa.alloc(cch)
        qf = AFt[:, off:off + cch]
        ob = ABa.alloc(cch)
        qb = ABt[:, ob:ob + cch]
        for i in range(nt_):
            r0, r1 = 128 * i, min(128 * (i + 1), rows)
            nr = r1 - r0
            for c0 in range(0, cols, cch):
                S.dma("gpsimd", qf[0:nr, :], w_in[r0:r1, c0:c0 + cch])
                obf = qb[0:nr, 0:cch]
                fp4_quant_tiles(p, S, AFa, ABa, qf[0:nr, :], obf, nr, cch, "w")
                S.dma("gpsimd", w_out[r0:r1, c0:c0 + cch], obf)
        AFa.free(off, cch)
        ABa.free(ob, cch)

    quant_w(wg8, wgq, 352, D)
    quant_w(wu8, wuq, 352, D)
    quant_w(wd8, wdq, 128, 2816)

    xln_off = ABa.alloc(2 * 1024)
    xlnT_off = ABa.alloc(8 * 2048)
    scr_off = ABa.alloc(1024)
    xlnT = ABt[:, xlnT_off:xlnT_off + 8 * 2048]
    scr = ABt[:, scr_off:scr_off + 1024]
    for i in range(NT):
        xln = ABt[:, xln_off + 1024 * (i % 2): xln_off + 1024 * (i % 2) + 1024]
        ln_one_tile(p, S, xres(i), xln, scr, stats)
        for j in range(8):
            src = xln[:, 128 * j:128 * (j + 1)]
            dst = xlnT[:, 2048 * j + 128 * i: 2048 * j + 128 * (i + 1)]
            transpose_tile(S, psT[(i * 8 + j) % 2], ident, dst, src,
                           evac_eng=("scalar", "vector")[j % 2])

    wslot_off = ABa.alloc(8 * 1024)
    wslot = ABt[:, wslot_off:wslot_off + 8 * 1024]
    prj_off = ABa.alloc(8 * 256)
    prjt = ABt[:, prj_off:prj_off + 8 * 256]
    for dk in range(8):
        S.dma("gpsimd", prjt[:, 256 * dk:256 * (dk + 1)],
              prj[128 * dk:128 * (dk + 1), :])

    def load_w(dram):
        for dk in range(8):
            S.dma("gpsimd", wslot[:, 1024 * dk:1024 * (dk + 1)],
                  dram[128 * dk:128 * (dk + 1), :])

    elu_off = AFa.alloc(512)
    eluf = AFt[:, elu_off:elu_off + 512]

    def elu_evac(dst_bf, psum, n):
        m1 = eluf[:, 0:n]

        def f(v):
            return v.tensor_scalar(m1, psum[:, 0:n], 0.0, None, OP.min)
        S.op("vector", f, [m1], [psum])

        def f(sc):
            return sc.activation(m1, m1, AF.Exp, bias=0.0, scale=1.0)
        S.op("scalar", f, [m1], [m1])

        def f(v):
            return v.scalar_tensor_tensor(dst_bf, psum[:, 0:n], 0.0, m1,
                                          OP.max, OP.add)
        S.op("vector", f, [dst_bf], [psum, m1])

    kp_off = ABa.alloc(16 * 256)
    kptm = ABt[:, kp_off:kp_off + 16 * 256]
    kfm_off = ABa.alloc(2 * 1024)
    load_w(wk)
    for i in range(NT):
        kfm = ABt[:, kfm_off + 1024 * (i % 2): kfm_off + 1024 * (i % 2) + 1024]
        for j in range(8):
            pb = ps[(i * 8 + j) % 4]

            def lt(dk, j=j):
                return wslot[:, 1024 * dk + 128 * j: 1024 * dk + 128 * (j + 1)]

            def rh(dk, i=i):
                return xlnT[:, 2048 * dk + 128 * i: 2048 * dk + 128 * (i + 1)]
            mm_tiled(p, S, pb[:, 0:128], lt, rh, 8)
            evac(S, kfm[:, 128 * j:128 * (j + 1)], pb[:, 0:128],
                 eng=("scalar", "vector")[j % 2])
        pb = ps[4]

        def lt(je, kfm=kfm):
            return kfm[:, 128 * je:128 * (je + 1)]

        def rh(je):
            return prjt[:, 256 * je:256 * (je + 1)]
        mm_tiled(p, S, pb[:, 0:256], lt, rh, 8)
        elu_evac(kptm[:, 256 * i:256 * (i + 1)], pb, 256)
    ABa.free(kfm_off, 2 * 1024)

    load_w(wv)
    vt_off = ABa.alloc(2 * 1024)
    kvps = [ps[0], ps[1], ps[2], ps[3]]
    for i in range(NT):
        vtm = ABt[:, vt_off + 1024 * (i % 2): vt_off + 1024 * (i % 2) + 1024]
        for ncc in range(2):
            pb = ps[4 + (i * 2 + ncc) % 2]

            def lt(dk, i=i):
                return xlnT[:, 2048 * dk + 128 * i: 2048 * dk + 128 * (i + 1)]

            def rh(dk, ncc=ncc):
                return wslot[:, 1024 * dk + 512 * ncc: 1024 * dk + 512 * (ncc + 1)]
            mm_tiled(p, S, pb[:, :], lt, rh, 8)
            evac(S, vtm[:, 512 * ncc:512 * (ncc + 1)], pb[:, :],
                 eng=("scalar", "vector")[ncc])
        for pj in range(2):
            for ncc in range(2):
                def f(t, i=i, pj=pj, ncc=ncc, vtm=vtm):
                    return t.matmul(
                        kvps[pj * 2 + ncc][:, :],
                        kptm[:, 256 * i + 128 * pj: 256 * i + 128 * (pj + 1)],
                        vtm[:, 512 * ncc:512 * (ncc + 1)],
                        start=(i == 0), stop=(i == NT - 1))
                S.op("tensor", f, [kvps[pj * 2 + ncc][:, :]],
                     [kptm, ABt[:, vt_off:vt_off + 2048]],
                     inc=(i == NT - 1))
    ABa.free(vt_off, 2 * 1024)

    kvp_off = AFa.alloc(2 * 1024)
    for pj in range(2):
        for ncc in range(2):
            evac(S, AFt[:, kvp_off + 1024 * pj + 512 * ncc:
                        kvp_off + 1024 * pj + 512 * (ncc + 1)],
                 kvps[pj * 2 + ncc][:, :], eng=("scalar", "vector")[ncc])
    for pj in range(2):
        S.dma("sync", kvo[128 * pj:128 * (pj + 1), :],
              AFt[:, kvp_off + 1024 * pj:kvp_off + 1024 * (pj + 1)])
    AFa.free(kvp_off, 2 * 1024)

    load_w(wq)
    qp_off = ABa.alloc(2 * 2048)
    qpfm = ABt[:, qp_off:qp_off + 2 * 2048]
    qfm_off = ABa.alloc(8 * 512)
    for tc in range(4):
        qfm = ABt[:, qfm_off:qfm_off + 8 * 512]
        for j in range(8):
            pb = ps[j % 4]

            def lt(dk, j=j):
                return wslot[:, 1024 * dk + 128 * j: 1024 * dk + 128 * (j + 1)]

            def rh(dk, tc=tc):
                return xlnT[:, 2048 * dk + 512 * tc: 2048 * dk + 512 * (tc + 1)]
            mm_tiled(p, S, pb[:, :], lt, rh, 8)
            evac(S, qfm[:, 512 * j:512 * (j + 1)], pb[:, :],
                 eng=("scalar", "vector")[j % 2])
        for pj in range(2):
            pb = ps[4 + pj]

            def lt(je, pj=pj):
                return prjt[:, 256 * je + 128 * pj: 256 * je + 128 * (pj + 1)]

            def rh(je, qfm=qfm):
                return qfm[:, 512 * je:512 * (je + 1)]
            mm_tiled(p, S, pb[:, :], lt, rh, 8)
            elu_evac(qpfm[:, 2048 * pj + 512 * tc: 2048 * pj + 512 * (tc + 1)],
                     pb, 512)
    for pj in range(2):
        S.dma("sync", qpo[128 * pj:128 * (pj + 1), :],
              qpfm[:, 2048 * pj:2048 * (pj + 1)])

    prog_end(p, S)
    return p


def build_launch1b():
    p = Prog("l1b")
    x0 = p.inp("x0", [TPC, D])
    qpi = p.inp("qp", [256, TPC], BF16)
    kvp = p.inp("kvparts", [1024, D])
    fw = p.inp("fWT", [D, D])
    fb = p.inp("fb", [D])
    fvr = p.inp("fVr", [D, 32])
    fct = p.inp("fCT", [32, D])
    bwa = p.inp("bWattn", [D, 3 * D])

    x2o = p.out("x2", [TPC, D])
    qkvo = p.out("qkv", [TPC, 3 * D], BF16)

    S, AFt, ABt, AFa, ABa, ps, psT, ident, ones = prog_start(p, 26000, 52000)

    xres_off = AFa.alloc(NT * 1024)
    xres = lambda i: AFt[:, xres_off + 1024 * i: xres_off + 1024 * (i + 1)]
    st_off = AFa.alloc(64)
    stats = AFt[:, st_off:st_off + 64]
    zcol_off = AFa.alloc(16)
    zcols = AFt[:, zcol_off:zcol_off + 16]

    for i in range(NT):
        S.dma("sync", xres(i), x0[128 * i:128 * (i + 1), :])

    kvf_off = AFa.alloc(2 * 1024)
    kvt_off = AFa.alloc(2 * 1024)
    kv_off = ABa.alloc(2 * 1024)
    kvsb = ABt[:, kv_off:kv_off + 2048]
    for pj in range(2):
        acc = AFt[:, kvf_off + 1024 * pj:kvf_off + 1024 * (pj + 1)]
        for j in range(4):
            t = AFt[:, kvt_off + 1024 * (j % 2):kvt_off + 1024 * (j % 2) + 1024]
            S.dma("sync", t, kvp[256 * j + 128 * pj:256 * j + 128 * (pj + 1), :])
            if j == 0:
                def f(v, t=t, acc=acc):
                    return v.tensor_copy(acc, t)
                S.op("vector", f, [acc], [t])
            else:
                def f(v, t=t, acc=acc):
                    return v.tensor_tensor(acc, acc, t, OP.add)
                S.op("vector", f, [acc], [acc, t])
        def f(v, acc=acc, pj=pj):
            return v.tensor_copy(kvsb[:, 1024 * pj:1024 * (pj + 1)], acc)
        S.op("vector", f, [kvsb], [acc])
    AFa.free(kvf_off, 2 * 1024)
    AFa.free(kvt_off, 2 * 1024)

    qp_off = ABa.alloc(2 * 2048)
    qpfm = ABt[:, qp_off:qp_off + 2 * 2048]
    for pj in range(2):
        S.dma("sync", qpfm[:, 2048 * pj:2048 * (pj + 1)],
              qpi[128 * pj:128 * (pj + 1), :])

    for i in range(NT):
        pb = ps[i % 2]

        def lt(pj, i=i):
            return qpfm[:, 2048 * pj + 128 * i: 2048 * pj + 128 * (i + 1)]

        def rh(pj):
            return ones[:, 0:1]
        mm_tiled(p, S, pb[:, 0:1], lt, rh, 2)
        zc = zcols[:, i:i + 1]

        def f(v, pb=pb, zc=zc):
            return v.tensor_scalar(zc, pb[:, 0:1], 1e-8, None, OP.add)
        S.op("vector", f, [zc], [pb])

        def f(v, zc=zc):
            return v.reciprocal(zc, zc)
        S.op("vector", f, [zc], [zc])

    for i in range(NT):
        for ncc in range(2):
            pb = ps[2 + (i * 2 + ncc) % 2]

            def lt(pj, i=i):
                return qpfm[:, 2048 * pj + 128 * i: 2048 * pj + 128 * (i + 1)]

            def rh(pj, ncc=ncc):
                return kvsb[:, 1024 * pj + 512 * ncc: 1024 * pj + 512 * (ncc + 1)]
            mm_tiled(p, S, pb[:, :], lt, rh, 2)
            xs = xres(i)[:, 512 * ncc:512 * (ncc + 1)]

            def f(v, pb=pb, xs=xs, i=i):
                return v.scalar_tensor_tensor(xs, pb[:, :], zcols[:, i:i + 1],
                                              xs, OP.mult, OP.add)
            S.op("vector", f, [xs], [pb, zcols, xs])
    ABa.free(qp_off, 2 * 2048)
    ABa.free(kv_off, 2048)

    xln_off = ABa.alloc(2 * 1024)
    xlnT_off = ABa.alloc(8 * 2048)
    scr_off = ABa.alloc(1024)
    xlnT = ABt[:, xlnT_off:xlnT_off + 8 * 2048]
    scr = ABt[:, scr_off:scr_off + 1024]

    def ln_T():
        for i in range(NT):
            xln = ABt[:, xln_off + 1024 * (i % 2): xln_off + 1024 * (i % 2) + 1024]
            ln_one_tile(p, S, xres(i), xln, scr, stats)
            for j in range(8):
                src = xln[:, 128 * j:128 * (j + 1)]
                dst = xlnT[:, 2048 * j + 128 * i: 2048 * j + 128 * (i + 1)]
                transpose_tile(S, psT[(i * 8 + j) % 2], ident, dst, src,
                               evac_eng=("scalar", "vector")[j % 2])

    ln_T()
    wslot_off = ABa.alloc(8 * 1024)
    wslot = ABt[:, wslot_off:wslot_off + 8 * 1024]
    for dk in range(8):
        S.dma("gpsimd", wslot[:, 1024 * dk:1024 * (dk + 1)],
              fw[128 * dk:128 * (dk + 1), :])
    fvr_off = ABa.alloc(256)
    fvrt = ABt[:, fvr_off:fvr_off + 256]
    for dk in range(8):
        S.dma("gpsimd", fvrt[:, 32 * dk:32 * (dk + 1)],
              fvr[128 * dk:128 * (dk + 1), :])
    fct_off = ABa.alloc(1024)
    fctt = ABt[:, fct_off:fct_off + 1024]
    S.dma("gpsimd", fctt[0:32, :], fct[:, :])
    fb_off = ABa.alloc(1024)
    fbt = ABt[:, fb_off:fb_off + 1024]
    S.dma("gpsimd", fbt[0:1, :], fb.unsqueeze(0))

    z2_off = ABa.alloc(2048)
    z2 = ABt[:, z2_off:z2_off + 2048]
    for tc in range(4):
        pb = ps[tc % 2]

        def lt(dk):
            return fvrt[:, 32 * dk:32 * (dk + 1)]

        def rh(dk, tc=tc):
            return xlnT[:, 2048 * dk + 512 * tc: 2048 * dk + 512 * (tc + 1)]
        mm_tiled(p, S, pb[0:32, :], lt, rh, 8)
        evac(S, z2[0:32, 512 * tc:512 * (tc + 1)], pb[0:32, :],
             eng=("scalar", "vector")[tc % 2])

    for i in range(NT):
        for ncc in range(2):
            pb = ps[(i * 2 + ncc) % 4]

            def lt(dk, i=i):
                return xlnT[:, 2048 * dk + 128 * i: 2048 * dk + 128 * (i + 1)]

            def rh(dk, ncc=ncc):
                return wslot[:, 1024 * dk + 512 * ncc: 1024 * dk + 512 * (ncc + 1)]
            mm_tiled(p, S, pb[:, :], lt, rh, 8,
                     extra_first=(ones[0:1, 0:128],
                                  fbt[0:1, 512 * ncc:512 * (ncc + 1)]))

            def f(t, pb=pb, i=i, ncc=ncc):
                return t.matmul(pb[:, :], z2[0:32, 128 * i:128 * (i + 1)],
                                fctt[0:32, 512 * ncc:512 * (ncc + 1)],
                                start=False, stop=True)
            S.op("tensor", f, [pb[:, :]], [z2, fctt])
            xs = xres(i)[:, 512 * ncc:512 * (ncc + 1)]

            def f(v, pb=pb, xs=xs):
                return v.scalar_tensor_tensor(xs, pb[:, :], 1.0, xs,
                                              OP.mult, OP.add)
            S.op("vector", f, [xs], [pb, xs])
    ABa.free(z2_off, 2048)
    ABa.free(fvr_off, 256)
    ABa.free(fct_off, 1024)
    ABa.free(fb_off, 1024)
    ABa.free(wslot_off, 8 * 1024)

    for i in range(NT):
        S.dma("sync", x2o[128 * i:128 * (i + 1), :], xres(i))

    bw_off = ABa.alloc(8 * 3072)
    bwt = ABt[:, bw_off:bw_off + 8 * 3072]
    for dk in range(8):
        S.dma("gpsimd", bwt[:, 3072 * dk:3072 * (dk + 1)],
              bwa[128 * dk:128 * (dk + 1), :])
    stg_off = ABa.alloc(2 * 3072)
    ln_T()
    for i in range(NT):
        stg = ABt[:, stg_off + 3072 * (i % 2):stg_off + 3072 * (i % 2) + 3072]
        for ncc in range(6):
            pb = ps[ncc % 4]

            def lt(dk, i=i):
                return xlnT[:, 2048 * dk + 128 * i: 2048 * dk + 128 * (i + 1)]

            def rh(dk, ncc=ncc):
                return bwt[:, 3072 * dk + 512 * ncc: 3072 * dk + 512 * (ncc + 1)]
            mm_tiled(p, S, pb[:, :], lt, rh, 8)
            scale = 0.125 if ncc < 2 else None
            evac(S, stg[:, 512 * ncc:512 * (ncc + 1)], pb[:, :],
                 eng=("scalar", "vector")[ncc % 2], scale=scale)
        S.dma("sync", qkvo[128 * i:128 * (i + 1), :], stg)

    prog_end(p, S)
    return p


def build_launch2(nh=4, ntqc=16, rev=False):
    p = Prog("l2")
    nc = p.nc
    qf = p.inp("qfm", [128, 2 * T], BF16)
    kf = p.inp("kfm", [128, 2 * T], BF16)
    vt = p.inp("vtm", [T, 256], BF16)
    yo = p.out("yfm", [128, 2 * T], BF16)

    S, AFt, ABt, AFa, ABa, ps, psT, ident, ones = prog_start(p, 4000, 62000)
    reg_consts(p, S, [1e12])

    qf_off = ABa.alloc(2 * T)
    kf_off = ABa.alloc(2 * T)
    vt_off = ABa.alloc(64 * 256)
    qfs = ABt[:, qf_off:qf_off + 2 * T]
    kfs = ABt[:, kf_off:kf_off + 2 * T]
    vts = ABt[:, vt_off:vt_off + 64 * 256]
    for j in range(2):
        S.dma("sync", qfs[:, T * j:T * (j + 1)], qf[:, T * j:T * (j + 1)])
        S.dma("sync", kfs[:, T * j:T * (j + 1)], kf[:, T * j:T * (j + 1)])
    for i in range(8):
        S.dma("scalar", vts[:, 2048 * i:2048 * (i + 1)].rearrange(
            "p (a b) -> p a b", b=256),
            vt[1024 * i:1024 * (i + 1), :].rearrange("(a p) b -> p a b", p=128))

    msk_off = ABa.alloc(4 * 512)
    io1_off = AFa.alloc(512)
    io2_off = AFa.alloc(512)
    io1 = AFt[:, io1_off:io1_off + 512]
    io2 = AFt[:, io2_off:io2_off + 512]

    def f(g):
        return g.iota(io1[:, :], [[1, 512]], channel_multiplier=0,
                      allow_small_or_imprecise_dtypes=True)
    S.op("gpsimd", f, [io1], [])

    def f(g):
        return g.iota(io2[:, 0:1], [[0, 1]], channel_multiplier=1,
                      allow_small_or_imprecise_dtypes=True)
    S.op("gpsimd", f, [io2[:, 0:1]], [])
    for j in range(4):
        mj = ABt[:, msk_off + 512 * j:msk_off + 512 * (j + 1)]

        def f(v, mj=mj, j=j):
            return v.scalar_tensor_tensor(mj, io1[:, :], -128.0 * j,
                                          io2[:, 0:1].broadcast_to((128, 512)),
                                          OP.add, OP.is_ge)
        S.op("vector", f, [mj], [io1, io2])

    att_off = ABa.alloc(3 * 512)
    sb_off = ABa.alloc(3 * 512)
    u_off = ABa.alloc(3 * 512)
    wc_off = ABa.alloc(3 * 512)
    stg_off = ABa.alloc(2 * 512)

    hl = list(range(nh))
    if rev:
        hl = hl[::-1]
    for h in hl:
        pb_ = 64 * (h % 2)
        cb = T * (h // 2)
        for tqc in range(ntqc):
            yps = ps[4 + tqc % 2]
            ns = 4 * tqc + 4
            for st in range(ns):
                r = st % 3
                sps = ps[st % 2]
                attb = ABt[:, att_off + 512 * r:att_off + 512 * (r + 1)]
                sbt = AFt[:, 1024 + 512 * r:1024 + 512 * (r + 1)] if False \
                    else ABt[:, sb_off + 512 * r:sb_off + 512 * (r + 1)]
                ut = ABt[:, u_off + 512 * r:u_off + 512 * (r + 1)]
                wct = ABt[:, wc_off + 512 * r:wc_off + 512 * (r + 1)]

                def f(t, st=st, tqc=tqc, sps=sps, pb_=pb_, cb=cb):
                    return t.matmul(
                        sps[:, :],
                        kfs[pb_:pb_ + 64, cb + 128 * st:cb + 128 * (st + 1)],
                        qfs[pb_:pb_ + 64, cb + 512 * tqc:cb + 512 * (tqc + 1)],
                        start=True, stop=True)
                S.op("tensor", f, [sps[:, :]],
                     [kfs[pb_:pb_ + 64, cb + 128 * st:cb + 128 * (st + 1)],
                      qfs[pb_:pb_ + 64, cb + 512 * tqc:cb + 512 * (tqc + 1)]])

                def f(sc, sps=sps, sbt=sbt):
                    return sc.copy(sbt, sps[:, :])
                S.op("scalar", f, [sbt], [sps])

                def f(sc, sps=sps, wct=wct):
                    return sc.activation(wct, sps[:, :], AF.Relu,
                                         bias=1e12, scale=1e10)
                S.op("scalar", f, [wct], [sps])

                def f(v, sbt=sbt, ut=ut):
                    return v.tensor_scalar(ut, sbt, -1e-5, 1e-3, OP.mult, OP.add)
                S.op("vector", f, [ut], [sbt])

                def f(v, sbt=sbt, wct2=attb):
                    return v.tensor_scalar(wct2, sbt, 0.0, None, OP.max)
                S.op("vector", f, [attb], [sbt])

                def f(v, sbt=sbt, attb=attb):
                    return v.scalar_tensor_tensor(attb, sbt, 0.01, attb,
                                                  OP.add, OP.mult)
                S.op("vector", f, [attb], [sbt, attb])

                def f(v, ut=ut, wct=wct):
                    return v.tensor_tensor(ut, ut, wct, OP.min)
                S.op("vector", f, [ut], [ut, wct])

                def f(v, attb=attb, ut=ut):
                    return v.scalar_tensor_tensor(attb, attb, 1e-3, ut,
                                                  OP.mult, OP.add)
                S.op("vector", f, [attb], [attb, ut])

                dj = st - 4 * tqc
                if dj >= 0:
                    mj = ABt[:, msk_off + 512 * dj:msk_off + 512 * (dj + 1)]

                    def f(v, attb=attb, mj=mj):
                        return v.tensor_tensor(attb, attb, mj, OP.mult)
                    S.op("vector", f, [attb], [attb, mj])

                def f(t, st=st, h=h, attb=attb, yps=yps, ns=ns):
                    return t.matmul(
                        yps[0:64, :],
                        vts[:, 256 * st + 64 * h:256 * st + 64 * h + 64],
                        attb, start=(st == 0), stop=(st == ns - 1))
                S.op("tensor", f, [yps[0:64, :]],
                     [vts[:, 256 * st + 64 * h:256 * st + 64 * h + 64], attb],
                     inc=(st == ns - 1))
            stg = ABt[:, stg_off + 512 * (tqc % 2):stg_off + 512 * (tqc % 2) + 512]

            def f(v, yps=yps, stg=stg):
                return v.tensor_copy(stg[0:64, :], yps[0:64, :])
            S.op("vector", f, [stg[0:64, :]], [yps[0:64, :]])
            S.dma("sync", yo[pb_:pb_ + 64, cb + 512 * tqc:cb + 512 * (tqc + 1)],
                  stg[0:64, :])

    prog_end(p, S)
    return p


def build_launch3():
    p = Prog("l3")
    x2i = p.inp("x2", [TPC, D])
    yT = p.inp("yT", [D, TPC], BF16)
    bwp = p.inp("bWproj", [D, D])
    fw = p.inp("fWT", [D, D])
    fb = p.inp("fb", [D])
    fvr = p.inp("fVr", [D, 32])
    fct = p.inp("fCT", [32, D])
    dwq = p.inp("dWqT", [D, 128])
    dwk = p.inp("dWkT", [D, 128])
    dwv = p.inp("dWvT", [D, 128])
    dwg = p.inp("dWgT", [D, 128])
    dwab = p.inp("dWabT", [D, 2])
    dbias = p.inp("dbias", [8, 128])

    x4o = p.out("x4", [TPC, D])
    dqo = p.out("dq", [TPC, 128], BF16)
    dko = p.out("dk", [TPC, 128], BF16)
    dvo = p.out("dv", [TPC, 128], BF16)
    dgo = p.out("dgate", [TPC, 128], BF16)
    dabo = p.out("dab", [TPC, 2])

    S, AFt, ABt, AFa, ABa, ps, psT, ident, ones = prog_start(p, 24000, 54500)

    xres_off = AFa.alloc(NT * 1024)
    xres = lambda i: AFt[:, xres_off + 1024 * i: xres_off + 1024 * (i + 1)]
    st_off = AFa.alloc(64)
    stats = AFt[:, st_off:st_off + 64]

    for i in range(NT):
        S.dma("sync", xres(i), x2i[128 * i:128 * (i + 1), :])

    yT_off = ABa.alloc(8 * 2048)
    yTs = ABt[:, yT_off:yT_off + 8 * 2048]
    for dk_ in range(8):
        S.dma("gpsimd", yTs[:, 2048 * dk_:2048 * (dk_ + 1)],
              yT[128 * dk_:128 * (dk_ + 1), :])
    wslot_off = ABa.alloc(8 * 1024)
    wslot = ABt[:, wslot_off:wslot_off + 8 * 1024]
    for dk_ in range(8):
        S.dma("gpsimd", wslot[:, 1024 * dk_:1024 * (dk_ + 1)],
              bwp[128 * dk_:128 * (dk_ + 1), :])
    for i in range(NT):
        for ncc in range(2):
            pb = ps[(i * 2 + ncc) % 4]

            def lt(dk_, i=i):
                return yTs[:, 2048 * dk_ + 128 * i: 2048 * dk_ + 128 * (i + 1)]

            def rh(dk_, ncc=ncc):
                return wslot[:, 1024 * dk_ + 512 * ncc: 1024 * dk_ + 512 * (ncc + 1)]
            mm_tiled(p, S, pb[:, :], lt, rh, 8)
            xs = xres(i)[:, 512 * ncc:512 * (ncc + 1)]

            def f(v, pb=pb, xs=xs):
                return v.scalar_tensor_tensor(xs, pb[:, :], 1.0, xs,
                                              OP.mult, OP.add)
            S.op("vector", f, [xs], [pb, xs])
    ABa.free(yT_off, 8 * 2048)

    xln_off = ABa.alloc(2 * 1024)
    xlnT_off = ABa.alloc(8 * 2048)
    scr_off = ABa.alloc(1024)
    xlnT = ABt[:, xlnT_off:xlnT_off + 8 * 2048]
    scr = ABt[:, scr_off:scr_off + 1024]

    def ln_T():
        for i in range(NT):
            xln = ABt[:, xln_off + 1024 * (i % 2): xln_off + 1024 * (i % 2) + 1024]
            ln_one_tile(p, S, xres(i), xln, scr, stats)
            for j in range(8):
                src = xln[:, 128 * j:128 * (j + 1)]
                dst = xlnT[:, 2048 * j + 128 * i: 2048 * j + 128 * (i + 1)]
                transpose_tile(S, psT[(i * 8 + j) % 2], ident, dst, src,
                               evac_eng=("scalar", "vector")[j % 2])

    ln_T()
    for dk_ in range(8):
        S.dma("gpsimd", wslot[:, 1024 * dk_:1024 * (dk_ + 1)],
              fw[128 * dk_:128 * (dk_ + 1), :])
    fvr_off = ABa.alloc(256)
    fvrt = ABt[:, fvr_off:fvr_off + 256]
    for dk_ in range(8):
        S.dma("gpsimd", fvrt[:, 32 * dk_:32 * (dk_ + 1)],
              fvr[128 * dk_:128 * (dk_ + 1), :])
    fct_off = ABa.alloc(1024)
    fctt = ABt[:, fct_off:fct_off + 1024]
    S.dma("gpsimd", fctt[0:32, :], fct[:, :])
    fb_off = ABa.alloc(1024)
    fbt = ABt[:, fb_off:fb_off + 1024]
    S.dma("gpsimd", fbt[0:1, :], fb.unsqueeze(0))

    z2_off = ABa.alloc(2048)
    z2 = ABt[:, z2_off:z2_off + 2048]
    for tc in range(4):
        pb = ps[tc % 2]

        def lt(dk_):
            return fvrt[:, 32 * dk_:32 * (dk_ + 1)]

        def rh(dk_, tc=tc):
            return xlnT[:, 2048 * dk_ + 512 * tc: 2048 * dk_ + 512 * (tc + 1)]
        mm_tiled(p, S, pb[0:32, :], lt, rh, 8)
        evac(S, z2[0:32, 512 * tc:512 * (tc + 1)], pb[0:32, :],
             eng=("scalar", "vector")[tc % 2])

    for i in range(NT):
        for ncc in range(2):
            pb = ps[(i * 2 + ncc) % 4]

            def lt(dk_, i=i):
                return xlnT[:, 2048 * dk_ + 128 * i: 2048 * dk_ + 128 * (i + 1)]

            def rh(dk_, ncc=ncc):
                return wslot[:, 1024 * dk_ + 512 * ncc: 1024 * dk_ + 512 * (ncc + 1)]
            mm_tiled(p, S, pb[:, :], lt, rh, 8,
                     extra_first=(ones[0:1, 0:128],
                                  fbt[0:1, 512 * ncc:512 * (ncc + 1)]))

            def f(t, pb=pb, i=i, ncc=ncc):
                return t.matmul(pb[:, :], z2[0:32, 128 * i:128 * (i + 1)],
                                fctt[0:32, 512 * ncc:512 * (ncc + 1)],
                                start=False, stop=True)
            S.op("tensor", f, [pb[:, :]], [z2, fctt])
            xs = xres(i)[:, 512 * ncc:512 * (ncc + 1)]

            def f(v, pb=pb, xs=xs):
                return v.scalar_tensor_tensor(xs, pb[:, :], 1.0, xs,
                                              OP.mult, OP.add)
            S.op("vector", f, [xs], [pb, xs])
    ABa.free(z2_off, 2048)
    ABa.free(fvr_off, 256)
    ABa.free(fct_off, 1024)
    ABa.free(fb_off, 1024)

    for i in range(NT):
        S.dma("sync", x4o[128 * i:128 * (i + 1), :], xres(i))

    ln_T()
    dw_off = ABa.alloc(4 * 8 * 128 + 8 * 2)
    dws = ABt[:, dw_off:dw_off + 4 * 1024 + 16]
    for wi, wdram in enumerate([dwq, dwk, dwv, dwg]):
        for dk_ in range(8):
            S.dma("gpsimd", dws[:, 1024 * wi + 128 * dk_:1024 * wi + 128 * (dk_ + 1)],
                  wdram[128 * dk_:128 * (dk_ + 1), :])
    for dk_ in range(8):
        S.dma("gpsimd", dws[:, 4096 + 2 * dk_:4096 + 2 * (dk_ + 1)],
              dwab[128 * dk_:128 * (dk_ + 1), :])
    db_off = ABa.alloc(8 * 128)
    dbt = ABt[:, db_off:db_off + 8 * 128]
    for r in range(8):
        S.dma("gpsimd", dbt[r:r + 1, 0:128] if r == 0 else dbt[0:1, 128 * r:128 * (r + 1)],
              dbias[r:r + 1, :])

    nrm_off = AFa.alloc(8)
    nrm = AFt[:, nrm_off:nrm_off + 8]
    kf_off = AFa.alloc(512)
    kf32 = AFt[:, kf_off:kf_off + 512]
    stg_off = ABa.alloc(2 * 128)

    for i in range(NT):
        for wi, (outd, brow) in enumerate([(dqo, 0), (dko, 1), (dvo, 2),
                                           (dgo, 3)]):
            pb = ps[wi % 2]

            def lt(dk_, i=i):
                return xlnT[:, 2048 * dk_ + 128 * i: 2048 * dk_ + 128 * (i + 1)]

            def rh(dk_, wi=wi):
                return dws[:, 1024 * wi + 128 * dk_:1024 * wi + 128 * (dk_ + 1)]
            mm_tiled(p, S, pb[:, 0:128], lt, rh, 8,
                     extra_first=(ones[0:1, 0:128],
                                  dbt[0:1, 128 * brow:128 * (brow + 1)]))
            stg = ABt[:, stg_off:stg_off + 128]
            if wi == 1:
                def f(v, pb=pb):
                    return v.tensor_copy(kf32[:, 0:128], pb[:, 0:128])
                S.op("vector", f, [kf32[:, 0:128]], [pb])

                def f(v, stg=stg):
                    return v.scalar_tensor_tensor(
                        stg, kf32[:, 0:128], 1.0, kf32[:, 0:128],
                        OP.mult, OP.mult, accum_out=nrm[:, 0:1])
                S.op("vector", f, [stg, nrm[:, 0:1]], [kf32[:, 0:128]])

                def f(sc):
                    return sc.activation(nrm[:, 1:2], nrm[:, 0:1], AF.Sqrt,
                                         bias=0.0, scale=1.0)
                S.op("scalar", f, [nrm[:, 1:2]], [nrm[:, 0:1]])

                def f(v):
                    return v.tensor_scalar(nrm[:, 2:3], nrm[:, 1:2], 1e-12,
                                           None, OP.max)
                S.op("vector", f, [nrm[:, 2:3]], [nrm[:, 1:2]])

                def f(v):
                    return v.reciprocal(nrm[:, 3:4], nrm[:, 2:3])
                S.op("vector", f, [nrm[:, 3:4]], [nrm[:, 2:3]])

                def f(v, stg=stg):
                    return v.tensor_scalar(stg, kf32[:, 0:128], nrm[:, 3:4],
                                           None, OP.mult)
                S.op("vector", f, [stg], [kf32[:, 0:128], nrm])
            elif wi == 3:
                def f(sc, pb=pb, stg=stg):
                    return sc.activation(stg, pb[:, 0:128], AF.Sigmoid,
                                         bias=0.0, scale=1.0)
                S.op("scalar", f, [stg], [pb])
            else:
                evac(S, stg, pb[:, 0:128], eng=("scalar", "vector")[wi % 2])
            S.dma("sync", outd[128 * i:128 * (i + 1), :], stg)
        pb = ps[4]

        def lt(dk_, i=i):
            return xlnT[:, 2048 * dk_ + 128 * i: 2048 * dk_ + 128 * (i + 1)]

        def rh(dk_):
            return dws[:, 4096 + 2 * dk_:4096 + 2 * (dk_ + 1)]
        mm_tiled(p, S, pb[:, 0:2], lt, rh, 8,
                 extra_first=(ones[0:1, 0:128], dbt[0:1, 512 + 0:512 + 2]))

        def f(sc, pb=pb):
            return sc.activation(kf32[:, 0:2], pb[:, 0:2], AF.Sigmoid,
                                 bias=0.0, scale=1.0)
        S.op("scalar", f, [kf32[:, 0:2]], [pb])
        S.dma("sync", dabo[128 * i:128 * (i + 1), :], kf32[:, 0:2])

    prog_end(p, S)
    return p


def build_launch4(dbg=False):
    p = Prog("l4")
    NCH = 64
    kfm = p.inp("kfm", [128, T], BF16)
    ktm = p.inp("ktm", [T, 128], BF16)
    qpk = p.inp("qpk", [32, T], BF16)
    vbl = p.inp("vbl", [T, 32], BF16)
    abr = p.inp("abr", [NCH, 128])
    bbr = p.inp("bbr", [NCH, 128])
    bcT = p.inp("bcT", [128, NCH])
    opo = p.out("opart", [T, 128])
    if dbg:
        dbgo = p.out("dbg", [128, 6 * 128], mybir.dt.float32)
        p.dbg2o = p.out("dbg2", [128, 3 * 128], mybir.dt.float32)

    S, AFt, ABt, AFa, ABa, ps, psT, ident, ones = prog_start(p, 8000, 60000)

    kf_off = ABa.alloc(T)
    kfs = ABt[:, kf_off:kf_off + T]
    kt_off = ABa.alloc(T)
    kts = ABt[:, kt_off:kt_off + T]
    qp_off = ABa.alloc(T)
    qps = ABt[:, qp_off:qp_off + T]
    vb_off = ABa.alloc(NCH * 32)
    vbs = ABt[:, vb_off:vb_off + NCH * 32]
    for j in range(4):
        S.dma("sync", kfs[:, 2048 * j:2048 * (j + 1)],
              kfm[:, 2048 * j:2048 * (j + 1)])
        S.dma("sync",
              kts[:, 2048 * j:2048 * (j + 1)].rearrange("p (a b) -> p a b", b=128),
              ktm[2048 * j:2048 * (j + 1), :].rearrange("(a p) b -> p a b", p=128))
        S.dma("sync", qps[0:32, 2048 * j:2048 * (j + 1)],
              qpk[:, 2048 * j:2048 * (j + 1)])
    S.dma("sync", vbs[:, :].rearrange("p (a b) -> p a b", b=32),
          vbl.rearrange("(a p) b -> p a b", p=128))
    ab_off = AFa.alloc(128)
    abs_ = AFt[0:NCH, ab_off:ab_off + 128]
    bb_off = AFa.alloc(128)
    bbs = AFt[0:NCH, bb_off:bb_off + 128]
    bcT_off = AFa.alloc(NCH)
    bcTs = AFt[:, bcT_off:bcT_off + NCH]
    S.dma("sync", abs_, abr[:, :])
    S.dma("sync", bbs, bbr[:, :])
    S.dma("sync", bcTs, bcT[:, :])

    pr_off = AFa.alloc(8 * NCH + 4 * 128 + 512)
    la_all = AFt[0:NCH, pr_off + 8 * NCH:pr_off + 8 * NCH + 128]
    lab_all = AFt[0:NCH, pr_off + 8 * NCH + 128:pr_off + 8 * NCH + 256]
    lnb = AFt[0:NCH, pr_off + 8 * NCH + 256:pr_off + 8 * NCH + 384]
    laT = AFt[:, pr_off:pr_off + NCH]
    labT = AFt[:, pr_off + NCH:pr_off + 2 * NCH]
    nlaT = AFt[:, pr_off + 2 * NCH:pr_off + 3 * NCH]
    alT = AFt[:, pr_off + 3 * NCH:pr_off + 4 * NCH]
    gaT = AFt[:, pr_off + 4 * NCH:pr_off + 5 * NCH]
    aCb = AFt[:, pr_off + 5 * NCH:pr_off + 6 * NCH]
    bal = AFt[:, pr_off + 6 * NCH:pr_off + 7 * NCH]
    scr1 = AFt[:, pr_off + 7 * NCH:pr_off + 8 * NCH]

    def f(sc):
        return sc.activation(la_all, abs_, AF.Ln, bias=0.0, scale=1.0)
    S.op("scalar", f, [la_all], [abs_])

    def f(v):
        return v.tensor_tensor_scan(la_all, la_all, la_all, 0.0, OP.add,
                                    OP.bypass)
    S.op("vector", f, [la_all], [la_all])

    def f(sc):
        return sc.activation(lnb, bbs, AF.Ln, bias=0.0, scale=1.0)
    S.op("scalar", f, [lnb], [bbs])

    def f(v):
        return v.tensor_tensor(lab_all, la_all, lnb, OP.add)
    S.op("vector", f, [lab_all], [la_all, lnb])

    def t32(dst, src, n_in, n_out):
        def f(t):
            return t.transpose(ps[5][0:n_out, 0:n_in], src,
                               p.identF[0:n_in, 0:n_in])
        S.op("tensor", f, [ps[5][0:n_out, 0:n_in]], [src, ident[:, :]])

        def f(v):
            return v.tensor_copy(dst, ps[5][0:n_out, 0:n_in])
        S.op("vector", f, [dst], [ps[5][0:n_out, 0:n_in]])

    t32(laT, la_all, NCH, 128)
    t32(labT, lab_all, NCH, 128)

    def f(v):
        return v.tensor_scalar(nlaT, laT, -1.0, None, OP.mult)
    S.op("vector", f, [nlaT], [laT])

    def f(sc):
        return sc.activation(alT, laT, AF.Exp, bias=0.0, scale=1.0)
    S.op("scalar", f, [alT], [laT])

    onesF_off = AFa.alloc(128)
    onesF = AFt[:, onesF_off:onesF_off + 128]

    def f(g):
        return g.memset(onesF[0:1, :], 1.0)
    S.op("gpsimd", f, [onesF[0:1, :]], [])

    laCc = AFt[0:NCH, pr_off + 8 * NCH + 384:pr_off + 8 * NCH + 385]

    def f(v):
        return v.tensor_copy(laCc, la_all[0:NCH, 127:128])
    S.op("vector", f, [laCc], [la_all])
    laCr = AFt[0:1, pr_off + 8 * NCH + 386:pr_off + 8 * NCH + 386 + NCH]
    t32(laCr, laCc, NCH, 1)
    onesb = ones

    def f(t):
        return t.matmul(ps[4][:, 0:NCH], onesF[0:1, 0:128], laCr,
                        start=True, stop=True)
    S.op("tensor", f, [ps[4][:, 0:NCH]], [onesF, laCr])

    def f(sc):
        return sc.activation(aCb, ps[4][:, 0:NCH], AF.Exp, bias=0.0, scale=1.0)
    S.op("scalar", f, [aCb], [ps[4]])

    def f(v):
        return v.scalar_tensor_tensor(scr1, laT, -1.0, ps[4][:, 0:NCH],
                                      OP.mult, OP.add)
    S.op("vector", f, [scr1], [laT, ps[4]])

    def f(sc):
        return sc.activation(gaT, scr1, AF.Exp, bias=0.0, scale=1.0)
    S.op("scalar", f, [gaT], [scr1])

    def f(v):
        return v.scalar_tensor_tensor(bal, bcTs, -1.0, alT, OP.mult, OP.mult)
    S.op("vector", f, [bal], [bcTs, alT])

    mU_off = ABa.alloc(2 * 128)
    strictUneg = ABt[:, mU_off:mU_off + 128]
    maskUincl = ABt[:, mU_off + 128:mU_off + 256]
    io1 = AFt[:, pr_off + 8 * NCH + 512:pr_off + 8 * NCH + 512 + 128]
    io2 = AFt[:, pr_off + 8 * NCH + 640:pr_off + 8 * NCH + 641]

    def f(g):
        return g.iota(io1, [[1, 128]], channel_multiplier=0,
                      allow_small_or_imprecise_dtypes=True)
    S.op("gpsimd", f, [io1], [])

    def f(g):
        return g.iota(io2, [[0, 1]], channel_multiplier=1,
                      allow_small_or_imprecise_dtypes=True)
    S.op("gpsimd", f, [io2], [])

    def f(v):
        return v.scalar_tensor_tensor(maskUincl, io1, 0.0,
                                      io2.broadcast_to((128, 128)),
                                      OP.add, OP.is_ge)
    S.op("vector", f, [maskUincl], [io1, io2])

    def f(v):
        return v.scalar_tensor_tensor(scr1[:, 0:128] if False else strictUneg,
                                      io1, 0.0,
                                      io2.broadcast_to((128, 128)),
                                      OP.add, OP.is_gt)
    S.op("vector", f, [strictUneg], [io1, io2])

    def f(v):
        return v.tensor_scalar(strictUneg, strictUneg, -1.0, None, OP.mult)
    S.op("vector", f, [strictUneg], [strictUneg])

    TT_off = ABa.alloc(NCH * 128)
    Tts = ABt[:, TT_off:TT_off + NCH * 128]
    GG_off = ABa.alloc(NCH * 128)
    Ggs = ABt[:, GG_off:GG_off + NCH * 128]
    w1_off = ABa.alloc(6 * 128)
    wrk = lambda i: ABt[:, w1_off + 128 * i:w1_off + 128 * (i + 1)]
    rrow_off = AFa.alloc(256)
    larow = AFt[0:1, rrow_off:rrow_off + 128]
    labrow = AFt[0:1, rrow_off + 128:rrow_off + 256]
    o1x_off = AFa.alloc(256)

    def bf_t(dst, src):
        def f(t):
            return t.transpose(psT[0][:, 0:128], src, ident[:, :])
        S.op("tensor", f, [psT[0][:, 0:128]], [src, ident[:, :]])

        def f(v):
            return v.tensor_copy(dst, psT[0][:, 0:128])
        S.op("vector", f, [dst], [psT[0][:, 0:128]])

    for ch in range(NCH):
        Tt = Tts[:, 128 * ch:128 * (ch + 1)]
        Gg = Ggs[:, 128 * ch:128 * (ch + 1)]
        def f(t, ch=ch):
            return t.matmul(ps[0][:, 0:128], kfs[:, 128 * ch:128 * (ch + 1)],
                            kfs[:, 128 * ch:128 * (ch + 1)], start=True,
                            stop=True)
        S.op("tensor", f, [ps[0][:, 0:128]],
             [kfs[:, 128 * ch:128 * (ch + 1)]])
        t32(larow, laT[:, ch:ch + 1], 128, 1)
        t32(labrow, labT[:, ch:ch + 1], 128, 1)

        def f(t):
            return t.matmul(ps[1][:, 0:128], onesF[0:1, 0:128], labrow,
                            start=True, stop=True)
        S.op("tensor", f, [ps[1][:, 0:128]], [onesF, labrow])

        def f(t):
            return t.matmul(ps[2][:, 0:128], onesF[0:1, 0:128], larow,
                            start=True, stop=True)
        S.op("tensor", f, [ps[2][:, 0:128]], [onesF, larow])

        A1 = wrk(0)
        ex1 = AFt[:, o1x_off:o1x_off + 128]
        ex2 = AFt[:, o1x_off + 128:o1x_off + 256]

        def f(v, ch=ch):
            return v.tensor_scalar(ex1, ps[1][:, 0:128], nlaT[:, ch:ch + 1],
                                   None, OP.add)
        S.op("vector", f, [ex1], [ps[1], nlaT])

        def f(v):
            return v.tensor_scalar(ex1, ex1, 0.0, None, OP.min)
        S.op("vector", f, [ex1], [ex1])

        def f(sc, A1=A1):
            return sc.activation(A1, ex1, AF.Exp, bias=0.0, scale=1.0)
        S.op("scalar", f, [A1], [ex1])

        def f(v, ch=ch):
            return v.tensor_scalar(ex2, ps[2][:, 0:128], nlaT[:, ch:ch + 1],
                                   None, OP.add)
        S.op("vector", f, [ex2], [ps[2], nlaT])

        def f(v):
            return v.tensor_scalar(ex2, ex2, 0.0, None, OP.min)
        S.op("vector", f, [ex2], [ex2])

        def f(sc, Gg=Gg):
            return sc.activation(Gg, ex2, AF.Exp, bias=0.0, scale=1.0)
        S.op("scalar", f, [Gg], [ex2])

        if dbg and ch == 0:
            dstg2_off = AFa.alloc(3 * 128)
            dstg2 = AFt[:, dstg2_off:dstg2_off + 3 * 128]

            def f(v):
                return v.tensor_copy(dstg2[0:1, 0:128], larow)
            S.op("vector", f, [dstg2[:, 0:128]], [larow])

            def f(v):
                return v.tensor_copy(dstg2[:, 128:256], ps[2][:, 0:128])
            S.op("vector", f, [dstg2[:, 128:256]], [ps[2]])

            def f(v):
                return v.tensor_copy(dstg2[:, 256:384], ex2)
            S.op("vector", f, [dstg2[:, 256:384]], [ex2])
            S.dma("sync", p.dbg2o, dstg2)

        def f(v, Gg=Gg):
            return v.tensor_tensor(Gg, Gg, maskUincl, OP.mult)
        S.op("vector", f, [Gg], [Gg, maskUincl])

        Nt = wrk(1)

        def f(v, A1=A1, Nt=Nt):
            return v.tensor_tensor(Nt, A1, ps[0][:, 0:128], OP.mult)
        S.op("vector", f, [Nt], [A1, ps[0]])

        def f(v, Nt=Nt):
            return v.tensor_tensor(Nt, Nt, strictUneg, OP.mult)
        S.op("vector", f, [Nt], [Nt, strictUneg])

        curT = wrk(2)

        def f(v, curT=curT, Nt=Nt):
            return v.tensor_copy(curT, Nt)
        S.op("vector", f, [curT], [Nt])

        def f(v, Tt=Tt, Nt=Nt):
            return v.tensor_tensor(Tt, Nt, ident[:, :], OP.add)
        S.op("vector", f, [Tt], [Nt, ident])
        for j in range(1, 7):
            un = wrk(3)
            bf_t(un, curT)
            def f(t, un=un, curT=curT):
                return t.matmul(ps[3][:, 0:128], curT, un,
                                start=True, stop=True)
            S.op("tensor", f, [ps[3][:, 0:128]], [un, curT])
            squn = wrk(4)

            def f(v, squn=squn):
                return v.tensor_copy(squn, ps[3][:, 0:128])
            S.op("vector", f, [squn], [ps[3][:, 0:128]])
            if j < 6:
                def f(t, un=un, curT=curT):
                    return t.matmul(ps[4][:, 0:128], un, curT,
                                    start=True, stop=True)
                S.op("tensor", f, [ps[4][:, 0:128]], [un, curT])
            unI = wrk(5)

            def f(v, unI=unI, squn=squn):
                return v.tensor_tensor(unI, squn, ident[:, :], OP.add)
            S.op("vector", f, [unI], [squn, ident])

            def f(t, unI=unI, Tt=Tt):
                return t.matmul(ps[2][:, 0:128], unI, Tt, start=True,
                                stop=True)
            S.op("tensor", f, [ps[2][:, 0:128]], [unI, Tt])

            def f(v, Tt=Tt):
                return v.tensor_copy(Tt, ps[2][:, 0:128])
            S.op("vector", f, [Tt], [ps[2][:, 0:128]])
            if j < 6:
                def f(v, curT=curT):
                    return v.tensor_copy(curT, ps[4][:, 0:128])
                S.op("vector", f, [curT], [ps[4][:, 0:128]])

    s0_off = AFa.alloc(32)
    S0T = AFt[:, s0_off:s0_off + 32]

    def f(g):
        return g.memset(S0T, 0.0)
    S.op("gpsimd", f, [S0T], [])
    sb_off = ABa.alloc(4 * 128)
    S0bf = ABt[0:32, sb_off:sb_off + 128]
    S0Tbf = ABt[:, sb_off + 128:sb_off + 160]
    Rbf = ABt[:, sb_off + 160:sb_off + 192]
    Ybf = ABt[:, sb_off + 192:sb_off + 224]
    Ypr = ABt[:, sb_off + 224:sb_off + 256]
    YTbf = ABt[0:32, sb_off + 256:sb_off + 384]
    W2m = ABt[:, sb_off + 384:sb_off + 512]
    t1b = ABt[:, sb_off + 512:sb_off + 544]
    o1_off = AFa.alloc(256)
    o1f = AFt[:, o1_off:o1_off + 128]
    ostg = AFt[:, o1_off + 128:o1_off + 256]

    for ch in range(NCH):
        qch = qps[0:32, 128 * ch:128 * (ch + 1)]
        kfc = kfs[:, 128 * ch:128 * (ch + 1)]
        ktc = kts[:, 128 * ch:128 * (ch + 1)]
        vc = vbs[:, 32 * ch:32 * (ch + 1)]
        Tt = Tts[:, 128 * ch:128 * (ch + 1)]
        Gg = Ggs[:, 128 * ch:128 * (ch + 1)]

        def f(v):
            return v.tensor_copy(S0Tbf, S0T)
        S.op("vector", f, [S0Tbf], [S0T])
        def f(t):
            return t.transpose(ps[5][0:32, 0:128], S0T, p.identF[:, :])
        S.op("tensor", f, [ps[5][0:32, 0:128]], [S0T, p.identF])

        def f(v):
            return v.tensor_copy(S0bf, ps[5][0:32, 0:128])
        S.op("vector", f, [S0bf], [ps[5][0:32, 0:128]])

        def f(t, kfc=kfc):
            return t.matmul(ps[0][:, 0:32], kfc, S0Tbf, start=True, stop=True)
        S.op("tensor", f, [ps[0][:, 0:32]], [kfc, S0Tbf])

        def f(v, vc=vc, ch=ch):
            return v.tensor_scalar(t1b, vc, bcTs[:, ch:ch + 1], None, OP.mult)
        S.op("vector", f, [t1b], [vc, bcTs])

        def f(v, ch=ch):
            return v.scalar_tensor_tensor(Rbf, ps[0][:, 0:32],
                                          bal[:, ch:ch + 1], t1b,
                                          OP.mult, OP.add)
        S.op("vector", f, [Rbf], [ps[0], bal, t1b])

        def f(t, Tt=Tt):
            return t.matmul(ps[1][:, 0:32], Tt, Rbf, start=True, stop=True)
        S.op("tensor", f, [ps[1][:, 0:32]], [Tt, Rbf])

        def f(v):
            return v.tensor_copy(Ybf, ps[1][:, 0:32])
        S.op("vector", f, [Ybf], [ps[1]])

        def f(v, ch=ch):
            return v.tensor_scalar(Ypr, ps[1][:, 0:32], gaT[:, ch:ch + 1],
                                   None, OP.mult)
        S.op("vector", f, [Ypr], [ps[1], gaT])

        bf_t(YTbf, Ybf) if False else None
        def f(t):
            return t.transpose(psT[0][0:32, 0:128], Ybf, ident[:, :])
        S.op("tensor", f, [psT[0][0:32, 0:128]], [Ybf, ident])

        def f(v):
            return v.tensor_copy(YTbf, psT[0][0:32, 0:128])
        S.op("vector", f, [YTbf], [psT[0][0:32, 0:128]])

        def f(t, qch=qch):
            return t.matmul(ps[2][:, 0:128], YTbf, qch, start=True, stop=True)
        S.op("tensor", f, [ps[2][:, 0:128]], [YTbf, qch])

        def f(v, Gg=Gg):
            return v.tensor_tensor(W2m, ps[2][:, 0:128], Gg, OP.mult)
        S.op("vector", f, [W2m], [ps[2], Gg])

        def f(t, qch=qch):
            return t.matmul(ps[3][:, 0:128], qch, S0bf, start=True, stop=True)
        S.op("tensor", f, [ps[3][:, 0:128]], [qch, S0bf])

        def f(v, ch=ch):
            return v.tensor_scalar(o1f, ps[3][:, 0:128], alT[:, ch:ch + 1],
                                   None, OP.mult)
        S.op("vector", f, [o1f], [ps[3], alT])

        def f(t, ktc=ktc):
            return t.matmul(ps[4][:, 0:128], W2m, ktc, start=True, stop=True)
        S.op("tensor", f, [ps[4][:, 0:128]], [W2m, ktc])

        def f(v):
            return v.scalar_tensor_tensor(ostg, ps[4][:, 0:128], 1.0, o1f,
                                          OP.mult, OP.add)
        S.op("vector", f, [ostg], [ps[4], o1f])
        S.dma("sync", opo[128 * ch:128 * (ch + 1), :], ostg)

        def f(t, ktc=ktc):
            return t.matmul(ps[5][:, 0:32], ktc, Ypr, start=True, stop=True)
        S.op("tensor", f, [ps[5][:, 0:32]], [ktc, Ypr])

        def f(v, ch=ch):
            return v.scalar_tensor_tensor(S0T, S0T, aCb[:, ch:ch + 1],
                                          ps[5][:, 0:32], OP.mult, OP.add)
        S.op("vector", f, [S0T], [S0T, aCb, ps[5]])

        if dbg and ch == 0:
            dstg_off = AFa.alloc(6 * 128)
            dstg = AFt[:, dstg_off:dstg_off + 6 * 128]
            for di, src_ap in enumerate([nlaT[:, 0:64], la_all[0:64, 0:128],
                                         ex2, Ggs[:, 0:128],
                                         Tts[:, 0:128], ostg]):
                np_ = src_ap.shape[0]

                def f(v, di=di, src_ap=src_ap, np_=np_):
                    return v.tensor_copy(
                        dstg[0:np_, 128 * di:128 * di + src_ap.shape[1]],
                        src_ap)
                S.op("vector", f, [dstg[:, 128 * di:128 * (di + 1)]],
                     [src_ap])
            S.dma("sync", dbgo, dstg)

    prog_end(p, S)
    return p


def emit_ffn4(p, S, AFt, ABt, AFa, ABa, ps, psT, ident, ones, xres, stats,
              wgT, wuT, wdT):
    I = 2816
    IH = 1408
    wg_off = ABa.alloc(8 * IH)
    wgs = ABt[:, wg_off:wg_off + 8 * IH]
    wu_off = ABa.alloc(8 * IH)
    wus = ABt[:, wu_off:wu_off + 8 * IH]
    wd_off = ABa.alloc(11 * 1024)
    wds = ABt[:, wd_off:wd_off + 11 * 1024]
    lnf_off = AFa.alloc(1024)
    lnf = AFt[:, lnf_off:lnf_off + 1024]
    xq_off = ABa.alloc(1024)
    xq = ABt[:, xq_off:xq_off + 1024]
    xqT_off = ABa.alloc(8 * 128)
    xqT = ABt[:, xqT_off:xqT_off + 8 * 128]
    g_off = ABa.alloc(IH)
    gb = ABt[:, g_off:g_off + IH]
    u_off = ABa.alloc(IH)
    ub = ABt[:, u_off:u_off + IH]
    h_off = AFa.alloc(IH)
    hf = AFt[:, h_off:h_off + IH]
    hq_off = ABa.alloc(IH)
    hq = ABt[:, hq_off:hq_off + IH]
    hqT_off = ABa.alloc(11 * 128)
    hqT = ABt[:, hqT_off:hqT_off + 11 * 128]
    scr_off = ABa.alloc(1024)
    scr = ABt[:, scr_off:scr_off + 1024]

    CH = [512, 512, 384]
    for half in range(2):
        h0 = IH * half
        for dk_ in range(8):
            S.dma("gpsimd", wgs[:, IH * dk_:IH * (dk_ + 1)],
                  wgT[128 * dk_:128 * (dk_ + 1), h0:h0 + IH])
            S.dma("gpsimd", wus[:, IH * dk_:IH * (dk_ + 1)],
                  wuT[128 * dk_:128 * (dk_ + 1), h0:h0 + IH])
        for dk_ in range(11):
            S.dma("gpsimd", wds[:, 1024 * dk_:1024 * (dk_ + 1)],
                  wdT[128 * (11 * half + dk_):128 * (11 * half + dk_ + 1), :])
        for i in range(NT):
            ln_one_tile(p, S, xres(i), lnf, scr, stats)
            fp4_quant_tiles(p, S, AFa, ABa, lnf, xq, 128, 1024, "xq")
            for j in range(8):
                transpose_tile(S, psT[j % 2], ident,
                               xqT[:, 128 * j:128 * (j + 1)],
                               xq[:, 128 * j:128 * (j + 1)],
                               evac_eng=("scalar", "vector")[j % 2])
            c0 = 0
            for nc_i, w in enumerate(CH):
                pbg = ps[nc_i % 2]
                pbu = ps[2 + nc_i % 2]

                def ltx(dk_):
                    return xqT[:, 128 * dk_:128 * (dk_ + 1)]

                def rhg(dk_, c0=c0, w=w):
                    return wgs[:, IH * dk_ + c0:IH * dk_ + c0 + w]

                def rhu(dk_, c0=c0, w=w):
                    return wus[:, IH * dk_ + c0:IH * dk_ + c0 + w]
                mm_tiled(p, S, pbg[:, 0:w], ltx, rhg, 8)
                mm_tiled(p, S, pbu[:, 0:w], ltx, rhu, 8)

                def f(sc, pbg=pbg, c0=c0, w=w):
                    return sc.activation(gb[:, c0:c0 + w], pbg[:, 0:w],
                                         AF.Silu, bias=0.0, scale=1.0)
                S.op("scalar", f, [gb[:, c0:c0 + w]], [pbg])

                def f(v, pbu=pbu, c0=c0, w=w):
                    return v.tensor_copy(ub[:, c0:c0 + w], pbu[:, 0:w])
                S.op("vector", f, [ub[:, c0:c0 + w]], [pbu])

                def f(v, c0=c0, w=w):
                    return v.tensor_tensor(hf[:, c0:c0 + w], gb[:, c0:c0 + w],
                                           ub[:, c0:c0 + w], OP.mult)
                S.op("vector", f, [hf[:, c0:c0 + w]],
                     [gb[:, c0:c0 + w], ub[:, c0:c0 + w]])
                c0 += w
            fp4_quant_tiles(p, S, AFa, ABa, hf, hq, 128, IH, "hq")
            for j in range(11):
                transpose_tile(S, psT[j % 2], ident,
                               hqT[:, 128 * j:128 * (j + 1)],
                               hq[:, 128 * j:128 * (j + 1)],
                               evac_eng=("scalar", "vector")[j % 2])
            for ncc in range(2):
                pb = ps[4 + ncc]

                def lth(dk_):
                    return hqT[:, 128 * dk_:128 * (dk_ + 1)]

                def rhd(dk_, ncc=ncc):
                    return wds[:, 1024 * dk_ + 512 * ncc:
                               1024 * dk_ + 512 * (ncc + 1)]
                mm_tiled(p, S, pb[:, :], lth, rhd, 11)
                xs = xres(i)[:, 512 * ncc:512 * (ncc + 1)]

                def f(v, pb=pb, xs=xs):
                    return v.scalar_tensor_tensor(xs, pb[:, :], 1.0, xs,
                                                  OP.mult, OP.add)
                S.op("vector", f, [xs], [pb, xs])

    for off, n in [(wg_off, 8 * IH), (wu_off, 8 * IH), (wd_off, 11 * 1024),
                   (xq_off, 1024), (xqT_off, 8 * 128), (g_off, IH),
                   (u_off, IH), (hq_off, IH), (hqT_off, 11 * 128),
                   (scr_off, 1024)]:
        ABa.free(off, n)
    AFa.free(lnf_off, 1024)
    AFa.free(h_off, IH)


def build_launch5():
    p = Prog("l5")
    x4i = p.inp("x4", [TPC, D])
    opr = p.inp("oparts", [4 * TPC, 128])
    gti = p.inp("gate", [TPC, 128], BF16)
    dwo = p.inp("dWoT", [128, D])
    dbo = p.inp("dbo", [D])
    wgT = p.inp("wgqT", [D, 2816], BF16)
    wuT = p.inp("wuqT", [D, 2816], BF16)
    wdT = p.inp("wdqT", [2816, D], BF16)
    wif = p.inp("WiT_f", [D, 4384])
    wib = p.inp("WiT_b", [D, 4384])

    x6o = p.out("x6", [TPC, D])
    zxf = p.out("zx_f", [2336, TPC], BF16)
    zxb = p.out("zx_b", [2336, TPC], BF16)
    ztf = p.out("zt_f", [TPC, 2048], BF16)
    ztb = p.out("zt_b", [TPC, 2048], BF16)

    S, AFt, ABt, AFa, ABa, ps, psT, ident, ones = prog_start(p, 25500, 53000)

    xres_off = AFa.alloc(NT * 1024)
    xres = lambda i: AFt[:, xres_off + 1024 * i: xres_off + 1024 * (i + 1)]
    st_off = AFa.alloc(64)
    stats = AFt[:, st_off:st_off + 64]
    for i in range(NT):
        S.dma("sync", xres(i), x4i[128 * i:128 * (i + 1), :])

    wo_off = ABa.alloc(1024)
    wos = ABt[:, wo_off:wo_off + 1024]
    S.dma("gpsimd", wos, dwo[:, :])
    bo_off = ABa.alloc(1024)
    bos = ABt[:, bo_off:bo_off + 1024]
    S.dma("gpsimd", bos[0:1, :], dbo.unsqueeze(0))
    gt_off = ABa.alloc(128)
    gts = ABt[:, gt_off:gt_off + 128]
    oacc_off = AFa.alloc(2 * 128)
    og_off = ABa.alloc(128)
    ogs = ABt[:, og_off:og_off + 128]
    for i in range(NT):
        oa = AFt[:, oacc_off:oacc_off + 128]
        ot = AFt[:, oacc_off + 128:oacc_off + 256]
        for j in range(4):
            src = opr[TPC * j + 128 * i:TPC * j + 128 * (i + 1), :]
            dst = oa if j == 0 else ot
            S.dma("sync", dst, src)
            if j > 0:
                def f(v, oa=oa, ot=ot):
                    return v.tensor_tensor(oa, oa, ot, OP.add)
                S.op("vector", f, [oa], [oa, ot])
        S.dma("sync", gts, gti[128 * i:128 * (i + 1), :])

        def f(v, oa=oa):
            return v.tensor_tensor(ogs, oa, gts, OP.mult)
        S.op("vector", f, [ogs], [oa, gts])
        for ncc in range(2):
            pb = ps[ncc % 2]

            def lt(dk_):
                return ogs[:, 0:128]
            def f(t, pb=pb, ncc=ncc):
                return t.matmul(pb[:, :], ogs[:, 0:128],
                                wos[:, 512 * ncc:512 * (ncc + 1)],
                                start=True, stop=False)
            S.op("tensor", f, [pb[:, :]], [ogs, wos], inc=False)

            def f(t, pb=pb, ncc=ncc):
                return t.matmul(pb[:, :], ones[0:1, 0:128],
                                bos[0:1, 512 * ncc:512 * (ncc + 1)],
                                start=False, stop=True)
            S.op("tensor", f, [pb[:, :]], [ones, bos])
            xs = xres(i)[:, 512 * ncc:512 * (ncc + 1)]

            def f(v, pb=pb, xs=xs):
                return v.scalar_tensor_tensor(xs, pb[:, :], 1.0, xs,
                                              OP.mult, OP.add)
            S.op("vector", f, [xs], [pb, xs])
    ABa.free(wo_off, 1024)
    ABa.free(bo_off, 1024)
    ABa.free(gt_off, 128)
    ABa.free(og_off, 128)
    AFa.free(oacc_off, 2 * 128)

    emit_ffn4(p, S, AFt, ABt, AFa, ABa, ps, psT, ident, ones, xres, stats,
              wgT, wuT, wdT)
    for i in range(NT):
        S.dma("sync", x6o[128 * i:128 * (i + 1), :], xres(i))

    xln_off = ABa.alloc(2 * 1024)
    xlnT_off = ABa.alloc(8 * 2048)
    scr_off = ABa.alloc(1024)
    xlnT = ABt[:, xlnT_off:xlnT_off + 8 * 2048]
    scr = ABt[:, scr_off:scr_off + 1024]
    for i in range(NT):
        xln = ABt[:, xln_off + 1024 * (i % 2): xln_off + 1024 * (i % 2) + 1024]
        ln_one_tile(p, S, xres(i), xln, scr, stats)
        for j in range(8):
            src = xln[:, 128 * j:128 * (j + 1)]
            dst = xlnT[:, 2048 * j + 128 * i: 2048 * j + 128 * (i + 1)]
            transpose_tile(S, psT[(i * 8 + j) % 2], ident, dst, src,
                           evac_eng=("scalar", "vector")[j % 2])

    wi_off = ABa.alloc(8 * 2048)
    wis = ABt[:, wi_off:wi_off + 8 * 2048]
    stg_off = ABa.alloc(2 * 2048)

    for dire, (wdram, zxo, zto) in enumerate([(wif, zxf, ztf),
                                              (wib, zxb, ztb)]):
        for ck in range(4):
            for dk_ in range(8):
                S.dma("gpsimd", wis[:, 2048 * dk_ + 512 * 0:2048 * dk_ + 512],
                      wdram[128 * dk_:128 * (dk_ + 1), 512 * ck:512 * (ck + 1)]) \
                    if False else None
        for dk_ in range(8):
            S.dma("gpsimd", wis[:, 2048 * dk_:2048 * (dk_ + 1)],
                  wdram[128 * dk_:128 * (dk_ + 1), 0:2048])
        for i in range(NT):
            stg = ABt[:, stg_off + 2048 * (i % 2):stg_off + 2048 * (i % 2) + 2048]
            for ncc in range(4):
                pb = ps[ncc % 4]

                def lt(dk_, i=i):
                    return xlnT[:, 2048 * dk_ + 128 * i: 2048 * dk_ + 128 * (i + 1)]

                def rh(dk_, ncc=ncc):
                    return wis[:, 2048 * dk_ + 512 * ncc: 2048 * dk_ + 512 * (ncc + 1)]
                mm_tiled(p, S, pb[:, :], lt, rh, 8)
                evac(S, stg[:, 512 * ncc:512 * (ncc + 1)], pb[:, :],
                     eng=("scalar", "vector")[ncc % 2])
            S.dma("sync", zto[128 * i:128 * (i + 1), :], stg)
        for ct in range(19):
            m = 128 if ct < 18 else 32
            for dk_ in range(8):
                S.dma("gpsimd", wis[:, 2048 * dk_:2048 * dk_ + m],
                      wdram[128 * dk_:128 * (dk_ + 1),
                            2048 + 128 * ct:2048 + 128 * ct + m])
            for tc in range(4):
                pb = ps[tc % 4]

                def lt(dk_, m=m):
                    return wis[:, 2048 * dk_:2048 * dk_ + m]

                def rh(dk_, tc=tc):
                    return xlnT[:, 2048 * dk_ + 512 * tc: 2048 * dk_ + 512 * (tc + 1)]
                mm_tiled(p, S, pb[0:m, :], lt, rh, 8)
                stg = ABt[:, stg_off + 512 * (tc % 2):stg_off + 512 * (tc % 2) + 512]
                evac(S, stg[0:m, 0:512], pb[0:m, :],
                     eng=("scalar", "vector")[tc % 2])
                S.dma("sync", zxo[128 * ct:128 * ct + m,
                                  512 * tc:512 * (tc + 1)], stg[0:m, 0:512])

    prog_end(p, S)
    return p


def build_launch6a():
    p = Prog("l6a")
    xi = p.inp("xbc", [1152, T], BF16)
    cw = p.inp("convw", [1152, 4])
    cb = p.inp("convb", [1152, 1])
    yo = p.out("conv", [1152, T], BF16)

    S, AFt, ABt, AFa, ABa, ps, psT, ident, ones = prog_start(p, 4000, 40000)
    x_off = ABa.alloc(T + 8)
    acc_off = AFa.alloc(2048)
    cw_off = AFa.alloc(8)
    cws = AFt[:, cw_off:cw_off + 4]
    cbs = AFt[:, cw_off + 4:cw_off + 5]
    stg_off = ABa.alloc(2 * 2048)

    for ct in range(9):
        xs = ABt[:, x_off:x_off + T + 8]

        def f(g):
            return g.memset(xs[:, 0:8], 0.0)
        S.op("gpsimd", f, [xs[:, 0:8]], [])
        for j in range(4):
            S.dma("sync", xs[:, 8 + 2048 * j:8 + 2048 * (j + 1)],
                  xi[128 * ct:128 * (ct + 1), 2048 * j:2048 * (j + 1)])
        S.dma("gpsimd", cws, cw[128 * ct:128 * (ct + 1), :])
        S.dma("gpsimd", cbs, cb[128 * ct:128 * (ct + 1), :])
        for tc in range(4):
            a = AFt[:, acc_off:acc_off + 2048]
            base = 8 + 2048 * tc - 3

            def f(v, a=a, base=base):
                return v.tensor_scalar(a, xs[:, base:base + 2048],
                                       cws[:, 0:1], None, OP.mult)
            S.op("vector", f, [a], [xs, cws])
            for j in range(1, 4):
                def f(v, a=a, base=base, j=j):
                    return v.scalar_tensor_tensor(
                        a, xs[:, base + j:base + j + 2048], cws[:, j:j + 1],
                        a, OP.mult, OP.add)
                S.op("vector", f, [a], [xs, cws, a])
            stg = ABt[:, stg_off + 2048 * (tc % 2):stg_off + 2048 * (tc % 2) + 2048]

            def f(sc, a=a, stg=stg):
                return sc.activation(stg, a, AF.Silu, bias=cbs[:, 0:1],
                                     scale=1.0)
            S.op("scalar", f, [stg], [a, cbs])
            S.dma("sync", yo[128 * ct:128 * (ct + 1),
                             2048 * tc:2048 * (tc + 1)], stg)
    prog_end(p, S)
    return p


def build_launch6b():
    p = Prog("l6b")
    NCH = 64
    Xt = p.inp("Xtm", [T, 1024], BF16)
    Bf = p.inp("Bfm", [128, T], BF16)
    Bt = p.inp("Btm", [T, 128], BF16)
    Cf = p.inp("Cfm", [128, T], BF16)
    dtr = p.inp("dtraw", [16, T], BF16)
    hpar = p.inp("hpar", [16, 4])
    DpE = p.inp("DpE", [1, 1024])
    Yo = p.out("Ytm", [T, 1024], BF16)

    S, AFt, ABt, AFa, ABa, ps, psT, ident, ones = prog_start(p, 24500, 48500)
    nc = p.nc
    lfd = nc.dram_tensor("lflat", [NCH, 2048], F32).ap()

    Bf_off = ABa.alloc(T)
    Bfs = ABt[:, Bf_off:Bf_off + T]
    Bt_off = ABa.alloc(T)
    Bts = ABt[:, Bt_off:Bt_off + T]
    Cf_off = ABa.alloc(T)
    Cfs = ABt[:, Cf_off:Cf_off + T]
    dt_off = ABa.alloc(T)
    dts = ABt[0:16, dt_off:dt_off + T]
    for j in range(4):
        S.dma("sync", Bfs[:, 2048 * j:2048 * (j + 1)], Bf[:, 2048 * j:2048 * (j + 1)])
        S.dma("sync", Cfs[:, 2048 * j:2048 * (j + 1)], Cf[:, 2048 * j:2048 * (j + 1)])
        S.dma("sync",
              Bts[:, 2048 * j:2048 * (j + 1)].rearrange("p (a b) -> p a b", b=128),
              Bt[2048 * j:2048 * (j + 1), :].rearrange("(a p) b -> p a b", p=128))
    dtr_off = ABa.alloc(T)
    dtrs = ABt[0:16, dtr_off:dtr_off + T]
    S.dma("sync", dtrs, dtr[:, :])
    hp_off = AFa.alloc(64)
    hps = AFt[0:16, hp_off:hp_off + 4]
    S.dma("sync", hps, hpar[:, :])
    dpe_off = AFa.alloc(1024)
    dpes = AFt[0:1, dpe_off:dpe_off + 1024]
    S.dma("sync", dpes, DpE[:, :])

    pc_off = AFa.alloc(8)
    dtb = AFt[0:16, pc_off:pc_off + 1]
    negA = AFt[0:16, pc_off + 1:pc_off + 2]

    def t16(dst, src, n_in, n_out, base=0):
        def f(t):
            return t.transpose(ps[5][0:n_out, 0:n_in], src,
                               p.identF[base:base + n_in, base:base + n_in])
        S.op("tensor", f, [ps[5][0:n_out, 0:n_in]], [src, p.identF[:, :]])

        def f(v):
            return v.tensor_copy(dst, ps[5][0:n_out, 0:n_in])
        S.op("vector", f, [dst], [ps[5][0:n_out, 0:n_in]])

    def f(v):
        return v.tensor_copy(dtb, hps[0:16, 0:1])
    S.op("vector", f, [dtb], [hps])

    def f(sc):
        return sc.activation(negA, hps[0:16, 1:2], AF.Exp, bias=0.0, scale=1.0)
    S.op("scalar", f, [negA], [hps])

    def f(v):
        return v.tensor_scalar(negA, negA, -1.0, None, OP.mult)
    S.op("vector", f, [negA], [negA])

    la_off = AFa.alloc(T + 4096)
    lall = AFt[0:16, la_off:la_off + T]
    lnd = AFt[32:48, la_off:la_off + T]
    wk1 = AFt[0:16, la_off + T:la_off + T + 2048]
    wk2 = AFt[0:16, la_off + T + 2048:la_off + T + 4096]
    for j in range(4):
        sl = slice(2048 * j, 2048 * (j + 1))

        def f(v, sl=sl):
            return v.tensor_scalar(wk1[:, 0:2048], dtrs[:, sl], dtb[:, 0:1],
                                   None, OP.add)
        S.op("vector", f, [wk1], [dtrs, dtb])

        def f(v, sl=sl):
            return v.scalar_tensor_tensor(lall[:, sl], wk1[:, 0:2048], -1.0,
                                          wk1[:, 0:2048], OP.mult, OP.max)
        S.op("vector", f, [lall[:, sl]], [wk1])

        def f(sc, sl=sl):
            return sc.activation(lall[:, sl], lall[:, sl], AF.Exp, bias=0.0,
                                 scale=-1.0)
        S.op("scalar", f, [lall[:, sl]], [lall[:, sl]])

        def f(sc, sl=sl):
            return sc.activation(lall[:, sl], lall[:, sl], AF.Ln, bias=1.0,
                                 scale=1.0)
        S.op("scalar", f, [lall[:, sl]], [lall[:, sl]])

        def f(v, sl=sl):
            return v.scalar_tensor_tensor(wk2[:, 0:2048], wk1[:, 0:2048], 0.0,
                                          lall[:, sl], OP.max, OP.add)
        S.op("vector", f, [wk2], [wk1, lall[:, sl]])
        def f(v, sl=sl):
            return v.tensor_copy(dts[:, sl], wk2[:, 0:2048])
        S.op("vector", f, [dts[:, sl]], [wk2])

        def f(sc, sl=sl):
            return sc.activation(lnd[:, sl], wk2[:, 0:2048], AF.Ln, bias=0.0,
                                 scale=1.0)
        S.op("scalar", f, [lnd[:, sl]], [wk2])

        def f(v, sl=sl):
            return v.tensor_scalar(lall[:, sl], wk2[:, 0:2048], negA[:, 0:1],
                                   None, OP.mult)
        S.op("vector", f, [lall[:, sl]], [wk2, negA])

    for ch in range(NCH):
        sl = slice(128 * ch, 128 * (ch + 1))

        def f(v, sl=sl):
            return v.tensor_tensor_scan(lall[:, sl], lall[:, sl], lall[:, sl],
                                        0.0, OP.add, OP.bypass)
        S.op("vector", f, [lall[:, sl]], [lall[:, sl]])

    ABa.free(dtr_off, T)
    S.dma("sync",
          lfd.rearrange("c (h t) -> h c t", h=16),
          lall.rearrange("p (c t) -> p c t", t=128))

    mu_off = ABa.alloc(128)
    p.maskU = ABt[:, mu_off:mu_off + 128]
    io_off = AFa.alloc(132)
    io1 = AFt[:, io_off:io_off + 128]
    io2 = AFt[:, io_off + 128:io_off + 129]

    def f(g):
        return g.iota(io1, [[1, 128]], channel_multiplier=0,
                      allow_small_or_imprecise_dtypes=True)
    S.op("gpsimd", f, [io1], [])

    def f(g):
        return g.iota(io2, [[0, 1]], channel_multiplier=1,
                      allow_small_or_imprecise_dtypes=True)
    S.op("gpsimd", f, [io2], [])

    def f(v):
        return v.scalar_tensor_tensor(p.maskU, io1, 0.0,
                                      io2.broadcast_to((128, 128)),
                                      OP.add, OP.is_ge)
    S.op("vector", f, [p.maskU], [io1, io2])

    oF_off = AFa.alloc(128)
    onesF = AFt[:, oF_off:oF_off + 128]

    def f(g):
        return g.memset(onesF[0:1, :], 1.0)
    S.op("gpsimd", f, [onesF[0:1, :]], [])
    dpx_off = AFa.alloc(1024)
    dpx = AFt[:, dpx_off:dpx_off + 1024]
    for half in range(2):
        def f(t, half=half):
            return t.matmul(ps[0][:, 0:512], onesF[0:1, 0:128],
                            dpes[0:1, 512 * half:512 * (half + 1)],
                            start=True, stop=True)
        S.op("tensor", f, [ps[0][:, 0:512]], [onesF, dpes])

        def f(v, half=half):
            return v.tensor_copy(dpx[:, 512 * half:512 * (half + 1)],
                                 ps[0][:, 0:512])
        S.op("vector", f, [dpx[:, 512 * half:512 * (half + 1)]], [ps[0]])

    st_off = AFa.alloc(1024)
    stf = AFt[:, st_off:st_off + 1024]

    def f(g):
        return g.memset(stf, 0.0)
    S.op("gpsimd", f, [stf], [])
    stb_off = ABa.alloc(1024)
    stb = ABt[:, stb_off:stb_off + 1024]

    def f(v):
        return v.tensor_copy(stb, stf)
    S.op("vector", f, [stb], [stf])

    X_off = ABa.alloc(2 * 1024)
    lf_off = AFa.alloc(2 * 2048)
    att_off = ABa.alloc(128)
    atts = ABt[:, att_off:att_off + 128]
    Dall_off = ABa.alloc(2048)
    Dall = ABt[:, Dall_off:Dall_off + 2048]
    ltc_off = AFa.alloc(16 + 16 + 16 + 16)
    lTc = AFt[:, ltc_off:ltc_off + 16]
    lbTc = AFt[:, ltc_off + 16:ltc_off + 32]
    e1T = AFt[:, ltc_off + 32:ltc_off + 48]
    e2T = AFt[:, ltc_off + 48:ltc_off + 64]
    lCr_off = AFa.alloc(20)
    lCc = AFt[0:16, lCr_off:lCr_off + 1]
    lCr = AFt[0:1, lCr_off + 2:lCr_off + 18]
    ecf_off = AFa.alloc(1044)
    ecf = AFt[0:1, ecf_off:ecf_off + 1024]
    ecsrc = AFt[0:1, ecf_off + 1024:ecf_off + 1040]
    E_off = AFa.alloc(2048)
    Ef = AFt[:, E_off:E_off + 2048]
    t1_off = ABa.alloc(3 * 1024)
    t1b = ABt[:, t1_off:t1_off + 1024]
    t3b = ABt[:, t1_off + 1024:t1_off + 2048]
    Xpp = ABt[:, t1_off + 2048:t1_off + 3072]
    ecx_off = AFa.alloc(1024)
    ecx = AFt[:, ecx_off:ecx_off + 1024]
    ystg_off = ABa.alloc(2 * 1024)

    for ch in range(NCH):
        sl = slice(128 * ch, 128 * (ch + 1))
        Xc = ABt[:, X_off + 1024 * (ch % 2):X_off + 1024 * (ch % 2) + 1024]
        S.dma("scalar", Xc.rearrange("p b -> p b"),
              Xt[sl, :])
        lfr = AFt[0:1, lf_off + 2048 * (ch % 2):lf_off + 2048 * (ch % 2) + 2048]
        S.dma("sync", lfr, lfd[ch:ch + 1, :])

        def f(t, sl=sl):
            return t.matmul(ps[0][:, 0:128], Bfs[:, sl], Cfs[:, sl],
                            start=True, stop=True)
        S.op("tensor", f, [ps[0][:, 0:128]], [Bfs[:, sl], Cfs[:, sl]])

        def f(v):
            return v.tensor_tensor(atts[:, 0:128], ps[0][:, 0:128],
                                   p.maskU[:, :], OP.mult)
        S.op("vector", f, [atts[:, 0:128]], [ps[0], p.maskU[:, :]])

        t16(lTc, lall[:, sl], 16, 128)
        t16(lbTc, lnd[:, sl], 16, 128, base=32)

        def f(v):
            return v.tensor_tensor(lbTc, lbTc, lTc, OP.subtract)
        S.op("vector", f, [lbTc], [lbTc, lTc])

        for half in range(4):
            def f(t, half=half, lfr=lfr):
                return t.matmul(ps[1 + half % 2][:, 0:512],
                                onesF[0:1, 0:128],
                                lfr[0:1, 512 * half:512 * (half + 1)],
                                start=True, stop=True)
            S.op("tensor", f, [ps[1 + half % 2][:, 0:512]], [onesF, lfr])

            def f(v, half=half):
                return v.tensor_copy(Ef[:, 512 * half:512 * (half + 1)],
                                     ps[1 + half % 2][:, 0:512])
            S.op("vector", f, [Ef[:, 512 * half:512 * (half + 1)]],
                 [ps[1 + half % 2]])

        lbe = lbTc.unsqueeze(2).broadcast_to((128, 16, 128))
        E3 = Ef.rearrange("p (h t) -> p h t", h=16)

        def f(v):
            return v.tensor_tensor(E3, E3, lbe, OP.add)
        S.op("vector", f, [Ef], [Ef, lbTc])

        def f(v):
            return v.tensor_scalar(Ef, Ef, 80.0, None, OP.min)
        S.op("vector", f, [Ef], [Ef])

        def f(sc):
            return sc.activation(Dall, Ef, AF.Exp, bias=0.0, scale=1.0)
        S.op("scalar", f, [Dall], [Ef])

        ae = atts[:, 0:128].unsqueeze(1).broadcast_to((128, 16, 128))
        D3 = Dall.rearrange("p (h t) -> p h t", h=16)

        def f(v):
            return v.tensor_tensor(D3, D3, ae, OP.mult)
        S.op("vector", f, [Dall], [Dall, atts])

        for h in range(16):
            pb = ps[2 + h // 8]

            def f(t, h=h, pb=pb, Xc=Xc):
                return t.matmul(pb[:, 64 * (h % 8):64 * (h % 8) + 64],
                                Dall[:, 128 * h:128 * (h + 1)],
                                Xc[:, 64 * h:64 * (h + 1)],
                                start=True, stop=True)
            S.op("tensor", f, [pb[:, 64 * (h % 8):64 * (h % 8) + 64]],
                 [Dall, Xc], inc=(h % 8 == 7))

        for half in range(2):
            pb = ps[4 + half]

            def f(t, half=half, pb=pb, sl=sl):
                return t.matmul(pb[:, :], Cfs[:, sl],
                                stb[:, 512 * half:512 * (half + 1)],
                                start=True, stop=True)
            S.op("tensor", f, [pb[:, :]], [Cfs[:, sl], stb])

        def f(sc):
            return sc.activation(e2T, lTc, AF.Exp, bias=0.0, scale=1.0)
        S.op("scalar", f, [e2T], [lTc])

        ystg = ABt[:, ystg_off + 1024 * (ch % 2):ystg_off + 1024 * (ch % 2) + 1024]
        e2e = e2T.unsqueeze(2).broadcast_to((128, 16, 64))
        t13 = t1b.rearrange("p (h q) -> p h q", h=16)

        Xc3 = Xc.rearrange("p (h q) -> p h q", h=16)

        def f(v, Xc3=Xc3):
            return v.tensor_tensor(t13, Xc3,
                                   dpx.rearrange("p (h q) -> p h q", h=16),
                                   OP.mult)
        S.op("vector", f, [t1b], [Xc, dpx])
        for half in range(2):
            hsl = slice(512 * half, 512 * (half + 1))

            def f(v, half=half, hsl=hsl):
                return v.scalar_tensor_tensor(
                    t3b[:, hsl], ps[2 + half][:, :], 1.0, t1b[:, hsl],
                    OP.mult, OP.add)
            S.op("vector", f, [t3b[:, hsl]], [ps[2 + half], t1b])
        for half in range(2):
            hsl = slice(512 * half, 512 * (half + 1))
            y3h = ystg[:, hsl].rearrange("p (h q) -> p h q", h=8)
            e2h = e2T[:, 8 * half:8 * (half + 1)].unsqueeze(2).broadcast_to(
                (128, 8, 64))

            def f(v, half=half, y3h=y3h, e2h=e2h):
                return v.tensor_tensor(y3h, ps[4 + half][:, :].rearrange(
                    "p (h q) -> p h q", h=8), e2h, OP.mult)
            S.op("vector", f, [ystg[:, hsl]], [ps[4 + half], e2T])

            def f(v, hsl=hsl, ystg=ystg):
                return v.tensor_tensor(ystg[:, hsl], ystg[:, hsl],
                                       t3b[:, hsl], OP.add)
            S.op("vector", f, [ystg[:, hsl]], [ystg[:, hsl], t3b[:, hsl]])
        S.dma("sync", Yo[sl, :], ystg)

        def f(v, ch=ch):
            return v.tensor_copy(lCc, lall[0:16, 128 * ch + 127:128 * (ch + 1)])
        S.op("vector", f, [lCc], [lall])
        t16(lCr, lCc, 16, 1)

        def f(t):
            return t.matmul(ps[0][:, 0:16], onesF[0:1, 0:128], lCr,
                            start=True, stop=True)
        S.op("tensor", f, [ps[0][:, 0:16]], [onesF, lCr])

        def f(v):
            return v.scalar_tensor_tensor(e1T, lbTc, 1.0, ps[0][:, 0:16],
                                          OP.mult, OP.add)
        S.op("vector", f, [e1T], [lbTc, ps[0]])

        def f(sc):
            return sc.activation(e1T, e1T, AF.Exp, bias=0.0, scale=1.0)
        S.op("scalar", f, [e1T], [e1T])

        e1e = e1T.unsqueeze(2).broadcast_to((128, 16, 64))
        X3 = Xpp.rearrange("p (h q) -> p h q", h=16)

        def f(v, Xc3=Xc3):
            return v.tensor_tensor(X3, Xc3, e1e, OP.mult)
        S.op("vector", f, [Xpp], [Xc, e1T])

        for half in range(2):
            pb = ps[half]

            def f(t, half=half, pb=pb, sl=sl):
                return t.matmul(pb[:, :], Bts[:, sl],
                                Xpp[:, 512 * half:512 * (half + 1)],
                                start=True, stop=True)
            S.op("tensor", f, [pb[:, :]], [Bts[:, sl], Xpp])

        def f(v):
            return v.tensor_copy(ecsrc[0:1, 0:16], lCr)
        S.op("vector", f, [ecsrc[0:1, 0:16]], [lCr])
        ecs = ecsrc[0:1, 0:16].unsqueeze(2).broadcast_to((1, 16, 64))

        def f(v):
            return v.tensor_copy(
                ecf[0:1, :].rearrange("p (h q) -> p h q", h=16), ecs)
        S.op("vector", f, [ecf], [ecsrc])

        def f(sc):
            return sc.activation(ecf[0:1, :], ecf[0:1, :], AF.Exp, bias=0.0,
                                 scale=1.0)
        S.op("scalar", f, [ecf], [ecf])
        for half in range(2):
            def f(t, half=half):
                return t.matmul(ps[6][:, 0:512] if False else
                                ps[2 + half][:, :], onesF[0:1, 0:128],
                                ecf[0:1, 512 * half:512 * (half + 1)],
                                start=True, stop=True)
            S.op("tensor", f, [ps[2 + half][:, :]], [onesF, ecf])

            def f(v, half=half):
                return v.tensor_copy(ecx[:, 512 * half:512 * (half + 1)],
                                     ps[2 + half][:, :])
            S.op("vector", f, [ecx[:, 512 * half:512 * (half + 1)]],
                 [ps[2 + half]])
        for half in range(2):
            hsl = slice(512 * half, 512 * (half + 1))

            def f(v, hsl=hsl):
                return v.tensor_tensor(stf[:, hsl], stf[:, hsl], ecx[:, hsl],
                                       OP.mult)
            S.op("vector", f, [stf[:, hsl]], [stf[:, hsl], ecx[:, hsl]])

            def f(v, half=half, hsl=hsl):
                return v.scalar_tensor_tensor(stf[:, hsl], ps[half][:, :],
                                              1.0, stf[:, hsl], OP.mult,
                                              OP.add)
            S.op("vector", f, [stf[:, hsl]], [ps[half], stf[:, hsl]])

        def f(v):
            return v.tensor_copy(stb, stf)
        S.op("vector", f, [stb], [stf])

    prog_end(p, S)
    return p


def build_launch7():
    p = Prog("l7")
    Yf = p.inp("Yf", [TPC, 2048], BF16)
    Yb = p.inp("Yb", [TPC, 2048], BF16)
    zf = p.inp("zf", [TPC, 2048], BF16)
    zb = p.inp("zb", [TPC, 2048], BF16)
    nw = p.inp("nw", [2, 2048])
    of = p.out("YNf", [TPC, 2048], BF16)
    ob = p.out("YNb", [TPC, 2048], BF16)

    S, AFt, ABt, AFa, ABa, ps, psT, ident, ones = prog_start(p, 16000, 30000)
    oF_off = AFa.alloc(128)
    onesF = AFt[:, oF_off:oF_off + 128]

    def f(g):
        return g.memset(onesF[0:1, :], 1.0)
    S.op("gpsimd", f, [onesF[0:1, :]], [])
    nw_off = AFa.alloc(4 * 2048)
    nrow = [AFt[0:1, nw_off:nw_off + 2048],
            AFt[0:1, nw_off + 2048:nw_off + 2 * 2048]]
    S.dma("sync", nrow[0], nw[0:1, :])
    S.dma("sync", nrow[1], nw[1:2, :])
    nwE = [AFt[:, nw_off + 2 * 2048:nw_off + 3 * 2048],
           AFt[:, nw_off + 3 * 2048:nw_off + 4 * 2048]]
    for d in range(2):
        for hf in range(4):
            def f(t, d=d, hf=hf):
                return t.matmul(ps[0][:, 0:512], onesF[0:1, 0:128],
                                nrow[d][0:1, 512 * hf:512 * (hf + 1)],
                                start=True, stop=True)
            S.op("tensor", f, [ps[0][:, 0:512]], [onesF, nrow[d]])

            def f(v, d=d, hf=hf):
                return v.tensor_copy(nwE[d][:, 512 * hf:512 * (hf + 1)],
                                     ps[0][:, 0:512])
            S.op("vector", f, [nwE[d][:, 512 * hf:512 * (hf + 1)]], [ps[0]])

    yt_off = ABa.alloc(2 * 2048)
    zt_off = ABa.alloc(2 * 2048)
    sz_off = ABa.alloc(2048)
    szs = ABt[:, sz_off:sz_off + 2048]
    tf_off = AFa.alloc(2048)
    tfs = AFt[:, tf_off:tf_off + 2048]
    st_off = AFa.alloc(8)
    stats = AFt[:, st_off:st_off + 8]
    stg_off = ABa.alloc(2 * 2048)
    scr_off = ABa.alloc(2048)
    scrb = ABt[:, scr_off:scr_off + 2048]

    for d, (Yi, zi, oo) in enumerate([(Yf, zf, of), (Yb, zb, ob)]):
        for i in range(NT):
            yts = ABt[:, yt_off + 2048 * (i % 2):yt_off + 2048 * (i % 2) + 2048]
            zts = ABt[:, zt_off + 2048 * (i % 2):zt_off + 2048 * (i % 2) + 2048]
            S.dma("sync", yts, Yi[128 * i:128 * (i + 1), :])
            S.dma("scalar", zts, zi[128 * i:128 * (i + 1), :])

            def f(sc, zts=zts):
                return sc.activation(szs, zts, AF.Silu, bias=0.0, scale=1.0)
            S.op("scalar", f, [szs], [zts])

            def f(v, yts=yts):
                return v.tensor_tensor(tfs, yts, szs, OP.mult)
            S.op("vector", f, [tfs], [yts, szs])

            def f(v):
                return v.scalar_tensor_tensor(scrb, tfs, 1.0, tfs, OP.mult,
                                              OP.mult, accum_out=stats[:, 0:1])
            S.op("vector", f, [scrb, stats[:, 0:1]], [tfs])

            def f(v):
                return v.tensor_scalar(stats[:, 1:2], stats[:, 0:1],
                                       1.0 / 2048.0, 1e-5, OP.mult, OP.add)
            S.op("vector", f, [stats[:, 1:2]], [stats[:, 0:1]])

            def f(sc):
                return sc.activation(stats[:, 2:3], stats[:, 1:2], AF.Ln,
                                     bias=0.0, scale=1.0)
            S.op("scalar", f, [stats[:, 2:3]], [stats[:, 1:2]])

            def f(sc):
                return sc.activation(stats[:, 3:4], stats[:, 2:3], AF.Exp,
                                     bias=0.0, scale=-0.5)
            S.op("scalar", f, [stats[:, 3:4]], [stats[:, 2:3]])

            stg = ABt[:, stg_off + 2048 * (i % 2):stg_off + 2048 * (i % 2) + 2048]

            def f(v, d=d):
                return v.tensor_scalar(tfs, tfs, stats[:, 3:4], None, OP.mult)
            S.op("vector", f, [tfs], [tfs, stats])

            def f(v, d=d, stg=stg):
                return v.tensor_tensor(stg, tfs, nwE[d], OP.mult)
            S.op("vector", f, [stg], [tfs, nwE[d]])
            S.dma("sync", oo[128 * i:128 * (i + 1), :], stg)

    prog_end(p, S)
    return p


def build_launch8():
    p = Prog("l8")
    x6i = p.inp("x6", [TPC, D])
    ynf = p.inp("YNfT", [2048, TPC], BF16)
    ynb = p.inp("YNbT", [2048, TPC], BF16)
    wof = p.inp("WoT_f", [2048, D])
    wob = p.inp("WoT_b", [2048, D])
    wgT = p.inp("wgqT", [D, 2816], BF16)
    wuT = p.inp("wuqT", [D, 2816], BF16)
    wdT = p.inp("wdqT", [2816, D], BF16)
    x8o = p.out("x8", [TPC, D])

    S, AFt, ABt, AFa, ABa, ps, psT, ident, ones = prog_start(p, 25500, 53000)
    xres_off = AFa.alloc(NT * 1024)
    xres = lambda i: AFt[:, xres_off + 1024 * i: xres_off + 1024 * (i + 1)]
    st_off = AFa.alloc(64)
    stats = AFt[:, st_off:st_off + 64]
    for i in range(NT):
        S.dma("sync", xres(i), x6i[128 * i:128 * (i + 1), :])

    yn_off = ABa.alloc(16 * 2048)
    yns = ABt[:, yn_off:yn_off + 16 * 2048]
    wo_off = ABa.alloc(16 * 1024)
    wos = ABt[:, wo_off:wo_off + 16 * 1024]
    for dk_ in range(16):
        S.dma("gpsimd", yns[:, 2048 * dk_:2048 * (dk_ + 1)],
              ynf[128 * dk_:128 * (dk_ + 1), :])
        S.dma("gpsimd", wos[:, 1024 * dk_:1024 * (dk_ + 1)],
              wof[128 * dk_:128 * (dk_ + 1), :])
    for half in range(2):
        if half == 1:
            for dk_ in range(16):
                S.dma("gpsimd", yns[:, 2048 * dk_:2048 * (dk_ + 1)],
                      ynb[128 * dk_:128 * (dk_ + 1), :])
                S.dma("gpsimd", wos[:, 1024 * dk_:1024 * (dk_ + 1)],
                      wob[128 * dk_:128 * (dk_ + 1), :])
        for i in range(NT):
            for ncc in range(2):
                pb = ps[(i * 2 + ncc) % 4]

                def lt(dk_, i=i):
                    return yns[:, 2048 * dk_ + 128 * i:2048 * dk_ + 128 * (i + 1)]

                def rh(dk_, ncc=ncc):
                    return wos[:, 1024 * dk_ + 512 * ncc:1024 * dk_ + 512 * (ncc + 1)]
                mm_tiled(p, S, pb[:, :], lt, rh, 16)
                xs = xres(i)[:, 512 * ncc:512 * (ncc + 1)]

                def f(v, pb=pb, xs=xs):
                    return v.scalar_tensor_tensor(xs, pb[:, :], 0.5, xs,
                                                  OP.mult, OP.add)
                S.op("vector", f, [xs], [pb, xs])
    ABa.free(yn_off, 16 * 2048)
    ABa.free(wo_off, 16 * 1024)

    emit_ffn4(p, S, AFt, ABt, AFa, ABa, ps, psT, ident, ones, xres, stats,
              wgT, wuT, wdT)
    for i in range(NT):
        S.dma("sync", x8o[128 * i:128 * (i + 1), :], xres(i))
    prog_end(p, S)
    return p


def build_launch2v2(pool_add=True):
    p = Prog("l2b")
    qf = p.inp("qfm", [128, 2 * T], BF16)
    kf = p.inp("kfm", [128, 2 * T], BF16)
    vt = p.inp("vtm", [T, 256], BF16)
    yo = p.out("yfm", [128, 2 * T], BF16)

    S, AFt, ABt, AFa, ABa, ps, psT, ident, ones = prog_start(p, 2000, 2000,
                                                             n_ps=0, n_psT=0)
    ps4 = [p.ps(f"ps4{i}", [128, 1024], F32) for i in range(3)]
    psY = [p.ps("psYa", [128, 512], F32), p.ps("psYb", [128, 512], F32)]
    S.same_engine_sync = False
    reg_consts(p, S, [1e12])

    qfs = p.sb("qfs", [128, 2 * T], BF16)[:, :]
    kfs = p.sb("kfs", [128, 2 * T], BF16)[:, :]
    vts = p.sb("vts", [128, 64 * 256], BF16)[:, :]
    for j in range(2):
        S.dma("sync", qfs[:, T * j:T * (j + 1)], qf[:, T * j:T * (j + 1)])
        S.dma("sync", kfs[:, T * j:T * (j + 1)], kf[:, T * j:T * (j + 1)])
    for i in range(8):
        S.dma("scalar", vts[:, 2048 * i:2048 * (i + 1)].rearrange(
            "p (a b) -> p a b", b=256),
            vt[1024 * i:1024 * (i + 1), :].rearrange("(a p) b -> p a b", p=128))

    mask4 = p.sb("mask4", [128, 2048], BF16)[:, :]
    io1 = AFt[:, 0:512]
    io2 = AFt[:, 512:513]

    def f(g):
        return g.iota(io1[:, :], [[1, 512]], channel_multiplier=0,
                      allow_small_or_imprecise_dtypes=True)
    S.op("gpsimd", f, [io1], [])

    def f(g):
        return g.iota(io2[:, 0:1], [[0, 1]], channel_multiplier=1,
                      allow_small_or_imprecise_dtypes=True)
    S.op("gpsimd", f, [io2[:, 0:1]], [])
    for j in range(4):
        mj = mask4[:, 512 * j:512 * (j + 1)]

        def f(v, mj=mj, j=j):
            return v.scalar_tensor_tensor(mj, io1[:, :], -128.0 * j,
                                          io2[:, 0:1].broadcast_to((128, 512)),
                                          OP.add, OP.is_ge)
        S.op("vector", f, [mj], [io1, io2])

    nb = 3
    W = 1024
    sbt_ = p.sb("sbt", [128, nb * W], BF16)
    utt_ = p.sb("utt", [128, nb * W], BF16)
    wct_ = p.sb("wct", [128, nb * W], BF16)
    r3t_ = p.sb("r3t", [128, nb * W], BF16)
    Ptt_ = p.sb("Ptt", [128, nb * W], BF16)
    att_ = p.sb("attt", [128, nb * W], BF16)
    stg_ = p.sb("stgt", [128, 2 * 512], BF16)

    groups = []
    for h in range(4):
        for tqc in range(16):
            ng = 2 * (tqc + 1)
            for g in range(ng):
                groups.append((h, tqc, g, ng))

    def emit_qk(idx):
        h, tqc, g, ng = groups[idx]
        pb_, cb = 64 * (h % 2), T * (h // 2)
        pq = ps4[idx % 3]
        for j4 in range(2):
            st = 2 * g + j4

            def f(t, st=st, tqc=tqc, pq=pq, j4=j4, pb_=pb_, cb=cb):
                return t.matmul(
                    pq[:, 512 * j4:512 * (j4 + 1)],
                    kfs[pb_:pb_ + 64, cb + 128 * st:cb + 128 * (st + 1)],
                    qfs[pb_:pb_ + 64, cb + 512 * tqc:cb + 512 * (tqc + 1)],
                    start=True, stop=True)
            S.op("tensor", f, [pq[:, 512 * j4:512 * (j4 + 1)]],
                 [kfs[pb_:pb_ + 64, cb + 128 * st:cb + 128 * (st + 1)],
                  qfs[pb_:pb_ + 64, cb + 512 * tqc:cb + 512 * (tqc + 1)]],
                 inc=(j4 == 1))

    def emit_elem(idx):
        h, tqc, g, ng = groups[idx]
        r = idx % nb
        pq = ps4[idx % 3]
        sb = sbt_[:, W * r:W * (r + 1)]
        ut = utt_[:, W * r:W * (r + 1)]
        wc = wct_[:, W * r:W * (r + 1)]
        r3 = r3t_[:, W * r:W * (r + 1)]
        Pt = Ptt_[:, W * r:W * (r + 1)]
        at = att_[:, W * r:W * (r + 1)]

        def f(sc, pq=pq, sb=sb):
            return sc.copy(sb, pq[:, :])
        S.op("scalar", f, [sb], [pq])

        def f(sc, pq=pq, wc=wc):
            return sc.activation(wc, pq[:, :], AF.Relu, bias=1e12, scale=1e10)
        S.op("scalar", f, [wc], [pq, p.c_consts[:, :]])

        def f(v, sb=sb, ut=ut):
            return v.tensor_scalar(ut, sb, -1e-5, 1e-3, OP.mult, OP.add)
        S.op("vector", f, [ut], [sb])

        def f(v, sb=sb, r3=r3):
            return v.tensor_scalar(r3, sb, 0.0, 1e-3, OP.max, OP.mult)
        S.op("vector", f, [r3], [sb])

        def f(v, sb=sb, r3=r3, Pt=Pt):
            return v.scalar_tensor_tensor(Pt, sb, 0.01, r3, OP.add, OP.mult)
        S.op("vector", f, [Pt], [sb, r3])

        def f(v, ut=ut, wc=wc):
            return v.tensor_tensor(ut, ut, wc, OP.min)
        S.op("vector", f, [ut], [ut, wc])

        def f(e, Pt=Pt, ut=ut, at=at):
            return e.tensor_tensor(at, Pt, ut, OP.add)
        S.op("gpsimd" if pool_add else "vector", f, [at], [Pt, ut])
        if g >= ng - 2:
            mslice = mask4[:, 1024 * (g - (ng - 2)):1024 * (g - (ng - 2)) + 1024]

            def f(e, at=at, mslice=mslice):
                return e.tensor_tensor(at, at, mslice, OP.mult)
            S.op("gpsimd" if pool_add else "vector", f, [at], [at, mslice])
        return at

    def emit_av(idx, at):
        h, tqc, g, ng = groups[idx]
        pb_, cb = 64 * (h % 2), T * (h // 2)
        yps = psY[tqc % 2]
        for j4 in range(2):
            st = 2 * g + j4

            def f(t, st=st, h=h, at=at, j4=j4, yps=yps, g=g, ng=ng):
                return t.matmul(
                    yps[0:64, :],
                    vts[:, 256 * st + 64 * h:256 * st + 64 * h + 64],
                    at[:, 512 * j4:512 * (j4 + 1)],
                    start=(g == 0 and j4 == 0), stop=(g == ng - 1 and j4 == 1))
            S.op("tensor", f, [yps[0:64, :]],
                 [vts[:, 256 * st + 64 * h:256 * st + 64 * h + 64], at],
                 inc=(g == ng - 1 and j4 == 1))
        if g == ng - 1:
            stg = stg_[:, 512 * (tqc % 2):512 * (tqc % 2) + 512]

            def f(v, yps=yps, stg=stg):
                return v.tensor_copy(stg[0:64, :], yps[0:64, :])
            S.op("vector", f, [stg[0:64, :]], [yps[0:64, :]])
            S.dma("sync", yo[pb_:pb_ + 64, cb + 512 * tqc:cb + 512 * (tqc + 1)],
                  stg[0:64, :])

    emit_qk(0)
    emit_qk(1)
    for i in range(len(groups)):
        at = emit_elem(i)
        if i + 2 < len(groups):
            emit_qk(i + 2)
        emit_av(i, at)

    prog_end(p, S)
    return p


_PROGS = {}


def _prog(name):
    if name not in _PROGS:
        _PROGS[name] = {
            "l1a": build_launch1a, "l1b": build_launch1b, "l2": build_launch2v2,
            "l3": build_launch3, "l4": build_launch4, "l5": build_launch5,
            "l6a": build_launch6a, "l6b": build_launch6b, "l7": build_launch7,
            "l8": build_launch8,
        }[name]()
    return _PROGS[name]


def _run(name, in_maps):
    p = _prog(name)
    return run_bass_kernel_spmd(p.nc, in_maps, core_ids=list(range(8)))


CT = np.ascontiguousarray


def kernel(**inputs):
    inp = inputs
    x = np.asarray(inp["L0_attn_a_x"])

    aWqT = CT(inp["L0_attn_a_Wq"].T)
    aWkT = CT(inp["L0_attn_a_Wk"].T)
    aWvT = CT(inp["L0_attn_a_Wv"].T)
    ins = []
    for c in range(8):
        b, qq = c // 4, c % 4
        ins.append({
            "x0": CT(x[b, 2048 * qq:2048 * (qq + 1), :]),
            "aWqT": aWqT, "aWkT": aWkT, "aWvT": aWvT,
            "aproj": CT(inp["L0_attn_a_proj"]),
            "wg8": CT(inp["L1_ffn_w_gate"][352 * c:352 * (c + 1), :]),
            "wu8": CT(inp["L1_ffn_w_up"][352 * c:352 * (c + 1), :]),
            "wd8": CT(inp["L1_ffn_w_down"][128 * c:128 * (c + 1), :]),
        })
    ra = _run("l1a", ins)
    wgq = np.concatenate([np.asarray(ra.results[c]["wgq"]) for c in range(8)], 0)
    wuq = np.concatenate([np.asarray(ra.results[c]["wuq"]) for c in range(8)], 0)
    wdq = np.concatenate([np.asarray(ra.results[c]["wdq"]) for c in range(8)], 0)
    wgqT = CT(wgq.T)
    wuqT = CT(wuq.T)
    wdqT = CT(wdq.T)

    fWT = CT(inp["L0_ffn_W"].T)
    fCT_ = CT(inp["L0_ffn_C"].T)
    ins = []
    for c in range(8):
        b, qq = c // 4, c % 4
        kvparts = np.concatenate([np.asarray(ra.results[b * 4 + j]["kvpart"])
                                  for j in range(4)], 0)
        ins.append({
            "x0": CT(x[b, 2048 * qq:2048 * (qq + 1), :]),
            "qp": CT(np.asarray(ra.results[c]["qp"])),
            "kvparts": CT(kvparts),
            "fWT": fWT, "fb": CT(inp["L0_ffn_b"]),
            "fVr": CT(inp["L0_ffn_V_r"]), "fCT": fCT_,
            "bWattn": CT(inp["L0_attn_b_W_attn"]),
        })
    rb = _run("l1b", ins)
    x2 = [[np.asarray(rb.results[b * 4 + j]["x2"]) for j in range(4)]
          for b in range(2)]
    qkv = [np.concatenate([np.asarray(rb.results[b * 4 + j]["qkv"])
                           for j in range(4)], 0) for b in range(2)]

    ins = []
    for c in range(8):
        b, hg = c // 4, c % 4
        heads = [4 * hg + j for j in range(4)]
        qfm = np.zeros((128, 2 * 8192), dtype=BF)
        kfm = np.zeros((128, 2 * 8192), dtype=BF)
        vtm = np.zeros((8192, 256), dtype=BF)
        qk = qkv[b]
        for j, h in enumerate(heads):
            pb_, cb = 64 * (j % 2), 8192 * (j // 2)
            qfm[pb_:pb_ + 64, cb:cb + 8192] = qk[:, 64 * h:64 * (h + 1)].T
            kfm[pb_:pb_ + 64, cb:cb + 8192] = \
                qk[:, 1024 + 64 * h:1024 + 64 * (h + 1)].T
            vtm[:, 64 * j:64 * (j + 1)] = qk[:, 2048 + 64 * h:2048 + 64 * (h + 1)]
        ins.append({"qfm": CT(qfm), "kfm": CT(kfm), "vtm": CT(vtm)})
    r2 = _run("l2", ins)
    yT = [np.zeros((1024, 8192), dtype=BF) for _ in range(2)]
    for c in range(8):
        b, hg = c // 4, c % 4
        yfm = np.asarray(r2.results[c]["yfm"])
        for j in range(4):
            h = 4 * hg + j
            pb_, cb = 64 * (j % 2), 8192 * (j // 2)
            yT[b][64 * h:64 * (h + 1), :] = yfm[pb_:pb_ + 64, cb:cb + 8192]

    dbias = np.zeros((8, 128), np.float32)
    dbias[0, :] = inp["L1_attn_a_bq"]
    dbias[1, :] = inp["L1_attn_a_bk"]
    dbias[2, :] = inp["L1_attn_a_bv"]
    dbias[3, :] = inp["L1_attn_a_bg"]
    dbias[4, 0] = np.asarray(inp["L1_attn_a_ba"])[0]
    dbias[4, 1] = np.asarray(inp["L1_attn_a_bb"])[0]
    dWqT = CT(inp["L1_attn_a_Wq"].T)
    dWkT = CT(inp["L1_attn_a_Wk"].T)
    dWvT = CT(inp["L1_attn_a_Wv"].T)
    dWgT = CT(inp["L1_attn_a_Wg"].T)
    dWabT = CT(np.concatenate([inp["L1_attn_a_Wa"].T,
                               inp["L1_attn_a_Wb"].T], 1))
    ins = []
    for c in range(8):
        b, qq = c // 4, c % 4
        ins.append({
            "x2": CT(x2[b][qq]),
            "yT": CT(yT[b][:, 2048 * qq:2048 * (qq + 1)]),
            "bWproj": CT(inp["L0_attn_b_W_proj"]),
            "fWT": fWT, "fb": CT(inp["L0_ffn_b"]),
            "fVr": CT(inp["L0_ffn_V_r"]), "fCT": fCT_,
            "dWqT": dWqT, "dWkT": dWkT, "dWvT": dWvT, "dWgT": dWgT,
            "dWabT": dWabT, "dbias": dbias,
        })
    r3 = _run("l3", ins)
    x4 = [[np.asarray(r3.results[b * 4 + j]["x4"]) for j in range(4)]
          for b in range(2)]
    dq = [np.concatenate([np.asarray(r3.results[b * 4 + j]["dq"])
                          for j in range(4)], 0) for b in range(2)]
    dk = [np.concatenate([np.asarray(r3.results[b * 4 + j]["dk"])
                          for j in range(4)], 0) for b in range(2)]
    dv = [np.concatenate([np.asarray(r3.results[b * 4 + j]["dv"])
                          for j in range(4)], 0) for b in range(2)]
    dg = [np.concatenate([np.asarray(r3.results[b * 4 + j]["dgate"])
                          for j in range(4)], 0) for b in range(2)]
    dab = [np.concatenate([np.asarray(r3.results[b * 4 + j]["dab"])
                           for j in range(4)], 0) for b in range(2)]

    ins = []
    for c in range(8):
        b, vb = c // 4, c % 4
        ins.append({
            "kfm": CT(dk[b].T), "ktm": CT(dk[b]),
            "qpk": CT(dq[b].T[32 * vb:32 * (vb + 1), :]),
            "vbl": CT(dv[b][:, 32 * vb:32 * (vb + 1)]),
            "abr": CT(dab[b][:, 0].reshape(64, 128)),
            "bbr": CT(dab[b][:, 1].reshape(64, 128)),
            "bcT": CT(dab[b][:, 1].reshape(64, 128).T),
        })
    r4 = _run("l4", ins)

    ins = []
    for c in range(8):
        b, qq = c // 4, c % 4
        sl = slice(2048 * qq, 2048 * (qq + 1))
        op4 = np.concatenate([np.asarray(r4.results[b * 4 + j]["opart"])[sl]
                              for j in range(4)], 0)
        ins.append({
            "x4": CT(x4[b][qq]),
            "oparts": CT(op4),
            "gate": CT(dg[b][sl]),
            "dWoT": CT(inp["L1_attn_a_Wo"].T),
            "dbo": CT(inp["L1_attn_a_bo"]),
            "wgqT": wgqT, "wuqT": wuqT, "wdqT": wdqT,
            "WiT_f": CT(inp["L1_attn_b_Wi_f"].T),
            "WiT_b": CT(inp["L1_attn_b_Wi_b"].T),
        })
    r5 = _run("l5", ins)
    x6 = [[np.asarray(r5.results[b * 4 + j]["x6"]) for j in range(4)]
          for b in range(2)]
    zxf = [np.concatenate([np.asarray(r5.results[b * 4 + j]["zx_f"])
                           for j in range(4)], 1) for b in range(2)]
    zxb_nat = [np.concatenate([np.asarray(r5.results[b * 4 + j]["zx_b"])
                               for j in range(4)], 1) for b in range(2)]
    zxb = [CT(z[:, ::-1]) for z in zxb_nat]
    ztf = [np.concatenate([np.asarray(r5.results[b * 4 + j]["zt_f"])
                           for j in range(4)], 0) for b in range(2)]
    ztb_nat = [np.concatenate([np.asarray(r5.results[b * 4 + j]["zt_b"])
                               for j in range(4)], 0) for b in range(2)]

    ins = []
    meta = []
    for c in range(8):
        b, dire, hh = c // 4, (c // 2) % 2, c % 2
        zx = (zxf, zxb)[dire][b]
        sfx = "fb"[dire]
        cwf = np.asarray(inp[f"L1_attn_b_conv_w_{sfx}"])[:, 0, :]
        cbf = np.asarray(inp[f"L1_attn_b_conv_b_{sfx}"])[:, None]
        sl = slice(1152 * hh, 1152 * (hh + 1))
        ins.append({"xbc": CT(zx[sl, :]),
                    "convw": CT(cwf[sl]), "convb": CT(cbf[sl])})
        meta.append((b, dire, hh))
    r6a = _run("l6a", ins)
    conv = [[None, None], [None, None]]
    for b in range(2):
        for dire in range(2):
            cs = [np.asarray(r6a.results[b * 4 + dire * 2 + hh]["conv"])
                  for hh in range(2)]
            conv[b][dire] = np.concatenate(cs, 0)

    ins = []
    meta = []
    for c in range(8):
        b, dire, hh = c // 4, (c // 2) % 2, c % 2
        sfx = "fb"[dire]
        cv = conv[b][dire]
        zx = (zxf, zxb)[dire][b]
        Xh = cv[1024 * hh:1024 * (hh + 1)]
        Bm = cv[2048:2176]
        Cm = cv[2176:2304]
        dtraw = zx[2304 + 16 * hh:2304 + 16 * (hh + 1), :]
        hsl = slice(16 * hh, 16 * (hh + 1))
        hpar = np.zeros((16, 4), np.float32)
        hpar[:, 0] = inp[f"L1_attn_b_dt_bias_{sfx}"][hsl]
        hpar[:, 1] = inp[f"L1_attn_b_A_log_{sfx}"][hsl]
        DpE = np.repeat(np.asarray(inp[f"L1_attn_b_D_{sfx}"])[hsl],
                        64)[None, :].astype(np.float32)
        ins.append({
            "Xtm": CT(Xh.T), "Bfm": CT(Bm), "Btm": CT(Bm.T), "Cfm": CT(Cm),
            "dtraw": CT(dtraw), "hpar": hpar, "DpE": CT(DpE),
        })
        meta.append((b, dire, hh))
    r6b = _run("l6b", ins)
    Y = [[None, None], [None, None]]
    for b in range(2):
        for dire in range(2):
            ys = [np.asarray(r6b.results[b * 4 + dire * 2 + hh]["Ytm"])
                  for hh in range(2)]
            Y[b][dire] = np.concatenate(ys, 1)

    nw = np.stack([np.asarray(inp["L1_attn_b_norm_w_f"]),
                   np.asarray(inp["L1_attn_b_norm_w_b"])]).astype(np.float32)
    ins = []
    for c in range(8):
        b, qq = c // 4, c % 4
        sl = slice(2048 * qq, 2048 * (qq + 1))
        ztb_rev = ztb_nat[b][::-1]
        ins.append({
            "Yf": CT(Y[b][0][sl]), "Yb": CT(Y[b][1][sl]),
            "zf": CT(ztf[b][sl]), "zb": CT(ztb_rev[sl]),
            "nw": CT(nw),
        })
    r7 = _run("l7", ins)
    YNf = [np.concatenate([np.asarray(r7.results[b * 4 + j]["YNf"])
                           for j in range(4)], 0) for b in range(2)]
    YNb = [np.concatenate([np.asarray(r7.results[b * 4 + j]["YNb"])
                           for j in range(4)], 0) for b in range(2)]

    ins = []
    for c in range(8):
        b, qq = c // 4, c % 4
        sl = slice(2048 * qq, 2048 * (qq + 1))
        ynb_unrev = YNb[b][::-1]
        ins.append({
            "x6": CT(x6[b][qq]),
            "YNfT": CT(YNf[b].T[:, sl]),
            "YNbT": CT(ynb_unrev.T[:, sl]),
            "WoT_f": CT(inp["L1_attn_b_Wo_f"].T),
            "WoT_b": CT(inp["L1_attn_b_Wo_b"].T),
            "wgqT": wgqT, "wuqT": wuqT, "wdqT": wdqT,
        })
    r8 = _run("l8", ins)
    out = np.zeros((2, 8192, 1024), np.float32)
    for c in range(8):
        b, qq = c // 4, c % 4
        out[b, 2048 * qq:2048 * (qq + 1)] = np.asarray(r8.results[c]["x8"])
    return out
